# version 22
# baseline (speedup 1.0000x reference)
import numpy as np
from contextlib import ExitStack

import concourse.bass as bass
import concourse.mybir as mybir
from concourse.bass_utils import run_bass_kernel_spmd

F32 = mybir.dt.float32
BF16 = mybir.dt.bfloat16
AF = mybir.ActivationFunctionType
ALU = mybir.AluOpType

ENGS = ("tensor", "vector", "scalar", "sync", "gpsimd")


def make_cfg(B, T, C, DFF, ncores=8, TS=256):
    TLOC = T // ncores
    cfg = dict(B=B, T=T, C=C, DFF=DFF, NC=ncores, TS=TS, TLOC=TLOC,
               NSUP=TLOC // TS, OG=C // 128, CG=C // 128, FG=DFF // 128,
               DG=C // 128, WSC=1.0 / T)
    assert TLOC % TS == 0 and C % 128 == 0 and DFF % 128 == 0
    return cfg


class _Node:
    def __init__(self, eng, fn, reads, writes, is_dma=False, slot=None, tag=""):
        self.eng = eng
        self.fn = fn
        self.reads = reads
        self.writes = writes
        self.deps = []
        self.marked = False
        self.semname = None
        self.semval = None
        self.clock = None
        self.waits = ()
        self.is_dma = is_dma
        self.slot = slot
        self.tag = tag
        self.new_group = False
        self.manual_sem = None
        self.issue_wait = None
        self.barrier_targets = None


class _Slot:
    def __init__(self, name):
        self.name = name
        self.count = 0
        self.open_group = []
        self.last_group_end = None


class KB:
    def __init__(self, nc, es):
        self.nc = nc
        self.nodes = []
        self.last_w = {}
        self.readers = {}
        self.sems = {}
        self.es = es
        for e in ENGS:
            self.sems["e_" + e] = es.enter_context(nc.semaphore("se_" + e))
        self.slots = {}

    def slot(self, name):
        if name not in self.slots:
            self.slots[name] = _Slot(name)
            self.sems["s_" + name] = self.es.enter_context(
                self.nc.semaphore("ss_" + name))
        return self.slots[name]

    def _dep(self, node):
        deps = set()
        for k in node.reads:
            w = self.last_w.get(k)
            if w is not None:
                deps.add(w)
        for k in node.writes:
            w = self.last_w.get(k)
            if w is not None:
                deps.add(w)
            for r in self.readers.get(k, ()):
                deps.add(r)
        node.deps = [d for d in deps if d is not node]
        for k in node.reads:
            self.readers.setdefault(k, []).append(node)
        for k in node.writes:
            self.last_w[k] = node
            self.readers[k] = []

    def op(self, eng, fn, reads=(), writes=(), tag=""):
        n = _Node(eng, fn, tuple(reads), tuple(writes), tag=tag)
        self._dep(n)
        self.nodes.append(n)
        return n

    def dma(self, eng, slot_name, fn, reads=(), writes=(), new_group=False, tag=""):
        sl = self.slot(slot_name)
        n = _Node(eng, fn, tuple(reads), tuple(writes), is_dma=True, slot=sl, tag=tag)
        n.new_group = new_group
        self._dep(n)
        self.nodes.append(n)
        return n

    def manual(self, eng, fn, reads=(), writes=(), semname=None, inc=1, tag=""):
        n = _Node(eng, fn, tuple(reads), tuple(writes), tag=tag)
        n.manual_sem = (semname, inc)
        self._dep(n)
        self.nodes.append(n)
        return n

    def add_sem(self, name):
        if name not in self.sems:
            self.sems[name] = self.es.enter_context(self.nc.semaphore("sx_" + name))
        return self.sems[name]

    def barrier(self):
        n = _Node("barrier", None, (), (), tag="barrier")
        self.nodes.append(n)
        self.last_w = {k: v for k, v in self.last_w.items() if k.startswith("D:")}
        self.readers = {k: v for k, v in self.readers.items() if k.startswith("D:")}
        return n

    def resolve(self):
        def implicit(d, n):
            return (d.eng == "tensor" and n.eng == "tensor" and not d.is_dma
                    and d.manual_sem is None)

        last_per_eng = {}
        for n in self.nodes:
            if n.eng == "barrier":
                for e, ln in last_per_eng.items():
                    if not ln.is_dma and ln.manual_sem is None:
                        ln.marked = True
                continue
            for d in n.deps:
                if not d.is_dma and d.manual_sem is None and not implicit(d, n):
                    d.marked = True
            last_per_eng[n.eng] = n
        for ln in last_per_eng.values():
            if not ln.is_dma and ln.manual_sem is None:
                ln.marked = True

        nincs = {e: 0 for e in ENGS}
        eclock = {e: {} for e in ENGS}

        def merge(dst, src):
            for k, v in src.items():
                if dst.get(k, -1) < v:
                    dst[k] = v

        def close_slot(sl):
            if sl.open_group:
                v = 16 * sl.count
                for g in sl.open_group:
                    g.semval = v
                    g.clock = dict(g.clock or {})
                    g.clock["s_" + sl.name] = v
                sl.last_group_end = sl.open_group[-1]
                sl.open_group = []

        for n in self.nodes:
            if n.eng == "barrier":
                for sl in self.slots.values():
                    close_slot(sl)
                mc = {}
                for e in ENGS:
                    merge(mc, eclock[e])
                    if nincs[e] > 0:
                        mc["e_" + e] = max(mc.get("e_" + e, 0), nincs[e])
                for name, sl in self.slots.items():
                    if sl.count > 0:
                        mc["s_" + name] = max(mc.get("s_" + name, 0), 16 * sl.count)
                n.barrier_targets = dict(mc)
                for e in ENGS:
                    eclock[e] = dict(mc)
                continue
            e = n.eng
            if n.is_dma:
                sl = n.slot
                n.semname = "s_" + sl.name
                if n.new_group:
                    close_slot(sl)
                    if sl.count > 0:
                        n.issue_wait = ("s_" + sl.name, 16 * sl.count,
                                        sl.last_group_end)
            elif n.manual_sem is not None:
                n.semname, n.semval = n.manual_sem
            elif n.marked:
                nincs[e] += 1
                n.semname = "e_" + e
                n.semval = nincs[e]
            waits = []
            clk = eclock[e]
            for d in n.deps:
                if implicit(d, n) and not d.marked:
                    continue
                if d.is_dma and d.semval is None:
                    close_slot(d.slot)
                if implicit(d, n):
                    continue
                assert d.semname is not None and d.semval is not None, (
                    f"unsynced dep: {d.tag} <- {n.tag}")
                if clk.get(d.semname, 0) >= d.semval:
                    continue
                waits.append((d.semname, d.semval))
                merge(clk, d.clock or {})
                clk[d.semname] = max(clk.get(d.semname, 0), d.semval)
            if n.issue_wait is not None:
                sname, v, ref = n.issue_wait
                if clk.get(sname, 0) < v:
                    waits.append((sname, v))
                    if ref is not None:
                        merge(clk, ref.clock or {})
                    clk[sname] = max(clk.get(sname, 0), v)
            n.waits = tuple(waits)
            n.clock = dict(clk)
            if n.is_dma:
                sl = n.slot
                sl.count += 1
                sl.open_group.append(n)
            elif n.semname is not None and n.semval is not None:
                n.clock[n.semname] = n.semval
        for sl in self.slots.values():
            close_slot(sl)

    def replay(self, engine_name, eng):
        for n in self.nodes:
            if n.eng == "barrier":
                for sname, v in sorted(n.barrier_targets.items()):
                    eng.wait_ge(self.sems[sname], v)
                continue
            if n.eng != engine_name:
                continue
            for sname, v in n.waits:
                eng.wait_ge(self.sems[sname], v)
            instr = n.fn(eng)
            if instr is None:
                continue
            if n.is_dma:
                instr.then_inc(self.sems[n.semname], 16)
            elif n.manual_sem is not None:
                instr.then_inc(self.sems[n.semname], n.manual_sem[1])
            elif n.marked:
                instr.then_inc(self.sems[n.semname], 1)


def build_program(cfg):
    B, T, C, DFF = cfg["B"], cfg["T"], cfg["C"], cfg["DFF"]
    TS, TLOC, NSUP = cfg["TS"], cfg["TLOC"], cfg["NSUP"]
    OG, CG, FG, DG = cfg["OG"], cfg["CG"], cfg["FG"], cfg["DG"]
    NC = cfg["NC"]
    TSH = TS + 2
    NSUP2 = B * NSUP
    GRP2 = 8
    NGRP2 = FG // GRP2
    invC = 1.0 / C
    eps = 1e-5

    nc = bass.Bass("TRN2", target_bir_lowering=False, debug=False)

    d_x = nc.dram_tensor("x_fm", [B, C, TLOC + 2], F32, kind="ExternalInput").ap()
    d_wk = nc.dram_tensor("wkT", [C, C], F32, kind="ExternalInput").ap()
    d_wv = nc.dram_tensor("wvT", [C, C], F32, kind="ExternalInput").ap()
    d_wr = nc.dram_tensor("wrT", [C, C], F32, kind="ExternalInput").ap()
    d_wo = nc.dram_tensor("woT", [C, C], F32, kind="ExternalInput").ap()
    d_w1 = nc.dram_tensor("w1", [C, DFF], F32, kind="ExternalInput").ap()
    d_w2 = nc.dram_tensor("w2", [DFF, C], F32, kind="ExternalInput").ap()
    d_sdf = nc.dram_tensor("sd_fm", [128, OG], F32, kind="ExternalInput").ap()
    d_sff = nc.dram_tensor("sf_fm", [128, OG], F32, kind="ExternalInput").ap()
    d_mk = nc.dram_tensor("mixk_fm", [128, OG], F32, kind="ExternalInput").ap()
    d_mv = nc.dram_tensor("mixv_fm", [128, OG], F32, kind="ExternalInput").ap()
    d_mr = nc.dram_tensor("mixr_fm", [128, OG], F32, kind="ExternalInput").ap()
    d_b1 = nc.dram_tensor("b1_fm", [128, FG], F32, kind="ExternalInput").ap()
    d_th = nc.dram_tensor("theta_fm", [128, FG], F32, kind="ExternalInput").ap()
    d_qr = nc.dram_tensor("qwr_fm", [128, FG], F32, kind="ExternalInput").ap()
    d_qi = nc.dram_tensor("qwi_fm", [128, FG], F32, kind="ExternalInput").ap()
    d_b2 = nc.dram_tensor("b2_fm", [128, OG], F32, kind="ExternalInput").ap()
    d_sdr = nc.dram_tensor("sd_row", [1, C], F32, kind="ExternalInput").ap()
    d_i256 = nc.dram_tensor("iota256", [1, TS], F32, kind="ExternalInput").ap()
    d_i8 = nc.dram_tensor("iota8", [1, NSUP], F32, kind="ExternalInput").ap()
    d_ones1 = nc.dram_tensor("ones1", [1, 128], F32, kind="ExternalInput").ap()
    d_invC = nc.dram_tensor("invC_row", [1, 128], F32, kind="ExternalInput").ap()
    d_onesP = nc.dram_tensor("onesP", [128, 1], F32, kind="ExternalInput").ap()
    d_mask = nc.dram_tensor("mask", [128, 8], F32, kind="ExternalInput").ap()
    d_out = nc.dram_tensor("out_fm", [B, C, TLOC], F32, kind="ExternalOutput").ap()
    d_num = nc.dram_tensor("sp_num", [NSUP2, 128, OG * TS], F32)
    d_den = nc.dram_tensor("sp_den", [NSUP2, 128, OG * TS], F32)
    d_nh = nc.dram_tensor("sp_nh", [NSUP2, 128, OG * TSH], BF16)
    d_nl = nc.dram_tensor("sp_nl", [NSUP2, 128, OG * TSH], BF16)
    d_xrh = nc.dram_tensor("sp_xrh", [NSUP2, 128, 4 * TS], BF16)
    d_xrl = nc.dram_tensor("sp_xrl", [NSUP2, 128, 4 * TS], BF16)
    d_xp = nc.dram_tensor("sp_xp", [NSUP2, 128, OG * TS], F32)
    d_tot = nc.dram_tensor("sp_tot", [128, 2 * B * OG], F32)
    d_gat = nc.dram_tensor("sp_gat", [NC * 128, 2 * B * OG], F32)
    d_w1s = nc.dram_tensor("sp_w1s", [DFF // 128, 128, 2 * C], BF16)
    d_w2s = nc.dram_tensor("sp_w2s", [DFF // 128, 128, 2 * C], BF16)

    es = ExitStack()
    CONST_F32 = 2944
    PHF = 23100
    PHB = 53600
    arena = es.enter_context(nc.sbuf_tensor("arena", [128, CONST_F32 + PHF], F32))
    arenab = es.enter_context(nc.sbuf_tensor("arenab", [128, PHB], BF16))
    _off = [0]
    _offb = [0]

    def alloc(nelem):
        o = _off[0]
        _off[0] += nelem
        assert _off[0] <= CONST_F32 + PHF, f"arena overflow {_off[0]}"
        return arena[:, o:o + nelem]

    def allocb(nelem):
        o = _offb[0]
        _offb[0] += nelem
        assert _offb[0] <= PHB, f"arenab overflow {_offb[0]}"
        return arenab[:, o:o + nelem]

    v_sd = alloc(OG); v_sf = alloc(OG)
    v_mk = alloc(OG); v_mv = alloc(OG); v_mr = alloc(OG)
    v_omk = alloc(OG); v_omv = alloc(OG); v_omr = alloc(OG)
    v_eu = alloc(OG); v_ewcol = alloc(OG)
    v_ew2 = alloc(OG)
    v_ew2r = alloc(4 * OG)
    v_mask = alloc(8)
    v_b1 = alloc(FG); v_th = alloc(FG); v_qr = alloc(FG); v_qi = alloc(FG)
    v_sc = alloc(FG); v_cth = alloc(FG); v_aqh = alloc(FG); v_aql = alloc(FG)
    v_E = alloc(FG); v_F = alloc(FG)
    v_b2 = alloc(OG)
    v_tmp32 = alloc(FG)
    v_A = alloc(4 * OG)
    v_onesP = alloc(1)
    v_eps = alloc(1)
    r_sd = alloc(C)[0:1, :]
    r_i256 = alloc(TS)[0:1, :]
    r_i8 = alloc(NSUP)[0:1, :]
    r_ones1 = alloc(128)[0:1, :]
    r_invC = alloc(128)[0:1, :]
    r_rows = alloc(TSH * 3)
    assert _off[0] <= CONST_F32, f"const region overflow: {_off[0]}"
    _off[0] = CONST_F32

    def phase_reset():
        _off[0] = CONST_F32
        _offb[0] = 0

    kb = KB(nc, es)
    kb.add_sem("cc")
    pb = [es.enter_context(nc.psum_tensor(f"pb{i}", [128, 512], F32)) for i in range(8)]

    TAG = [0]

    def t(s):
        TAG[0] += 1
        return f"{s}#{TAG[0]}"

    def ts_op(out, in0, s1, s2, op0, op1=None, reads=(), writes=(), tag=""):
        def f(e, out=out, in0=in0, s1=s1, s2=s2, op0=op0, op1=op1):
            if op1 is None:
                return e.tensor_scalar(out=out, in0=in0, scalar1=s1, scalar2=None,
                                       op0=op0)
            return e.tensor_scalar(out=out, in0=in0, scalar1=s1, scalar2=s2,
                                   op0=op0, op1=op1)
        return kb.op("vector", f, reads, writes, tag=t(tag or "ts"))

    def tt_op(out, in0, in1, op, reads=(), writes=(), tag=""):
        def f(e, out=out, in0=in0, in1=in1, op=op):
            return e.tensor_tensor(out=out, in0=in0, in1=in1, op=op)
        return kb.op("vector", f, reads, writes, tag=t(tag or "tt"))

    def stt_op(out, in0, sc, in1, op0, op1, reads=(), writes=(), tag=""):
        def f(e, out=out, in0=in0, sc=sc, in1=in1, op0=op0, op1=op1):
            return e.scalar_tensor_tensor(out=out, in0=in0, scalar=sc, in1=in1,
                                          op0=op0, op1=op1)
        return kb.op("vector", f, reads, writes, tag=t(tag or "stt"))

    def act_op(out, in_, func, bias=None, scale=None, reads=(), writes=(), tag=""):
        def f(e, out=out, in_=in_, func=func, bias=bias, scale=scale):
            kw = {}
            if bias is not None:
                kw["bias"] = bias
            if scale is not None:
                kw["scale"] = scale
            return e.activation(out=out, in_=in_, func=func, **kw)
        return kb.op("scalar", f, reads, writes, tag=t(tag or "act"))

    def copy_op(out, in_, reads=(), writes=(), tag=""):
        def f(e, out=out, in_=in_):
            return e.tensor_copy(out=out, in_=in_)
        return kb.op("vector", f, reads, writes, tag=t(tag or "cp"))

    def memset_op(ap, val, writes=(), tag=""):
        def f(e, ap=ap, val=val):
            return e.memset(ap, val)
        return kb.op("vector", f, (), writes, tag=t(tag or "ms"))

    def mm(out, lhsT, rhs, start, stop, reads=(), writes=(), tag=""):
        def f(e, out=out, lhsT=lhsT, rhs=rhs, start=start, stop=stop):
            return e.matmul(out, lhsT, rhs, start=start, stop=stop)
        return kb.op("tensor", f, reads, writes, tag=t(tag or "mm"))

    def mm3(ps, lhs_h, lhs_l, rhs_h, rhs_l, nk, reads, writes, tag=""):
        passes = ((lhs_h, rhs_h), (lhs_l, rhs_h), (lhs_h, rhs_l))
        for pi, (lf, rf) in enumerate(passes):
            for i in range(nk):
                mm(ps, lf(i), rf(i), pi == 0 and i == 0,
                   pi == 2 and i == nk - 1, reads=reads, writes=writes, tag=tag)

    def scan_op(out, data0, data1, initial, reads=(), writes=(), tag=""):
        def f(e, out=out, d0=data0, d1=data1, init=initial):
            return e.tensor_tensor_scan(out=out, data0=d0, data1=d1,
                                        initial=init, op0=ALU.mult, op1=ALU.add)
        return kb.op("vector", f, reads, writes, tag=t(tag or "scan"))

    def dma(eng, slot, out, in_, reads=(), writes=(), new_group=False, slow=False,
            tag=""):
        def f(e, out=out, in_=in_, slow=slow):
            if slow:
                return e.dma_start(out=out, in_=in_, allow_slow_non_contiguous=True)
            return e.dma_start(out=out, in_=in_)
        return kb.dma(eng, slot, f, reads, writes, new_group=new_group,
                      tag=t(tag or ("dma_" + slot)))

    def split(hi, lo, src, skey, hkey, lkey, tag=""):
        act_op(hi, src, AF.Copy, reads=(skey,), writes=(hkey,), tag=tag + "hi")
        tt_op(lo, src, hi, ALU.subtract, reads=(skey, hkey), writes=(lkey,),
              tag=tag + "lo")

    for ap_, d_, key in ((v_sd, d_sdf, "c:sd"), (v_sf, d_sff, "c:sf"),
                         (v_mk, d_mk, "c:mk"), (v_mv, d_mv, "c:mv"),
                         (v_mr, d_mr, "c:mr"), (v_b1, d_b1, "c:b1"),
                         (v_th, d_th, "c:th"), (v_qr, d_qr, "c:qr"),
                         (v_qi, d_qi, "c:qi"), (v_b2, d_b2, "c:b2"),
                         (r_sd, d_sdr, "c:sdrow"), (r_i256, d_i256, "c:i256"),
                         (r_i8, d_i8, "c:i8"), (r_ones1, d_ones1, "c:ones1"),
                         (r_invC, d_invC, "c:invC"), (v_onesP, d_onesP, "c:onesP"),
                         (v_mask, d_mask, "c:mask")):
        dma("sync", "cin", ap_, d_, writes=(key,), tag="cld")

    memset_op(v_eps, eps, writes=("c:eps",), tag="epsfill")
    for v_m, v_om, km, ko in ((v_mk, v_omk, "c:mk", "c:omk"),
                              (v_mv, v_omv, "c:mv", "c:omv"),
                              (v_mr, v_omr, "c:mr", "c:omr")):
        ts_op(v_om, v_m, -1.0, 1.0, ALU.mult, ALU.add, reads=(km,), writes=(ko,),
              tag="om")
    act_op(v_eu, v_sf, AF.Exp, scale=cfg["WSC"], reads=("c:sf",), writes=("c:eu",),
           tag="eu")
    act_op(v_ewcol, v_sd, AF.Exp, scale=cfg["WSC"], reads=("c:sd",), writes=("c:ew",),
           tag="ew")
    act_op(v_ew2, v_sd, AF.Exp, scale=float(TLOC) * cfg["WSC"], reads=("c:sd",),
           writes=("c:ew2",), tag="ew2")
    ts_op(v_ew2, v_ew2, -1.0, None, ALU.add, reads=("c:ew2",), writes=("c:ew2",),
          tag="ew2m1")
    for r in range(4):
        copy_op(v_ew2r[:, r * OG:(r + 1) * OG], v_ew2, reads=("c:ew2",),
                writes=("c:ew2r",), tag="ew2rep")
    PI = float(np.float32(np.pi))
    TWO_PI = 2.0 * np.pi
    c1 = 6.28125
    c2 = float(np.float32(TWO_PI - c1))
    c3 = float(np.float32(TWO_PI - c1 - float(np.float32(TWO_PI - c1))))
    for _ in range(7):
        ts_op(v_tmp32, v_th, PI, None, ALU.is_gt, reads=("c:th",), writes=("c:g",),
              tag="rrgt")
        for cc_ in (c1, c2, c3):
            stt_op(v_th, v_tmp32, -cc_, v_th, ALU.mult, ALU.add,
                   reads=("c:g", "c:th"), writes=("c:th",), tag="rrsub")
        ts_op(v_tmp32, v_th, -PI, None, ALU.is_lt, reads=("c:th",), writes=("c:g",),
              tag="rrlt")
        for cc_ in (c1, c2, c3):
            stt_op(v_th, v_tmp32, cc_, v_th, ALU.mult, ALU.add,
                   reads=("c:g", "c:th"), writes=("c:th",), tag="rradd")
    act_op(v_sc, v_th, AF.Sin, reads=("c:th",), writes=("c:sc",), tag="sinth")
    act_op(v_cth, v_th, AF.Sin, scale=0.5, reads=("c:th",), writes=("c:cth",),
           tag="sinth2")
    tt_op(v_cth, v_cth, v_cth, ALU.mult, reads=("c:cth",), writes=("c:cth",), tag="csq")
    ts_op(v_cth, v_cth, -2.0, 1.0, ALU.mult, ALU.add, reads=("c:cth",),
          writes=("c:cth",), tag="costh")
    P1H = float(np.float32(0.1))
    P1L = float(0.1 - P1H)
    tt_op(v_aqh, v_cth, v_qr, ALU.mult, reads=("c:cth", "c:qr"), writes=("c:aqh",),
          tag="qp1")
    tt_op(v_aql, v_sc, v_qi, ALU.mult, reads=("c:sc", "c:qi"), writes=("c:aql",),
          tag="qp2")
    tt_op(v_aqh, v_aqh, v_aql, ALU.add, reads=("c:aqh", "c:aql"), writes=("c:aqh",),
          tag="qp3")
    tt_op(v_tmp32, v_aqh, v_qi, ALU.mult, reads=("c:aqh", "c:qi"), writes=("c:t3",),
          tag="qp4")
    tt_op(v_aqh, v_tmp32, v_sc, ALU.mult, reads=("c:t3", "c:sc"), writes=("c:e1",),
          tag="e1")
    ts_op(v_E, v_aqh, P1H, None, ALU.mult, reads=("c:e1",), writes=("c:E",),
          tag="Eh")
    stt_op(v_E, v_aqh, P1L, v_E, ALU.mult, ALU.add, reads=("c:e1", "c:E"),
           writes=("c:E",), tag="El")
    tt_op(v_aqh, v_tmp32, v_cth, ALU.mult, reads=("c:t3", "c:cth"), writes=("c:e1",),
          tag="f1")
    ts_op(v_F, v_aqh, P1H, None, ALU.mult, reads=("c:e1",), writes=("c:F",),
          tag="Fh")
    stt_op(v_F, v_aqh, P1L, v_F, ALU.mult, ALU.add, reads=("c:e1", "c:F"),
           writes=("c:F",), tag="Fl")

    def ln_stats(xv, width, sqv, sq_key, x_key, tagp, packed=False, packed4=False,
                 sq_vec=False):
        if sq_vec:
            tt_op(sqv[:, 0:OG * width], xv[:, 0:OG * width], xv[:, 0:OG * width],
                  ALU.mult, reads=(x_key,), writes=(sq_key,), tag=tagp + "sq")
        else:
            act_op(sqv[:, 0:OG * width], xv[:, 0:OG * width], AF.Square,
                   reads=(x_key,), writes=(sq_key,), tag=tagp + "sq")
        if packed4:
            assert width <= 256
            rsum = pb[4][0:1, 0:width]
            rsq = pb[5][0:1, 0:width]
            ksum, ksq = "pb4", "pb5"
            mrep = pb[4][:, 256:256 + width]
            rrep = pb[5][:, 256:256 + width]
        elif packed:
            assert width <= 256
            rsum = pb[6][0:1, 0:width]
            rsq = pb[7][0:1, 0:width]
            ksum, ksq = "pb6", "pb7"
            mrep = pb[6][:, 256:256 + width]
            rrep = pb[7][:, 256:256 + width]
        else:
            rsum = pb[4][0:1, 0:width]
            rsq = pb[5][0:1, 0:width]
            ksum, ksq = "pb4", "pb5"
            mrep = pb[6][:, 0:width]
            rrep = pb[7][:, 0:width]
        for cg in range(CG):
            mm(rsum, v_onesP, xv[:, cg * width:(cg + 1) * width],
               cg == 0, cg == CG - 1, reads=("c:onesP", x_key), writes=(ksum,),
               tag=tagp + "summ")
        for cg in range(CG):
            mm(rsq, v_onesP, sqv[:, cg * width:(cg + 1) * width],
               cg == 0, cg == CG - 1, reads=("c:onesP", sq_key), writes=(ksq,),
               tag=tagp + "sqmm")
        m_row = r_rows[0:1, 0:width]
        v_row = r_rows[0:1, TSH:TSH + width]
        s_row = r_rows[0:1, 2 * TSH:2 * TSH + width]
        ts_op(m_row, rsum, invC, None, ALU.mult, reads=(ksum,), writes=("rows_m",),
              tag=tagp + "m")
        tt_op(v_row, m_row, m_row, ALU.mult, reads=("rows_m",), writes=("rows_v",),
              tag=tagp + "msq")
        stt_op(v_row, rsq, invC, v_row, ALU.mult, ALU.subtract,
               reads=(ksq, "rows_v"), writes=("rows_v",), tag=tagp + "var")
        act_op(v_row, v_row, AF.Ln, bias=v_eps[0:1, 0:1], reads=("rows_v", "c:eps"),
               writes=("rows_v",), tag=tagp + "lnv")
        act_op(s_row, v_row, AF.Exp, scale=-0.5, reads=("rows_v",), writes=("rows_s",),
               tag=tagp + "rstd")
        mm(mrep, r_ones1, m_row, True, True,
           reads=("c:ones1", "rows_m"), writes=(ksum if (packed or packed4) else "pb6",),
           tag=tagp + "mrep")
        mm(rrep, r_ones1, s_row, True, True,
           reads=("c:ones1", "rows_s"), writes=(ksq if (packed or packed4) else "pb7",),
           tag=tagp + "rrep")
        return mrep, rrep

    def weight_split(d_src, hi, lo, hkey, lkey, scr_list, scr_keys, slot, tag):
        nch = CG // 2
        for i in range(nch):
            scr = scr_list[i % len(scr_list)][:, 0:2 * C]
            skey = scr_keys[i % len(scr_keys)]
            dma("sync", slot, scr.rearrange("p (cg o) -> p cg o", cg=2),
                d_src[i * 256:(i + 1) * 256, :].rearrange("(cg p) o -> p cg o", p=128),
                reads=(), writes=(skey,), new_group=True, tag=tag + "ld")
            split(hi[:, i * 2 * C:(i + 1) * 2 * C], lo[:, i * 2 * C:(i + 1) * 2 * C],
                  scr, skey, hkey, lkey, tag=tag + "sp")

    phase_reset()
    a_x = [alloc(OG * TSH) for _ in range(2)]
    a_sq = alloc(OG * TSH)
    a_n = [alloc(OG * TSH) for _ in range(2)]
    a_d = alloc(4 * TS)
    a_mscr = alloc(4 * TS)
    a_ek = alloc(OG * TS)
    a_ekv = alloc(OG * TS)
    a_asc = alloc(OG * (TS + 1))
    a_bsc = alloc(OG * (TS + 1))
    a_ewrep = alloc(OG * TS)
    a_tots = alloc(4 * OG)
    a_gath = alloc(2 * B * NC * OG)
    a_Atmp = alloc(4 * OG)
    a_wkh = allocb(OG * C); a_wkl = allocb(OG * C)
    a_wvh = allocb(OG * C); a_wvl = allocb(OG * C)
    a_xkp = [[allocb(4 * TS), allocb(4 * TS)] for _ in range(2)]
    a_xvp = [[allocb(4 * TS), allocb(4 * TS)] for _ in range(2)]
    a_xrp = [[allocb(4 * TS), allocb(4 * TS)] for _ in range(2)]
    a_np = [[allocb(OG * TSH), allocb(OG * TSH)] for _ in range(2)]

    weight_split(d_wk, a_wkh, a_wkl, "wkh", "wkl", [a_x[0], a_x[1]], ["x0", "x1"],
                 "wsp", "wk")
    weight_split(d_wv, a_wvh, a_wvl, "wvh", "wvl", [a_x[0], a_x[1]], ["x0", "x1"],
                 "wsp", "wv")

    memset_op(a_ewrep, 1.0, writes=("ewrep",), tag="ewrep0")
    for og in range(OG):
        ts_op(a_ewrep[:, og * TS:(og + 1) * TS], a_ewrep[:, og * TS:(og + 1) * TS],
              v_ewcol[:, og:og + 1], None, ALU.mult,
              reads=("ewrep", "c:ew"), writes=("ewrep",), tag="ewrep")

    def p1a_Aload(u):
        b, s = divmod(u, NSUP)
        par = u % 2
        xv = a_x[par]
        xkey = f"x{par}"
        dma("sync", f"xin{par}", xv.rearrange("p (g w) -> p g w", g=OG),
            d_x[b, :, s * TS:s * TS + TSH].rearrange("(g p) w -> p g w", p=128),
            writes=(xkey,), new_group=True, tag="xload")

    def p1a_Afix(u):
        b, s = divmod(u, NSUP)
        par = u % 2
        xv, nv = a_x[par], a_n[par]
        xkey, nkey = f"x{par}", f"n{par}"
        mrep, rrep = ln_stats(xv, TSH, a_sq, "sq", xkey, "l1")
        nv3 = nv.rearrange("p (g w) -> p g w", g=OG)
        tt_op(nv3, xv.rearrange("p (g w) -> p g w", g=OG),
              mrep.unsqueeze(1).broadcast_to([128, OG, TSH]), ALU.subtract,
              reads=(xkey, "pb6"), writes=(nkey,), tag="ncent")
        tt_op(nv3, nv3, rrep.unsqueeze(1).broadcast_to([128, OG, TSH]), ALU.mult,
              reads=(nkey, "pb7"), writes=(nkey,), tag="nmul")
        split(a_np[par][0], a_np[par][1], nv, nkey, f"nh{par}", f"nl{par}", tag="nsp")
        for cg in range(4):
            off = cg * TSH
            sh = nv[:, off:off + TS] if cg < 2 else nv[:, off + 2:off + TS + 2]
            mid = nv[:, off + 1:off + TS + 1]
            dsl = a_d[:, cg * TS:(cg + 1) * TS]
            tt_op(dsl, sh, mid, ALU.subtract, reads=(nkey,), writes=("d",),
                  tag="dshift")
        for om_, pr, kk in ((v_omk, a_xkp[par], f"xk{par}"),
                            (v_omv, a_xvp[par], f"xv{par}"),
                            (v_omr, a_xrp[par], f"xr{par}")):
            ko = {id(v_omk): "c:omk", id(v_omv): "c:omv", id(v_omr): "c:omr"}[id(om_)]
            for cg in range(4):
                off = cg * TSH
                mid = nv[:, off + 1:off + TS + 1]
                stt_op(a_mscr[:, cg * TS:(cg + 1) * TS],
                       a_d[:, cg * TS:(cg + 1) * TS], om_[:, cg:cg + 1], mid,
                       ALU.mult, ALU.add, reads=("d", nkey, ko), writes=("mscr",),
                       tag="mix")
            split(pr[0], pr[1], a_mscr, "mscr", kk + "h", kk + "l", tag="xsp")

    def p1a_B(u):
        b, s = divmod(u, NSUP)
        par = u % 2
        nkey = f"n{par}"

        def rhs_pair(pref, half):
            def f(cg, pref=pref, half=half):
                if cg < 4:
                    src = (a_xkp if pref == "k" else a_xvp)[par][half]
                    return src[:, cg * TS:(cg + 1) * TS]
                nph = a_np[par][half]
                return nph[:, cg * TSH + 1:cg * TSH + 1 + TS]
            return f

        rk_h = (f"xk{par}h", f"nh{par}")
        rk_l = (f"xk{par}l", f"nl{par}")
        rv_h = (f"xv{par}h", f"nh{par}")
        rv_l = (f"xv{par}l", f"nl{par}")
        for og in range(OG):
            bk = (2 * og) % 6
            bkv = (2 * og + 1) % 6
            psk = pb[bk][:, 0:TS]
            psv = pb[bkv][:, 0:TS]
            mm3(psk,
                lambda cg, og=og: a_wkh[:, cg * C + og * 128:cg * C + (og + 1) * 128],
                lambda cg, og=og: a_wkl[:, cg * C + og * 128:cg * C + (og + 1) * 128],
                rhs_pair("k", 0), rhs_pair("k", 1), CG,
                reads=("wkh", "wkl") + rk_h + rk_l, writes=(f"pb{bk}",), tag="kmm")
            mm3(psv,
                lambda cg, og=og: a_wvh[:, cg * C + og * 128:cg * C + (og + 1) * 128],
                lambda cg, og=og: a_wvl[:, cg * C + og * 128:cg * C + (og + 1) * 128],
                rhs_pair("v", 0), rhs_pair("v", 1), CG,
                reads=("wvh", "wvl") + rv_h + rv_l, writes=(f"pb{bkv}",), tag="vmm")
            eksl = a_ek[:, og * TS:(og + 1) * TS]
            ekvsl = a_ekv[:, og * TS:(og + 1) * TS]
            kek, kekv = f"ek{og}", f"ekv{og}"
            act_op(eksl, psk, AF.Exp, reads=(f"pb{bk}",), writes=(kek,), tag="ek")
            tt_op(ekvsl, eksl, psv, ALU.mult, reads=(kek, f"pb{bkv}"),
                  writes=(kekv,), tag="ekv")
            W1p = TS + 1
            asl = a_asc[:, og * W1p:(og + 1) * W1p]
            bsl = a_bsc[:, og * W1p:(og + 1) * W1p]
            ka, kbk = f"asc{og}", f"bsc{og}"
            if s == 0:
                memset_op(asl[:, 0:1], 0.0, writes=(ka,), tag="acar0")
                memset_op(bsl[:, 0:1], 0.0, writes=(kbk,), tag="bcar0")
            else:
                copy_op(asl[:, 0:1], asl[:, TS:TS + 1], reads=(ka,), writes=(ka,),
                        tag="acar")
                copy_op(bsl[:, 0:1], bsl[:, TS:TS + 1], reads=(kbk,), writes=(kbk,),
                        tag="bcar")
            scan_op(asl[:, 1:TS + 1], a_ewrep[:, og * TS:(og + 1) * TS], ekvsl,
                    asl[:, 0:1], reads=("ewrep", kekv, ka), writes=(ka,), tag="ascan")
            scan_op(bsl[:, 1:TS + 1], a_ewrep[:, og * TS:(og + 1) * TS], eksl,
                    bsl[:, 0:1], reads=("ewrep", kek, kbk), writes=(kbk,), tag="bscan")
            stt_op(asl[:, 0:TS], ekvsl, v_eu[:, og:og + 1],
                   asl[:, 0:TS], ALU.mult, ALU.add,
                   reads=(kekv, "c:eu", ka), writes=(ka,), tag="numloc")
            stt_op(bsl[:, 0:TS], eksl, v_eu[:, og:og + 1],
                   bsl[:, 0:TS], ALU.mult, ALU.add,
                   reads=(kek, "c:eu", kbk), writes=(kbk,), tag="denloc")
            if s == NSUP - 1:
                copy_op(a_tots[:, (2 * b) * OG + og:(2 * b) * OG + og + 1],
                        asl[:, TS:TS + 1], reads=(ka,), writes=("tots",), tag="tota")
                copy_op(a_tots[:, (2 * b + 1) * OG + og:(2 * b + 1) * OG + og + 1],
                        bsl[:, TS:TS + 1], reads=(kbk,), writes=("tots",), tag="totb")
        allk = tuple(f"asc{og}" for og in range(OG))
        allkb = tuple(f"bsc{og}" for og in range(OG))
        dma("gpsimd", f"sp{par}", d_num.ap()[u].rearrange("p (og w) -> p og w", og=OG),
            a_asc.rearrange("p (og w) -> p og w", og=OG)[:, :, 0:TS], reads=allk,
            writes=("D:num",), new_group=True, tag="spnum")
        dma("gpsimd", f"sp{par}", d_den.ap()[u].rearrange("p (og w) -> p og w", og=OG),
            a_bsc.rearrange("p (og w) -> p og w", og=OG)[:, :, 0:TS], reads=allkb,
            writes=("D:den",), tag="spden")
        dma("gpsimd", f"sp{par}", d_nh.ap()[u], a_np[par][0], reads=(f"nh{par}",),
            writes=("D:nh",), tag="spnh")
        dma("gpsimd", f"sp{par}", d_nl.ap()[u], a_np[par][1], reads=(f"nl{par}",),
            writes=("D:nl",), tag="spnl")
        dma("gpsimd", f"sp{par}", d_xrh.ap()[u], a_xrp[par][0], reads=(f"xr{par}h",),
            writes=("D:xrh",), tag="spxrh")
        dma("gpsimd", f"sp{par}", d_xrl.ap()[u], a_xrp[par][1], reads=(f"xr{par}l",),
            writes=("D:xrl",), tag="spxrl")

    p1a_Aload(0)
    p1a_Aload(1)
    p1a_Afix(0)
    for u in range(NSUP2):
        p1a_B(u)
        if u + 1 < NSUP2:
            p1a_Afix(u + 1)
        if u + 2 < NSUP2:
            p1a_Aload(u + 2)

    dma("gpsimd", "tot", d_tot.ap(), a_tots, reads=("tots",),
        writes=("D:tot",), new_group=True, tag="totst")

    def f_cc(e):
        return e.collective_compute(
            "AllGather", ALU.bypass, replica_groups=[list(range(NC))],
            ins=[d_tot.ap().opt()], outs=[d_gat.ap().opt()])
    kb.manual("gpsimd", f_cc, reads=("D:tot",), writes=("D:gat",), semname="cc",
              inc=1, tag="allgather")

    dma("gpsimd", "gat",
        a_gath.rearrange("p (j w) -> p j w", j=NC),
        d_gat.ap().rearrange("(j p) w -> p j w", p=128),
        reads=("D:gat",), writes=("gath",), new_group=True, slow=True, tag="gatld")

    memset_op(v_A, 0.0, writes=("A",), tag="A0")
    for j in range(NC - 1):
        ts_op(a_Atmp, v_ew2r, v_mask[:, j:j + 1], 1.0, ALU.mult, ALU.add,
              reads=("c:ew2r", "c:mask"), writes=("Atmp",), tag="dsel")
        tt_op(v_A, v_A, a_Atmp, ALU.mult, reads=("A", "Atmp"), writes=("A",),
              tag="Amul")
        ts_op(a_Atmp, a_gath[:, j * (2 * B) * OG:(j + 1) * (2 * B) * OG],
              v_mask[:, j:j + 1], None, ALU.mult, reads=("gath", "c:mask"),
              writes=("Atmp",), tag="Tmask")
        tt_op(v_A, v_A, a_Atmp, ALU.add, reads=("A", "Atmp"), writes=("A",),
              tag="Aadd")

    kb.barrier()
    phase_reset()
    b_num = [alloc(OG * TS) for _ in range(2)]
    b_den = [alloc(OG * TS) for _ in range(2)]
    b_x = [alloc(OG * TS) for _ in range(2)]
    b_sr = alloc(OG * TS)
    b_attn = alloc(OG * TS)
    b_xp = [alloc(OG * TS) for _ in range(2)]
    b_ewpb = alloc(OG * TS)
    b_ews = alloc(NSUP * OG)
    b_As = [[alloc(OG), alloc(OG)] for _ in range(2)]
    b_wrh = allocb(OG * C); b_wrl = allocb(OG * C)
    b_woh = allocb(OG * C); b_wol = allocb(OG * C)
    b_np = [[allocb(OG * TSH), allocb(OG * TSH)] for _ in range(2)]
    b_xrp = [[allocb(4 * TS), allocb(4 * TS)] for _ in range(2)]
    b_gp = [allocb(OG * TS), allocb(OG * TS)]

    weight_split(d_wr, b_wrh, b_wrl, "wrh", "wrl", [b_num[0], b_num[1]],
                 ["bnum0", "bnum1"], "wsp2", "wr")
    weight_split(d_wo, b_woh, b_wol, "woh", "wol", [b_num[0], b_num[1]],
                 ["bnum0", "bnum1"], "wsp2", "wo")

    for og in range(OG):
        ps = pb[og % 2][:, 0:TS]
        mm(ps, r_sd[0:1, og * 128:(og + 1) * 128], r_i256, True, True,
           reads=("c:sdrow", "c:i256"), writes=(f"pb{og % 2}",), tag="ewpmm")
        act_op(b_ewpb[:, og * TS:(og + 1) * TS], ps, AF.Exp, scale=cfg["WSC"],
               reads=(f"pb{og % 2}",), writes=("ewpb",), tag="ewpexp")
    for og in range(OG):
        ps = pb[og % 2][:, 0:NSUP]
        mm(ps, r_sd[0:1, og * 128:(og + 1) * 128], r_i8, True, True,
           reads=("c:sdrow", "c:i8"), writes=(f"pb{og % 2}",), tag="ewsmm")
        act_op(b_ews[:, og * NSUP:(og + 1) * NSUP], ps, AF.Exp,
               scale=float(TS) * cfg["WSC"], reads=(f"pb{og % 2}",), writes=("ews",),
               tag="ewsexp")
    ews_v = b_ews.rearrange("p (og s) -> p og s", og=OG)

    def p1b_Aload(u):
        b, s = divmod(u, NSUP)
        par = u % 2
        numv, denv = b_num[par], b_den[par]
        dma("sync", f"ld1b{par}", numv, d_num.ap()[u], reads=("D:num",),
            writes=(f"bnum{par}",), new_group=True, tag="ldnum")
        dma("sync", f"ld1b{par}", denv, d_den.ap()[u], reads=("D:den",),
            writes=(f"bden{par}",), tag="ldden")
        dma("sync", f"ld1b{par}", b_np[par][0], d_nh.ap()[u], reads=("D:nh",),
            writes=(f"bnh{par}",), tag="ldnh")
        dma("sync", f"ld1b{par}", b_np[par][1], d_nl.ap()[u], reads=("D:nl",),
            writes=(f"bnl{par}",), tag="ldnl")
        dma("sync", f"ld1b{par}", b_xrp[par][0], d_xrh.ap()[u], reads=("D:xrh",),
            writes=(f"bxrh{par}",), tag="ldxrh")
        dma("sync", f"ld1b{par}", b_xrp[par][1], d_xrl.ap()[u], reads=("D:xrl",),
            writes=(f"bxrl{par}",), tag="ldxrl")
        dma("sync", f"ld1b{par}", b_x[par].rearrange("p (g w) -> p g w", g=OG),
            d_x[b, :, s * TS + 1:s * TS + 1 + TS].rearrange("(g p) w -> p g w", p=128),
            reads=(), writes=(f"bx{par}",), tag="ldx")

    def p1b_Afix(u):
        b, s = divmod(u, NSUP)
        par = u % 2
        numv, denv = b_num[par], b_den[par]
        for kind in range(2):
            r = 2 * b + kind
            tt_op(b_As[par][kind], v_A[:, r * OG:(r + 1) * OG], ews_v[:, :, s],
                  ALU.mult, reads=("A", "ews"), writes=(f"As{par}{kind}",),
                  tag="Ascale")
        for og in range(OG):
            sl = slice(og * TS, (og + 1) * TS)
            stt_op(numv[:, sl], b_ewpb[:, sl], b_As[par][0][:, og:og + 1],
                   numv[:, sl], ALU.mult, ALU.add,
                   reads=("ewpb", f"As{par}0", f"bnum{par}"),
                   writes=(f"bnum{par}",), tag="fixn")
            stt_op(denv[:, sl], b_ewpb[:, sl], b_As[par][1][:, og:og + 1],
                   denv[:, sl], ALU.mult, ALU.add,
                   reads=("ewpb", f"As{par}1", f"bden{par}"),
                   writes=(f"bden{par}",), tag="fixd")
        act_op(denv, denv, AF.Ln, reads=(f"bden{par}",), writes=(f"bden{par}",),
               tag="lnden")
        act_op(denv, denv, AF.Exp, scale=-1.0, reads=(f"bden{par}",),
               writes=(f"bden{par}",), tag="rden")
        tt_op(numv, numv, denv, ALU.mult, reads=(f"bnum{par}", f"bden{par}"),
              writes=(f"bnum{par}",), tag="wkv")

    def p1b_B(u):
        b, s = divmod(u, NSUP)
        par = u % 2
        numv, xpv = b_num[par], b_xp[par]

        def rrhs(half):
            def f(cg, half=half):
                if cg < 4:
                    return b_xrp[par][half][:, cg * TS:(cg + 1) * TS]
                return b_np[par][half][:, cg * TSH + 1:cg * TSH + 1 + TS]
            return f

        rkeys = (f"bxrh{par}", f"bxrl{par}", f"bnh{par}", f"bnl{par}")
        for og in range(OG):
            bk = og % 4
            ps = pb[bk][:, 0:TS]
            mm3(ps,
                lambda cg, og=og: b_wrh[:, cg * C + og * 128:cg * C + (og + 1) * 128],
                lambda cg, og=og: b_wrl[:, cg * C + og * 128:cg * C + (og + 1) * 128],
                rrhs(0), rrhs(1), CG,
                reads=("wrh", "wrl") + rkeys, writes=(f"pb{bk}",), tag="rmm")
            act_op(b_sr[:, og * TS:(og + 1) * TS], ps, AF.Sigmoid,
                   reads=(f"pb{bk}",), writes=("sr",), tag="sr")
            sl = slice(og * TS, (og + 1) * TS)
            tt_op(numv[:, sl], numv[:, sl], b_sr[:, sl], ALU.mult,
                  reads=(f"bnum{par}", "sr"), writes=(f"bnum{par}",), tag="gated")
            split(b_gp[0][:, sl], b_gp[1][:, sl], numv[:, sl], f"bnum{par}",
                  "gph", "gpl", tag="gsp")
        for dg in range(DG):
            bk = 4 + dg % 2
            ps = pb[bk][:, 0:TS]
            mm3(ps,
                lambda og, dg=dg: b_woh[:, og * C + dg * 128:og * C + (dg + 1) * 128],
                lambda og, dg=dg: b_wol[:, og * C + dg * 128:og * C + (dg + 1) * 128],
                lambda og: b_gp[0][:, og * TS:(og + 1) * TS],
                lambda og: b_gp[1][:, og * TS:(og + 1) * TS], OG,
                reads=("woh", "wol", "gph", "gpl"), writes=(f"pb{bk}",), tag="womm")
            act_op(b_attn[:, dg * TS:(dg + 1) * TS], ps, AF.Copy,
                   reads=(f"pb{bk}",), writes=("attn",), tag="attnev")
        mrep, rrep = ln_stats(b_attn, TS, xpv, f"bxp{par}", "attn", "l2",
                              packed=True)
        at3 = b_attn.rearrange("p (g w) -> p g w", g=OG)
        tt_op(at3, at3, mrep.unsqueeze(1).broadcast_to([128, OG, TS]), ALU.subtract,
              reads=("attn", "pb6"), writes=("attn",), tag="pcent")
        tt_op(at3, at3, rrep.unsqueeze(1).broadcast_to([128, OG, TS]), ALU.mult,
              reads=("attn", "pb7"), writes=("attn",), tag="pmul")
        tt_op(xpv, b_attn, b_x[par], ALU.add,
              reads=("attn", f"bx{par}"), writes=(f"bxp{par}",), tag="resid")
        dma("gpsimd", f"spx{par}", d_xp.ap()[u], xpv, reads=(f"bxp{par}",),
            writes=("D:xp",), new_group=True, tag="spxp")

    p1b_Aload(0)
    p1b_Aload(1)
    p1b_Afix(0)
    for u in range(NSUP2):
        p1b_B(u)
        if u + 1 < NSUP2:
            p1b_Afix(u + 1)
        if u + 2 < NSUP2:
            p1b_Aload(u + 2)

    kb.barrier()
    phase_reset()
    c_q = alloc(GRP2 * TS)
    c_ws = alloc(2 * C)
    c_xp = [alloc(OG * TS) for _ in range(4)]
    c_n2 = alloc(OG * TS)
    c_out = alloc(OG * TS)
    c_hb = [alloc(GRP2 * TS) for _ in range(2)]
    c_sh = alloc(GRP2 * TS)
    c_rows2 = alloc(2 * TS)
    c_w1 = [allocb(2 * C) for _ in range(2)]
    c_w2s = [allocb(2 * C) for _ in range(2)]
    c_actp = [[allocb(FG * TS), allocb(FG * TS)] for _ in range(2)]
    c_n2p = [[allocb(OG * TS), allocb(OG * TS)] for _ in range(2)]

    def p2_Pload(u):
        par = u % 4
        dma("sync", f"xpin{par}", c_xp[par], d_xp.ap()[u], reads=("D:xp",),
            writes=(f"cxp{par}",), new_group=True, tag="ldxp")

    def p2_Pstats(u):
        par = u % 4
        mem = u % 2
        xpv = c_xp[par]
        xkey = f"cxp{par}"
        tt_op(c_n2, xpv, xpv, ALU.mult, reads=(xkey,), writes=("n2f",),
              tag="l3sq")
        rsum = pb[4][0:1, mem * TS:(mem + 1) * TS]
        rsq = pb[5][0:1, mem * TS:(mem + 1) * TS]
        for cg in range(CG):
            mm(rsum, v_onesP, xpv[:, cg * TS:(cg + 1) * TS], cg == 0,
               cg == CG - 1, reads=("c:onesP", xkey), writes=("pb4",),
               tag="l3summ")
        for cg in range(CG):
            mm(rsq, v_onesP, c_n2[:, cg * TS:(cg + 1) * TS], cg == 0,
               cg == CG - 1, reads=("c:onesP", "n2f"), writes=("pb5",),
               tag="l3sqmm")
        if mem == 0:
            m_row = r_rows[0:1, 0:TS]
            s_row = r_rows[0:1, 2 * TSH:2 * TSH + TS]
            mkey, skey = "rows_m", "rows_s"
        else:
            m_row = c_rows2[0:1, 0:TS]
            s_row = c_rows2[0:1, TS:2 * TS]
            mkey, skey = "r2m", "r2s"
        v_row = r_rows[0:1, TSH:TSH + TS]
        ts_op(m_row, rsum, invC, None, ALU.mult, reads=("pb4",), writes=(mkey,),
              tag="l3m")
        tt_op(v_row, m_row, m_row, ALU.mult, reads=(mkey,), writes=("rows_v",),
              tag="l3msq")
        stt_op(v_row, rsq, invC, v_row, ALU.mult, ALU.subtract,
               reads=("pb5", "rows_v"), writes=("rows_v",), tag="l3var")
        act_op(v_row, v_row, AF.Ln, bias=v_eps[0:1, 0:1],
               reads=("rows_v", "c:eps"), writes=("rows_v",), tag="l3lnv")
        act_op(s_row, v_row, AF.Exp, scale=-0.5, reads=("rows_v",),
               writes=(skey,), tag="l3rstd")
        mm(pb[6][:, mem * TS:(mem + 1) * TS], r_ones1, m_row, mem == 0,
           mem == 1, reads=("c:ones1", mkey), writes=("pb6",), tag="l3mrep")
        mm(pb[7][:, mem * TS:(mem + 1) * TS], r_ones1, s_row, mem == 0,
           mem == 1, reads=("c:ones1", skey), writes=("pb7",), tag="l3rrep")

    def p2_Papply(u):
        par = u % 4
        mem = u % 2
        xpv = c_xp[par]
        xkey = f"cxp{par}"
        mrep = pb[6][:, mem * TS:(mem + 1) * TS]
        rrep = pb[7][:, mem * TS:(mem + 1) * TS]
        n23 = c_n2.rearrange("p (g w) -> p g w", g=OG)
        tt_op(n23, xpv.rearrange("p (g w) -> p g w", g=OG),
              mrep.unsqueeze(1).broadcast_to([128, OG, TS]), ALU.subtract,
              reads=(xkey, "pb6"), writes=("n2f",), tag="n2c")
        tt_op(n23, n23, rrep.unsqueeze(1).broadcast_to([128, OG, TS]), ALU.mult,
              reads=("n2f", "pb7"), writes=("n2f",), tag="n2m")
        split(c_n2p[mem][0], c_n2p[mem][1], c_n2, "n2f", f"n2h{mem}",
              f"n2l{mem}", tag="n2sp")

    def p2_FG(u):
        for grp in range(NGRP2):
            for fgl in range(GRP2):
                fg = grp * GRP2 + fgl
                wpar = fg % 2
                wpb = c_w1[wpar]
                if u == 0:
                    dma("sync", f"w1in{wpar}",
                        c_ws[:, 0:C].rearrange("p (cg f) -> p cg f", cg=CG),
                        d_w1[:, fg * 128:(fg + 1) * 128].rearrange(
                            "(cg p) f -> p cg f", p=128),
                        reads=("ws",), writes=("ws",), new_group=True, tag="w1ldf")
                    split(wpb[:, 0:C], wpb[:, C:2 * C], c_ws[:, 0:C], "ws",
                          f"w1p{wpar}", f"w1p{wpar}", tag="w1sp")
                    dma("gpsimd", "wst", d_w1s.ap()[fg], wpb,
                        reads=(f"w1p{wpar}",), writes=("D:w1s",),
                        new_group=True, tag="w1st")
                else:
                    dma("sync", f"w1in{wpar}", wpb, d_w1s.ap()[fg],
                        reads=("D:w1s",), writes=(f"w1p{wpar}",),
                        new_group=True, tag="w1ldp")
                hbk = fg % 6
                first = True
                for mem in range(2):
                    ps = pb[hbk][:, mem * TS:(mem + 1) * TS]
                    for pi, (lo_w, lo_r) in enumerate(((0, 0), (C, 0), (0, 1))):
                        for cg in range(CG):
                            mm(ps, wpb[:, lo_w + cg * 128:lo_w + (cg + 1) * 128],
                               c_n2p[mem][lo_r][:, cg * TS:(cg + 1) * TS],
                               first, mem == 1 and pi == 2 and cg == CG - 1,
                               reads=(f"w1p{wpar}", f"n2h{mem}", f"n2l{mem}"),
                               writes=(f"pb{hbk}",), tag="w1mm")
                            first = False
                for mem in range(2):
                    act_op(c_hb[mem][:, fgl * TS:(fgl + 1) * TS],
                           pb[hbk][:, mem * TS:(mem + 1) * TS], AF.Identity,
                           bias=v_b1[:, fg:fg + 1], reads=(f"pb{hbk}", "c:b1"),
                           writes=(f"hb{mem}f{fgl}",), tag="hbev")
            Ec = v_E[:, grp * GRP2:(grp + 1) * GRP2]
            Fc = v_F[:, grp * GRP2:(grp + 1) * GRP2]
            qb = (c_q, c_out)
            qk = ("q", "cout")
            shb = (c_sh, c_ws)
            shk = ("sh", "ws")
            for mem in range(2):
                hball = tuple(f"hb{mem}f{i}" for i in range(GRP2))
                ts_op(shb[mem], c_hb[mem], P1H, None, ALU.mult,
                      reads=hball, writes=(shk[mem],), tag="hh1")
                stt_op(shb[mem], c_hb[mem], P1L, shb[mem], ALU.mult, ALU.add,
                       reads=hball + (shk[mem],), writes=(shk[mem],), tag="hh2")
            for mem in range(2):
                act_op(qb[mem], shb[mem], AF.Sin, reads=(shk[mem],),
                       writes=(qk[mem],), tag="sinb")
            for mem in range(2):
                act_op(shb[mem], shb[mem], AF.Sin, scale=0.5, reads=(shk[mem],),
                       writes=(shk[mem],), tag="sinh2")
            for mem in range(2):
                act_op(shb[mem], shb[mem], AF.Square, reads=(shk[mem],),
                       writes=(shk[mem],), tag="shsq")
            for mem in range(2):
                act_op(shb[mem], shb[mem], AF.Copy, scale=-2.0, bias=1.0,
                       reads=(shk[mem],), writes=(shk[mem],), tag="cosb")
            for mem in range(2):
                for fgl in range(GRP2):
                    sl = slice(fgl * TS, (fgl + 1) * TS)
                    stt_op(qb[mem][:, sl], qb[mem][:, sl], Fc[:, fgl:fgl + 1],
                           c_hb[mem][:, sl], ALU.mult, ALU.add,
                           reads=(qk[mem], "c:F", f"hb{mem}f{fgl}"),
                           writes=(qk[mem],), tag="fsb")
            for mem in range(2):
                for fgl in range(GRP2):
                    sl = slice(fgl * TS, (fgl + 1) * TS)
                    stt_op(qb[mem][:, sl], shb[mem][:, sl], Ec[:, fgl:fgl + 1],
                           qb[mem][:, sl], ALU.mult, ALU.add,
                           reads=(qk[mem], "c:E", shk[mem]),
                           writes=(qk[mem],), tag="eq")
            for mem in range(2):
                akey = f"act{mem}g{grp}"
                a_h = c_actp[mem][0][:, grp * GRP2 * TS:(grp + 1) * GRP2 * TS]
                a_l = c_actp[mem][1][:, grp * GRP2 * TS:(grp + 1) * GRP2 * TS]
                act_op(a_h, qb[mem], AF.Relu, reads=(qk[mem],),
                       writes=(akey + "h",), tag="asphi")
                stt_op(a_l, qb[mem], 0.0, a_h, ALU.max, ALU.subtract,
                       reads=(qk[mem], akey + "h"), writes=(akey + "l",),
                       tag="asplo")

    def p2_DG(u):
        for fgl in range(FG):
            wpar = fgl % 2
            wpb = c_w2s[wpar]
            if u == 0:
                dma("gpsimd", f"w2in{wpar}",
                    c_ws[:, 0:C], d_w2[fgl * 128:(fgl + 1) * 128, :].rearrange(
                        "(o p) d -> p (o d)", p=128),
                    reads=("ws",), writes=("ws",), new_group=True, tag="w2ldf")
                split(wpb[:, 0:C], wpb[:, C:2 * C], c_ws[:, 0:C], "ws",
                      f"w2p{wpar}", f"w2p{wpar}", tag="w2sp")
                dma("gpsimd", "wst", d_w2s.ap()[fgl], wpb,
                    reads=(f"w2p{wpar}",), writes=("D:w2s",),
                    new_group=True, tag="w2st")
            else:
                dma("gpsimd", f"w2in{wpar}", wpb, d_w2s.ap()[fgl],
                    reads=("D:w2s",), writes=(f"w2p{wpar}",),
                    new_group=True, tag="w2ldp")
            for mem in range(2):
                akey = f"act{mem}g{fgl // GRP2}"
                ah = c_actp[mem][0][:, fgl * TS:(fgl + 1) * TS]
                al = c_actp[mem][1][:, fgl * TS:(fgl + 1) * TS]
                for dg in range(DG):
                    base = dg * 128
                    for pi, (off, ra) in enumerate(((0, ah), (C, ah), (0, al))):
                        mm(pb[dg][:, mem * TS:(mem + 1) * TS],
                           wpb[:, off + base:off + base + 128], ra,
                           fgl == 0 and mem == 0 and pi == 0,
                           fgl == FG - 1 and mem == 1 and pi == 2,
                           reads=(f"w2p{wpar}", akey + "h", akey + "l"),
                           writes=(f"pb{dg}",), tag="w2mm")

    def p2_Lpair(u):
        obuf = (c_out, c_q)
        okey = ("cout", "q")
        for mem in range(2):
            for dg in range(DG):
                ts_op(obuf[mem][:, dg * TS:(dg + 1) * TS],
                      pb[dg][:, mem * TS:(mem + 1) * TS],
                      v_b2[:, dg:dg + 1], None, ALU.add, reads=(f"pb{dg}", "c:b2"),
                      writes=(okey[mem],), tag="outev")
        for mem in range(2):
            um = u + mem
            par = um % 4
            b, s = divmod(um, NSUP)
            xkey = f"cxp{par}"
            ov = obuf[mem]
            ok = okey[mem]
            mrep, rrep = ln_stats(ov, TS, c_ws, "ws", ok, "l4", packed4=True)
            o3 = ov.rearrange("p (g w) -> p g w", g=OG)
            tt_op(o3, o3, mrep.unsqueeze(1).broadcast_to([128, OG, TS]),
                  ALU.subtract, reads=(ok, "pb4"), writes=(ok,), tag="fcent")
            tt_op(o3, o3, rrep.unsqueeze(1).broadcast_to([128, OG, TS]), ALU.mult,
                  reads=(ok, "pb5"), writes=(ok,), tag="fmul")
            tt_op(ov, ov, c_xp[par], ALU.add,
                  reads=(ok, xkey), writes=(ok,), tag="fresid")
            dma("gpsimd", "outst",
                d_out[b, :, s * TS:(s + 1) * TS].rearrange("(g p) w -> p g w", p=128),
                ov.rearrange("p (g w) -> p g w", g=OG), reads=(ok,),
                writes=("D:out",), new_group=True, tag="outst")

    NP2 = NSUP2 // 2
    p2_Pload(0)
    p2_Pload(1)
    p2_Pstats(0)
    p2_Pstats(1)
    p2_Papply(0)
    p2_Papply(1)
    for p in range(NP2):
        u = 2 * p
        if p > 0:
            p2_DG(u - 2)
            p2_Lpair(u - 2)
        if u + 2 < NSUP2:
            p2_Pload(u + 2)
            p2_Pload(u + 3)
            p2_Pstats(u + 2)
            p2_Pstats(u + 3)
        p2_FG(u)
        if u + 2 < NSUP2:
            p2_Papply(u + 2)
            p2_Papply(u + 3)
    p2_DG(NSUP2 - 2)
    p2_Lpair(NSUP2 - 2)

    kb.barrier()

    kb.resolve()
    blk = es.enter_context(nc.Block())
    with es:
        for name in ENGS:
            getattr(blk, name)(lambda eng, name=name: kb.replay(name, eng))
    return nc


def _fm_vec(v, groups):
    return np.ascontiguousarray(np.asarray(v, np.float32).reshape(groups, 128).T)


def stage_inputs(cfg, inputs):
    B, T, C, DFF, NC, TLOC = (cfg["B"], cfg["T"], cfg["C"], cfg["DFF"],
                              cfg["NC"], cfg["TLOC"])
    x = np.asarray(inputs["L0_attn_x"], np.float32)
    xt = np.transpose(x, (0, 2, 1))
    xpad = np.zeros((B, C, T + 2), np.float32)
    xpad[:, :, 1:T + 1] = xt
    shared = {
        "wkT": np.ascontiguousarray(np.asarray(inputs["L0_attn_Wk"], np.float32).T),
        "wvT": np.ascontiguousarray(np.asarray(inputs["L0_attn_Wv"], np.float32).T),
        "wrT": np.ascontiguousarray(np.asarray(inputs["L0_attn_Wr"], np.float32).T),
        "woT": np.ascontiguousarray(np.asarray(inputs["L0_attn_Wo"], np.float32).T),
        "w1": np.ascontiguousarray(np.asarray(inputs["L0_ffn_W1"], np.float32)),
        "w2": np.ascontiguousarray(np.asarray(inputs["L0_ffn_W2"], np.float32)),
        "sd_fm": _fm_vec(inputs["L0_attn_spatial_decay"], C // 128),
        "sf_fm": _fm_vec(inputs["L0_attn_spatial_first"], C // 128),
        "mixk_fm": _fm_vec(inputs["L0_attn_mix_k"], C // 128),
        "mixv_fm": _fm_vec(inputs["L0_attn_mix_v"], C // 128),
        "mixr_fm": _fm_vec(inputs["L0_attn_mix_r"], C // 128),
        "b1_fm": _fm_vec(inputs["L0_ffn_b1"], DFF // 128),
        "theta_fm": _fm_vec(inputs["L0_ffn_quantum_phases"], DFF // 128),
        "qwr_fm": _fm_vec(inputs["L0_ffn_quantum_weights_real"], DFF // 128),
        "qwi_fm": _fm_vec(inputs["L0_ffn_quantum_weights_imag"], DFF // 128),
        "b2_fm": _fm_vec(inputs["L0_ffn_b2"], C // 128),
        "sd_row": np.ascontiguousarray(
            np.asarray(inputs["L0_attn_spatial_decay"], np.float32).reshape(1, C)),
        "iota256": np.arange(cfg["TS"], dtype=np.float32).reshape(1, -1),
        "iota8": np.arange(cfg["NSUP"], dtype=np.float32).reshape(1, -1),
        "ones1": np.ones((1, 128), np.float32),
        "invC_row": np.full((1, 128), 1.0 / C, np.float32),
        "onesP": np.ones((128, 1), np.float32),
    }
    in_maps = []
    for i in range(NC):
        m = dict(shared)
        m["x_fm"] = np.ascontiguousarray(xpad[:, :, i * TLOC:(i + 1) * TLOC + 2])
        mask = np.zeros((128, 8), np.float32)
        mask[:, :i] = 1.0
        m["mask"] = mask
        in_maps.append(m)
    return in_maps


def assemble_output(cfg, results):
    outs = [np.transpose(np.asarray(r["out_fm"]), (0, 2, 1)) for r in results]
    return np.ascontiguousarray(np.concatenate(outs, axis=1))


def kernel(**inputs):
    x = np.asarray(inputs["L0_attn_x"])
    B, T, C = x.shape
    DFF = np.asarray(inputs["L0_ffn_W1"]).shape[1]
    cfg = make_cfg(B, T, C, DFF)
    nc_prog = build_program(cfg)
    in_maps = stage_inputs(cfg, inputs)
    res = run_bass_kernel_spmd(nc_prog, in_maps, core_ids=list(range(cfg["NC"])))
    return assemble_output(cfg, res.results)


# version 23
# speedup vs baseline: 1.0191x; 1.0191x over previous
import numpy as np
from contextlib import ExitStack

import concourse.bass as bass
import concourse.mybir as mybir
from concourse.bass_utils import run_bass_kernel_spmd

F32 = mybir.dt.float32
BF16 = mybir.dt.bfloat16
AF = mybir.ActivationFunctionType
ALU = mybir.AluOpType

ENGS = ("tensor", "vector", "scalar", "sync", "gpsimd")


def make_cfg(B, T, C, DFF, ncores=8, TS=256):
    TLOC = T // ncores
    cfg = dict(B=B, T=T, C=C, DFF=DFF, NC=ncores, TS=TS, TLOC=TLOC,
               NSUP=TLOC // TS, OG=C // 128, CG=C // 128, FG=DFF // 128,
               DG=C // 128, WSC=1.0 / T)
    assert TLOC % TS == 0 and C % 128 == 0 and DFF % 128 == 0
    return cfg


class _Node:
    def __init__(self, eng, fn, reads, writes, is_dma=False, slot=None, tag=""):
        self.eng = eng
        self.fn = fn
        self.reads = reads
        self.writes = writes
        self.deps = []
        self.marked = False
        self.semname = None
        self.semval = None
        self.clock = None
        self.waits = ()
        self.is_dma = is_dma
        self.slot = slot
        self.tag = tag
        self.new_group = False
        self.manual_sem = None
        self.issue_wait = None
        self.barrier_targets = None


class _Slot:
    def __init__(self, name):
        self.name = name
        self.count = 0
        self.open_group = []
        self.last_group_end = None


class KB:
    def __init__(self, nc, es):
        self.nc = nc
        self.nodes = []
        self.last_w = {}
        self.readers = {}
        self.sems = {}
        self.es = es
        for e in ENGS:
            self.sems["e_" + e] = es.enter_context(nc.semaphore("se_" + e))
        self.slots = {}

    def slot(self, name):
        if name not in self.slots:
            self.slots[name] = _Slot(name)
            self.sems["s_" + name] = self.es.enter_context(
                self.nc.semaphore("ss_" + name))
        return self.slots[name]

    def _dep(self, node):
        deps = set()
        for k in node.reads:
            w = self.last_w.get(k)
            if w is not None:
                deps.add(w)
        for k in node.writes:
            w = self.last_w.get(k)
            if w is not None:
                deps.add(w)
            for r in self.readers.get(k, ()):
                deps.add(r)
        node.deps = [d for d in deps if d is not node]
        for k in node.reads:
            self.readers.setdefault(k, []).append(node)
        for k in node.writes:
            self.last_w[k] = node
            self.readers[k] = []

    def op(self, eng, fn, reads=(), writes=(), tag=""):
        n = _Node(eng, fn, tuple(reads), tuple(writes), tag=tag)
        self._dep(n)
        self.nodes.append(n)
        return n

    def dma(self, eng, slot_name, fn, reads=(), writes=(), new_group=False, tag=""):
        sl = self.slot(slot_name)
        n = _Node(eng, fn, tuple(reads), tuple(writes), is_dma=True, slot=sl, tag=tag)
        n.new_group = new_group
        self._dep(n)
        self.nodes.append(n)
        return n

    def manual(self, eng, fn, reads=(), writes=(), semname=None, inc=1, tag=""):
        n = _Node(eng, fn, tuple(reads), tuple(writes), tag=tag)
        n.manual_sem = (semname, inc)
        self._dep(n)
        self.nodes.append(n)
        return n

    def add_sem(self, name):
        if name not in self.sems:
            self.sems[name] = self.es.enter_context(self.nc.semaphore("sx_" + name))
        return self.sems[name]

    def barrier(self):
        n = _Node("barrier", None, (), (), tag="barrier")
        self.nodes.append(n)
        self.last_w = {k: v for k, v in self.last_w.items() if k.startswith("D:")}
        self.readers = {k: v for k, v in self.readers.items() if k.startswith("D:")}
        return n

    def resolve(self):
        def implicit(d, n):
            return (d.eng == "tensor" and n.eng == "tensor" and not d.is_dma
                    and d.manual_sem is None)

        last_per_eng = {}
        for n in self.nodes:
            if n.eng == "barrier":
                for e, ln in last_per_eng.items():
                    if not ln.is_dma and ln.manual_sem is None:
                        ln.marked = True
                continue
            for d in n.deps:
                if not d.is_dma and d.manual_sem is None and not implicit(d, n):
                    d.marked = True
            last_per_eng[n.eng] = n
        for ln in last_per_eng.values():
            if not ln.is_dma and ln.manual_sem is None:
                ln.marked = True

        nincs = {e: 0 for e in ENGS}
        eclock = {e: {} for e in ENGS}

        def merge(dst, src):
            for k, v in src.items():
                if dst.get(k, -1) < v:
                    dst[k] = v

        def close_slot(sl):
            if sl.open_group:
                v = 16 * sl.count
                for g in sl.open_group:
                    g.semval = v
                    g.clock = dict(g.clock or {})
                    g.clock["s_" + sl.name] = v
                sl.last_group_end = sl.open_group[-1]
                sl.open_group = []

        for n in self.nodes:
            if n.eng == "barrier":
                for sl in self.slots.values():
                    close_slot(sl)
                mc = {}
                for e in ENGS:
                    merge(mc, eclock[e])
                    if nincs[e] > 0:
                        mc["e_" + e] = max(mc.get("e_" + e, 0), nincs[e])
                for name, sl in self.slots.items():
                    if sl.count > 0:
                        mc["s_" + name] = max(mc.get("s_" + name, 0), 16 * sl.count)
                n.barrier_targets = dict(mc)
                for e in ENGS:
                    eclock[e] = dict(mc)
                continue
            e = n.eng
            if n.is_dma:
                sl = n.slot
                n.semname = "s_" + sl.name
                if n.new_group:
                    close_slot(sl)
                    if sl.count > 0:
                        n.issue_wait = ("s_" + sl.name, 16 * sl.count,
                                        sl.last_group_end)
            elif n.manual_sem is not None:
                n.semname, n.semval = n.manual_sem
            elif n.marked:
                nincs[e] += 1
                n.semname = "e_" + e
                n.semval = nincs[e]
            waits = []
            clk = eclock[e]
            for d in n.deps:
                if implicit(d, n) and not d.marked:
                    continue
                if d.is_dma and d.semval is None:
                    close_slot(d.slot)
                if implicit(d, n):
                    continue
                assert d.semname is not None and d.semval is not None, (
                    f"unsynced dep: {d.tag} <- {n.tag}")
                if clk.get(d.semname, 0) >= d.semval:
                    continue
                waits.append((d.semname, d.semval))
                merge(clk, d.clock or {})
                clk[d.semname] = max(clk.get(d.semname, 0), d.semval)
            if n.issue_wait is not None:
                sname, v, ref = n.issue_wait
                if clk.get(sname, 0) < v:
                    waits.append((sname, v))
                    if ref is not None:
                        merge(clk, ref.clock or {})
                    clk[sname] = max(clk.get(sname, 0), v)
            n.waits = tuple(waits)
            n.clock = dict(clk)
            if n.is_dma:
                sl = n.slot
                sl.count += 1
                sl.open_group.append(n)
            elif n.semname is not None and n.semval is not None:
                n.clock[n.semname] = n.semval
        for sl in self.slots.values():
            close_slot(sl)

    def replay(self, engine_name, eng):
        for n in self.nodes:
            if n.eng == "barrier":
                for sname, v in sorted(n.barrier_targets.items()):
                    eng.wait_ge(self.sems[sname], v)
                continue
            if n.eng != engine_name:
                continue
            for sname, v in n.waits:
                eng.wait_ge(self.sems[sname], v)
            instr = n.fn(eng)
            if instr is None:
                continue
            if n.is_dma:
                instr.then_inc(self.sems[n.semname], 16)
            elif n.manual_sem is not None:
                instr.then_inc(self.sems[n.semname], n.manual_sem[1])
            elif n.marked:
                instr.then_inc(self.sems[n.semname], 1)


def build_program(cfg):
    B, T, C, DFF = cfg["B"], cfg["T"], cfg["C"], cfg["DFF"]
    TS, TLOC, NSUP = cfg["TS"], cfg["TLOC"], cfg["NSUP"]
    OG, CG, FG, DG = cfg["OG"], cfg["CG"], cfg["FG"], cfg["DG"]
    NC = cfg["NC"]
    TSH = TS + 2
    NSUP2 = B * NSUP
    GRP2 = 8
    NGRP2 = FG // GRP2
    invC = 1.0 / C
    eps = 1e-5

    nc = bass.Bass("TRN2", target_bir_lowering=False, debug=False)

    d_x = nc.dram_tensor("x_fm", [B, C, TLOC + 2], F32, kind="ExternalInput").ap()
    d_wk = nc.dram_tensor("wkT", [C, C], F32, kind="ExternalInput").ap()
    d_wv = nc.dram_tensor("wvT", [C, C], F32, kind="ExternalInput").ap()
    d_wr = nc.dram_tensor("wrT", [C, C], F32, kind="ExternalInput").ap()
    d_wo = nc.dram_tensor("woT", [C, C], F32, kind="ExternalInput").ap()
    d_w1 = nc.dram_tensor("w1", [C, DFF], F32, kind="ExternalInput").ap()
    d_w2 = nc.dram_tensor("w2", [DFF, C], F32, kind="ExternalInput").ap()
    d_sdf = nc.dram_tensor("sd_fm", [128, OG], F32, kind="ExternalInput").ap()
    d_sff = nc.dram_tensor("sf_fm", [128, OG], F32, kind="ExternalInput").ap()
    d_mk = nc.dram_tensor("mixk_fm", [128, OG], F32, kind="ExternalInput").ap()
    d_mv = nc.dram_tensor("mixv_fm", [128, OG], F32, kind="ExternalInput").ap()
    d_mr = nc.dram_tensor("mixr_fm", [128, OG], F32, kind="ExternalInput").ap()
    d_b1 = nc.dram_tensor("b1_fm", [128, FG], F32, kind="ExternalInput").ap()
    d_th = nc.dram_tensor("theta_fm", [128, FG], F32, kind="ExternalInput").ap()
    d_qr = nc.dram_tensor("qwr_fm", [128, FG], F32, kind="ExternalInput").ap()
    d_qi = nc.dram_tensor("qwi_fm", [128, FG], F32, kind="ExternalInput").ap()
    d_b2 = nc.dram_tensor("b2_fm", [128, OG], F32, kind="ExternalInput").ap()
    d_sdr = nc.dram_tensor("sd_row", [1, C], F32, kind="ExternalInput").ap()
    d_i256 = nc.dram_tensor("iota256", [1, TS], F32, kind="ExternalInput").ap()
    d_i8 = nc.dram_tensor("iota8", [1, NSUP], F32, kind="ExternalInput").ap()
    d_ones1 = nc.dram_tensor("ones1", [1, 128], F32, kind="ExternalInput").ap()
    d_invC = nc.dram_tensor("invC_row", [1, 128], F32, kind="ExternalInput").ap()
    d_onesP = nc.dram_tensor("onesP", [128, 1], F32, kind="ExternalInput").ap()
    d_mask = nc.dram_tensor("mask", [128, 8], F32, kind="ExternalInput").ap()
    d_out = nc.dram_tensor("out_fm", [B, C, TLOC], F32, kind="ExternalOutput").ap()
    d_num = nc.dram_tensor("sp_num", [NSUP2, 128, OG * TS], F32)
    d_den = nc.dram_tensor("sp_den", [NSUP2, 128, OG * TS], F32)
    d_nh = nc.dram_tensor("sp_nh", [NSUP2, 128, OG * TSH], BF16)
    d_nl = nc.dram_tensor("sp_nl", [NSUP2, 128, OG * TSH], BF16)
    d_xrh = nc.dram_tensor("sp_xrh", [NSUP2, 128, 4 * TS], BF16)
    d_xrl = nc.dram_tensor("sp_xrl", [NSUP2, 128, 4 * TS], BF16)
    d_xp = nc.dram_tensor("sp_xp", [NSUP2, 128, OG * TS], F32)
    d_tot = nc.dram_tensor("sp_tot", [128, 2 * B * OG], F32)
    d_gat = nc.dram_tensor("sp_gat", [NC * 128, 2 * B * OG], F32)
    d_w1s = nc.dram_tensor("sp_w1s", [DFF // 128, 128, 2 * C], BF16)
    d_w2s = nc.dram_tensor("sp_w2s", [DFF // 128, 128, 2 * C], BF16)

    es = ExitStack()
    CONST_F32 = 2944
    PHF = 23100
    PHB = 53600
    arena = es.enter_context(nc.sbuf_tensor("arena", [128, CONST_F32 + PHF], F32))
    arenab = es.enter_context(nc.sbuf_tensor("arenab", [128, PHB], BF16))
    _off = [0]
    _offb = [0]

    def alloc(nelem):
        o = _off[0]
        _off[0] += nelem
        assert _off[0] <= CONST_F32 + PHF, f"arena overflow {_off[0]}"
        return arena[:, o:o + nelem]

    def allocb(nelem):
        o = _offb[0]
        _offb[0] += nelem
        assert _offb[0] <= PHB, f"arenab overflow {_offb[0]}"
        return arenab[:, o:o + nelem]

    v_sd = alloc(OG); v_sf = alloc(OG)
    v_mk = alloc(OG); v_mv = alloc(OG); v_mr = alloc(OG)
    v_omk = alloc(OG); v_omv = alloc(OG); v_omr = alloc(OG)
    v_eu = alloc(OG); v_ewcol = alloc(OG)
    v_ew2 = alloc(OG)
    v_ew2r = alloc(4 * OG)
    v_mask = alloc(8)
    v_b1 = alloc(FG); v_th = alloc(FG); v_qr = alloc(FG); v_qi = alloc(FG)
    v_sc = alloc(FG); v_cth = alloc(FG); v_aqh = alloc(FG); v_aql = alloc(FG)
    v_E = alloc(FG); v_F = alloc(FG)
    v_b2 = alloc(OG)
    v_tmp32 = alloc(FG)
    v_A = alloc(4 * OG)
    v_onesP = alloc(1)
    v_eps = alloc(1)
    r_sd = alloc(C)[0:1, :]
    r_i256 = alloc(TS)[0:1, :]
    r_i8 = alloc(NSUP)[0:1, :]
    r_ones1 = alloc(128)[0:1, :]
    r_invC = alloc(128)[0:1, :]
    r_rows = alloc(TSH * 3)
    assert _off[0] <= CONST_F32, f"const region overflow: {_off[0]}"
    _off[0] = CONST_F32

    def phase_reset():
        _off[0] = CONST_F32
        _offb[0] = 0

    kb = KB(nc, es)
    kb.add_sem("cc")
    pb = [es.enter_context(nc.psum_tensor(f"pb{i}", [128, 512], F32)) for i in range(8)]

    TAG = [0]

    def t(s):
        TAG[0] += 1
        return f"{s}#{TAG[0]}"

    def ts_op(out, in0, s1, s2, op0, op1=None, reads=(), writes=(), tag=""):
        def f(e, out=out, in0=in0, s1=s1, s2=s2, op0=op0, op1=op1):
            if op1 is None:
                return e.tensor_scalar(out=out, in0=in0, scalar1=s1, scalar2=None,
                                       op0=op0)
            return e.tensor_scalar(out=out, in0=in0, scalar1=s1, scalar2=s2,
                                   op0=op0, op1=op1)
        return kb.op("vector", f, reads, writes, tag=t(tag or "ts"))

    def tt_op(out, in0, in1, op, reads=(), writes=(), tag=""):
        def f(e, out=out, in0=in0, in1=in1, op=op):
            return e.tensor_tensor(out=out, in0=in0, in1=in1, op=op)
        return kb.op("vector", f, reads, writes, tag=t(tag or "tt"))

    def stt_op(out, in0, sc, in1, op0, op1, reads=(), writes=(), tag=""):
        def f(e, out=out, in0=in0, sc=sc, in1=in1, op0=op0, op1=op1):
            return e.scalar_tensor_tensor(out=out, in0=in0, scalar=sc, in1=in1,
                                          op0=op0, op1=op1)
        return kb.op("vector", f, reads, writes, tag=t(tag or "stt"))

    def act_op(out, in_, func, bias=None, scale=None, reads=(), writes=(), tag=""):
        def f(e, out=out, in_=in_, func=func, bias=bias, scale=scale):
            kw = {}
            if bias is not None:
                kw["bias"] = bias
            if scale is not None:
                kw["scale"] = scale
            return e.activation(out=out, in_=in_, func=func, **kw)
        return kb.op("scalar", f, reads, writes, tag=t(tag or "act"))

    def copy_op(out, in_, reads=(), writes=(), tag=""):
        def f(e, out=out, in_=in_):
            return e.tensor_copy(out=out, in_=in_)
        return kb.op("vector", f, reads, writes, tag=t(tag or "cp"))

    def memset_op(ap, val, writes=(), tag=""):
        def f(e, ap=ap, val=val):
            return e.memset(ap, val)
        return kb.op("vector", f, (), writes, tag=t(tag or "ms"))

    def mm(out, lhsT, rhs, start, stop, reads=(), writes=(), tag=""):
        def f(e, out=out, lhsT=lhsT, rhs=rhs, start=start, stop=stop):
            return e.matmul(out, lhsT, rhs, start=start, stop=stop)
        return kb.op("tensor", f, reads, writes, tag=t(tag or "mm"))

    def mm3(ps, lhs_h, lhs_l, rhs_h, rhs_l, nk, reads, writes, tag=""):
        passes = ((lhs_h, rhs_h), (lhs_l, rhs_h), (lhs_h, rhs_l))
        for pi, (lf, rf) in enumerate(passes):
            for i in range(nk):
                mm(ps, lf(i), rf(i), pi == 0 and i == 0,
                   pi == 2 and i == nk - 1, reads=reads, writes=writes, tag=tag)

    def scan_op(out, data0, data1, initial, reads=(), writes=(), tag=""):
        def f(e, out=out, d0=data0, d1=data1, init=initial):
            return e.tensor_tensor_scan(out=out, data0=d0, data1=d1,
                                        initial=init, op0=ALU.mult, op1=ALU.add)
        return kb.op("vector", f, reads, writes, tag=t(tag or "scan"))

    def dma(eng, slot, out, in_, reads=(), writes=(), new_group=False, slow=False,
            tag=""):
        def f(e, out=out, in_=in_, slow=slow):
            if slow:
                return e.dma_start(out=out, in_=in_, allow_slow_non_contiguous=True)
            return e.dma_start(out=out, in_=in_)
        return kb.dma(eng, slot, f, reads, writes, new_group=new_group,
                      tag=t(tag or ("dma_" + slot)))

    def split(hi, lo, src, skey, hkey, lkey, tag=""):
        act_op(hi, src, AF.Copy, reads=(skey,), writes=(hkey,), tag=tag + "hi")
        tt_op(lo, src, hi, ALU.subtract, reads=(skey, hkey), writes=(lkey,),
              tag=tag + "lo")

    for ap_, d_, key in ((v_sd, d_sdf, "c:sd"), (v_sf, d_sff, "c:sf"),
                         (v_mk, d_mk, "c:mk"), (v_mv, d_mv, "c:mv"),
                         (v_mr, d_mr, "c:mr"), (v_b1, d_b1, "c:b1"),
                         (v_th, d_th, "c:th"), (v_qr, d_qr, "c:qr"),
                         (v_qi, d_qi, "c:qi"), (v_b2, d_b2, "c:b2"),
                         (r_sd, d_sdr, "c:sdrow"), (r_i256, d_i256, "c:i256"),
                         (r_i8, d_i8, "c:i8"), (r_ones1, d_ones1, "c:ones1"),
                         (r_invC, d_invC, "c:invC"), (v_onesP, d_onesP, "c:onesP"),
                         (v_mask, d_mask, "c:mask")):
        dma("sync", "cin", ap_, d_, writes=(key,), tag="cld")

    memset_op(v_eps, eps, writes=("c:eps",), tag="epsfill")
    for v_m, v_om, km, ko in ((v_mk, v_omk, "c:mk", "c:omk"),
                              (v_mv, v_omv, "c:mv", "c:omv"),
                              (v_mr, v_omr, "c:mr", "c:omr")):
        ts_op(v_om, v_m, -1.0, 1.0, ALU.mult, ALU.add, reads=(km,), writes=(ko,),
              tag="om")
    act_op(v_eu, v_sf, AF.Exp, scale=cfg["WSC"], reads=("c:sf",), writes=("c:eu",),
           tag="eu")
    act_op(v_ewcol, v_sd, AF.Exp, scale=cfg["WSC"], reads=("c:sd",), writes=("c:ew",),
           tag="ew")
    act_op(v_ew2, v_sd, AF.Exp, scale=float(TLOC) * cfg["WSC"], reads=("c:sd",),
           writes=("c:ew2",), tag="ew2")
    ts_op(v_ew2, v_ew2, -1.0, None, ALU.add, reads=("c:ew2",), writes=("c:ew2",),
          tag="ew2m1")
    for r in range(4):
        copy_op(v_ew2r[:, r * OG:(r + 1) * OG], v_ew2, reads=("c:ew2",),
                writes=("c:ew2r",), tag="ew2rep")
    PI = float(np.float32(np.pi))
    TWO_PI = 2.0 * np.pi
    c1 = 6.28125
    c2 = float(np.float32(TWO_PI - c1))
    c3 = float(np.float32(TWO_PI - c1 - float(np.float32(TWO_PI - c1))))
    for _ in range(7):
        ts_op(v_tmp32, v_th, PI, None, ALU.is_gt, reads=("c:th",), writes=("c:g",),
              tag="rrgt")
        for cc_ in (c1, c2, c3):
            stt_op(v_th, v_tmp32, -cc_, v_th, ALU.mult, ALU.add,
                   reads=("c:g", "c:th"), writes=("c:th",), tag="rrsub")
        ts_op(v_tmp32, v_th, -PI, None, ALU.is_lt, reads=("c:th",), writes=("c:g",),
              tag="rrlt")
        for cc_ in (c1, c2, c3):
            stt_op(v_th, v_tmp32, cc_, v_th, ALU.mult, ALU.add,
                   reads=("c:g", "c:th"), writes=("c:th",), tag="rradd")
    act_op(v_sc, v_th, AF.Sin, reads=("c:th",), writes=("c:sc",), tag="sinth")
    act_op(v_cth, v_th, AF.Sin, scale=0.5, reads=("c:th",), writes=("c:cth",),
           tag="sinth2")
    tt_op(v_cth, v_cth, v_cth, ALU.mult, reads=("c:cth",), writes=("c:cth",), tag="csq")
    ts_op(v_cth, v_cth, -2.0, 1.0, ALU.mult, ALU.add, reads=("c:cth",),
          writes=("c:cth",), tag="costh")
    P1H = float(np.float32(0.1))
    P1L = float(0.1 - P1H)
    tt_op(v_aqh, v_cth, v_qr, ALU.mult, reads=("c:cth", "c:qr"), writes=("c:aqh",),
          tag="qp1")
    tt_op(v_aql, v_sc, v_qi, ALU.mult, reads=("c:sc", "c:qi"), writes=("c:aql",),
          tag="qp2")
    tt_op(v_aqh, v_aqh, v_aql, ALU.add, reads=("c:aqh", "c:aql"), writes=("c:aqh",),
          tag="qp3")
    tt_op(v_tmp32, v_aqh, v_qi, ALU.mult, reads=("c:aqh", "c:qi"), writes=("c:t3",),
          tag="qp4")
    tt_op(v_aqh, v_tmp32, v_sc, ALU.mult, reads=("c:t3", "c:sc"), writes=("c:e1",),
          tag="e1")
    ts_op(v_E, v_aqh, P1H, None, ALU.mult, reads=("c:e1",), writes=("c:E",),
          tag="Eh")
    stt_op(v_E, v_aqh, P1L, v_E, ALU.mult, ALU.add, reads=("c:e1", "c:E"),
           writes=("c:E",), tag="El")
    tt_op(v_aqh, v_tmp32, v_cth, ALU.mult, reads=("c:t3", "c:cth"), writes=("c:e1",),
          tag="f1")
    ts_op(v_F, v_aqh, P1H, None, ALU.mult, reads=("c:e1",), writes=("c:F",),
          tag="Fh")
    stt_op(v_F, v_aqh, P1L, v_F, ALU.mult, ALU.add, reads=("c:e1", "c:F"),
           writes=("c:F",), tag="Fl")

    def ln_stats(xv, width, sqv, sq_key, x_key, tagp, packed=False, packed4=False,
                 sq_vec=False):
        if sq_vec:
            tt_op(sqv[:, 0:OG * width], xv[:, 0:OG * width], xv[:, 0:OG * width],
                  ALU.mult, reads=(x_key,), writes=(sq_key,), tag=tagp + "sq")
        else:
            act_op(sqv[:, 0:OG * width], xv[:, 0:OG * width], AF.Square,
                   reads=(x_key,), writes=(sq_key,), tag=tagp + "sq")
        if packed4:
            assert width <= 256
            rsum = pb[4][0:1, 0:width]
            rsq = pb[5][0:1, 0:width]
            ksum, ksq = "pb4", "pb5"
            mrep = pb[4][:, 256:256 + width]
            rrep = pb[5][:, 256:256 + width]
        elif packed:
            assert width <= 256
            rsum = pb[6][0:1, 0:width]
            rsq = pb[7][0:1, 0:width]
            ksum, ksq = "pb6", "pb7"
            mrep = pb[6][:, 256:256 + width]
            rrep = pb[7][:, 256:256 + width]
        else:
            rsum = pb[4][0:1, 0:width]
            rsq = pb[5][0:1, 0:width]
            ksum, ksq = "pb4", "pb5"
            mrep = pb[6][:, 0:width]
            rrep = pb[7][:, 0:width]
        for cg in range(CG):
            mm(rsum, v_onesP, xv[:, cg * width:(cg + 1) * width],
               cg == 0, cg == CG - 1, reads=("c:onesP", x_key), writes=(ksum,),
               tag=tagp + "summ")
        for cg in range(CG):
            mm(rsq, v_onesP, sqv[:, cg * width:(cg + 1) * width],
               cg == 0, cg == CG - 1, reads=("c:onesP", sq_key), writes=(ksq,),
               tag=tagp + "sqmm")
        m_row = r_rows[0:1, 0:width]
        v_row = r_rows[0:1, TSH:TSH + width]
        s_row = r_rows[0:1, 2 * TSH:2 * TSH + width]
        ts_op(m_row, rsum, invC, None, ALU.mult, reads=(ksum,), writes=("rows_m",),
              tag=tagp + "m")
        tt_op(v_row, m_row, m_row, ALU.mult, reads=("rows_m",), writes=("rows_v",),
              tag=tagp + "msq")
        stt_op(v_row, rsq, invC, v_row, ALU.mult, ALU.subtract,
               reads=(ksq, "rows_v"), writes=("rows_v",), tag=tagp + "var")
        act_op(v_row, v_row, AF.Ln, bias=v_eps[0:1, 0:1], reads=("rows_v", "c:eps"),
               writes=("rows_v",), tag=tagp + "lnv")
        act_op(s_row, v_row, AF.Exp, scale=-0.5, reads=("rows_v",), writes=("rows_s",),
               tag=tagp + "rstd")
        mm(mrep, r_ones1, m_row, True, True,
           reads=("c:ones1", "rows_m"), writes=(ksum if (packed or packed4) else "pb6",),
           tag=tagp + "mrep")
        mm(rrep, r_ones1, s_row, True, True,
           reads=("c:ones1", "rows_s"), writes=(ksq if (packed or packed4) else "pb7",),
           tag=tagp + "rrep")
        return mrep, rrep

    def weight_split(d_src, hi, lo, hkey, lkey, scr_list, scr_keys, slot, tag):
        nch = CG // 2
        for i in range(nch):
            scr = scr_list[i % len(scr_list)][:, 0:2 * C]
            skey = scr_keys[i % len(scr_keys)]
            dma("sync", slot, scr.rearrange("p (cg o) -> p cg o", cg=2),
                d_src[i * 256:(i + 1) * 256, :].rearrange("(cg p) o -> p cg o", p=128),
                reads=(), writes=(skey,), new_group=True, tag=tag + "ld")
            split(hi[:, i * 2 * C:(i + 1) * 2 * C], lo[:, i * 2 * C:(i + 1) * 2 * C],
                  scr, skey, hkey, lkey, tag=tag + "sp")

    phase_reset()
    a_x = [alloc(OG * TSH) for _ in range(2)]
    a_sq = alloc(OG * TSH)
    a_n = [alloc(OG * TSH) for _ in range(2)]
    a_d = alloc(4 * TS)
    a_mscr = alloc(4 * TS)
    a_ek = alloc(OG * TS)
    a_ekv = alloc(OG * TS)
    a_asc = alloc(OG * (TS + 1))
    a_bsc = alloc(OG * (TS + 1))
    a_ewrep = alloc(OG * TS)
    a_tots = alloc(4 * OG)
    a_gath = alloc(2 * B * NC * OG)
    a_Atmp = alloc(4 * OG)
    a_wkh = allocb(OG * C); a_wkl = allocb(OG * C)
    a_wvh = allocb(OG * C); a_wvl = allocb(OG * C)
    a_xkp = [[allocb(4 * TS), allocb(4 * TS)] for _ in range(2)]
    a_xvp = [[allocb(4 * TS), allocb(4 * TS)] for _ in range(2)]
    a_xrp = [[allocb(4 * TS), allocb(4 * TS)] for _ in range(2)]
    a_np = [[allocb(OG * TSH), allocb(OG * TSH)] for _ in range(2)]

    weight_split(d_wk, a_wkh, a_wkl, "wkh", "wkl", [a_x[0], a_x[1]], ["x0", "x1"],
                 "wsp", "wk")
    weight_split(d_wv, a_wvh, a_wvl, "wvh", "wvl", [a_x[0], a_x[1]], ["x0", "x1"],
                 "wsp", "wv")

    memset_op(a_ewrep, 1.0, writes=("ewrep",), tag="ewrep0")
    for og in range(OG):
        ts_op(a_ewrep[:, og * TS:(og + 1) * TS], a_ewrep[:, og * TS:(og + 1) * TS],
              v_ewcol[:, og:og + 1], None, ALU.mult,
              reads=("ewrep", "c:ew"), writes=("ewrep",), tag="ewrep")

    def p1a_A(u):
        b, s = divmod(u, NSUP)
        par = u % 2
        xv, nv = a_x[par], a_n[par]
        xkey, nkey = f"x{par}", f"n{par}"
        dma("sync", f"xin{par}", xv.rearrange("p (g w) -> p g w", g=OG),
            d_x[b, :, s * TS:s * TS + TSH].rearrange("(g p) w -> p g w", p=128),
            writes=(xkey,), new_group=True, tag="xload")
        mrep, rrep = ln_stats(xv, TSH, a_sq, "sq", xkey, "l1")
        nv3 = nv.rearrange("p (g w) -> p g w", g=OG)
        tt_op(nv3, xv.rearrange("p (g w) -> p g w", g=OG),
              mrep.unsqueeze(1).broadcast_to([128, OG, TSH]), ALU.subtract,
              reads=(xkey, "pb6"), writes=(nkey,), tag="ncent")
        tt_op(nv3, nv3, rrep.unsqueeze(1).broadcast_to([128, OG, TSH]), ALU.mult,
              reads=(nkey, "pb7"), writes=(nkey,), tag="nmul")
        split(a_np[par][0], a_np[par][1], nv, nkey, f"nh{par}", f"nl{par}", tag="nsp")
        for cg in range(4):
            off = cg * TSH
            sh = nv[:, off:off + TS] if cg < 2 else nv[:, off + 2:off + TS + 2]
            mid = nv[:, off + 1:off + TS + 1]
            dsl = a_d[:, cg * TS:(cg + 1) * TS]
            tt_op(dsl, sh, mid, ALU.subtract, reads=(nkey,), writes=("d",),
                  tag="dshift")
        for om_, pr, kk in ((v_omk, a_xkp[par], f"xk{par}"),
                            (v_omv, a_xvp[par], f"xv{par}"),
                            (v_omr, a_xrp[par], f"xr{par}")):
            ko = {id(v_omk): "c:omk", id(v_omv): "c:omv", id(v_omr): "c:omr"}[id(om_)]
            for cg in range(4):
                off = cg * TSH
                mid = nv[:, off + 1:off + TS + 1]
                stt_op(a_mscr[:, cg * TS:(cg + 1) * TS],
                       a_d[:, cg * TS:(cg + 1) * TS], om_[:, cg:cg + 1], mid,
                       ALU.mult, ALU.add, reads=("d", nkey, ko), writes=("mscr",),
                       tag="mix")
            split(pr[0], pr[1], a_mscr, "mscr", kk + "h", kk + "l", tag="xsp")

    def p1a_B(u):
        b, s = divmod(u, NSUP)
        par = u % 2
        nkey = f"n{par}"

        def rhs_pair(pref, half):
            def f(cg, pref=pref, half=half):
                if cg < 4:
                    src = (a_xkp if pref == "k" else a_xvp)[par][half]
                    return src[:, cg * TS:(cg + 1) * TS]
                nph = a_np[par][half]
                return nph[:, cg * TSH + 1:cg * TSH + 1 + TS]
            return f

        rk_h = (f"xk{par}h", f"nh{par}")
        rk_l = (f"xk{par}l", f"nl{par}")
        rv_h = (f"xv{par}h", f"nh{par}")
        rv_l = (f"xv{par}l", f"nl{par}")
        for og in range(OG):
            bk = (2 * og) % 6
            bkv = (2 * og + 1) % 6
            psk = pb[bk][:, 0:TS]
            psv = pb[bkv][:, 0:TS]
            mm3(psk,
                lambda cg, og=og: a_wkh[:, cg * C + og * 128:cg * C + (og + 1) * 128],
                lambda cg, og=og: a_wkl[:, cg * C + og * 128:cg * C + (og + 1) * 128],
                rhs_pair("k", 0), rhs_pair("k", 1), CG,
                reads=("wkh", "wkl") + rk_h + rk_l, writes=(f"pb{bk}",), tag="kmm")
            mm3(psv,
                lambda cg, og=og: a_wvh[:, cg * C + og * 128:cg * C + (og + 1) * 128],
                lambda cg, og=og: a_wvl[:, cg * C + og * 128:cg * C + (og + 1) * 128],
                rhs_pair("v", 0), rhs_pair("v", 1), CG,
                reads=("wvh", "wvl") + rv_h + rv_l, writes=(f"pb{bkv}",), tag="vmm")
            eksl = a_ek[:, og * TS:(og + 1) * TS]
            ekvsl = a_ekv[:, og * TS:(og + 1) * TS]
            kek, kekv = f"ek{og}", f"ekv{og}"
            act_op(eksl, psk, AF.Exp, reads=(f"pb{bk}",), writes=(kek,), tag="ek")
            tt_op(ekvsl, eksl, psv, ALU.mult, reads=(kek, f"pb{bkv}"),
                  writes=(kekv,), tag="ekv")
            W1p = TS + 1
            asl = a_asc[:, og * W1p:(og + 1) * W1p]
            bsl = a_bsc[:, og * W1p:(og + 1) * W1p]
            ka, kbk = f"asc{og}", f"bsc{og}"
            if s == 0:
                memset_op(asl[:, 0:1], 0.0, writes=(ka,), tag="acar0")
                memset_op(bsl[:, 0:1], 0.0, writes=(kbk,), tag="bcar0")
            else:
                copy_op(asl[:, 0:1], asl[:, TS:TS + 1], reads=(ka,), writes=(ka,),
                        tag="acar")
                copy_op(bsl[:, 0:1], bsl[:, TS:TS + 1], reads=(kbk,), writes=(kbk,),
                        tag="bcar")
            scan_op(asl[:, 1:TS + 1], a_ewrep[:, og * TS:(og + 1) * TS], ekvsl,
                    asl[:, 0:1], reads=("ewrep", kekv, ka), writes=(ka,), tag="ascan")
            scan_op(bsl[:, 1:TS + 1], a_ewrep[:, og * TS:(og + 1) * TS], eksl,
                    bsl[:, 0:1], reads=("ewrep", kek, kbk), writes=(kbk,), tag="bscan")
            stt_op(asl[:, 0:TS], ekvsl, v_eu[:, og:og + 1],
                   asl[:, 0:TS], ALU.mult, ALU.add,
                   reads=(kekv, "c:eu", ka), writes=(ka,), tag="numloc")
            stt_op(bsl[:, 0:TS], eksl, v_eu[:, og:og + 1],
                   bsl[:, 0:TS], ALU.mult, ALU.add,
                   reads=(kek, "c:eu", kbk), writes=(kbk,), tag="denloc")
            if s == NSUP - 1:
                copy_op(a_tots[:, (2 * b) * OG + og:(2 * b) * OG + og + 1],
                        asl[:, TS:TS + 1], reads=(ka,), writes=("tots",), tag="tota")
                copy_op(a_tots[:, (2 * b + 1) * OG + og:(2 * b + 1) * OG + og + 1],
                        bsl[:, TS:TS + 1], reads=(kbk,), writes=("tots",), tag="totb")
        allk = tuple(f"asc{og}" for og in range(OG))
        allkb = tuple(f"bsc{og}" for og in range(OG))
        dma("gpsimd", f"sp{par}", d_num.ap()[u].rearrange("p (og w) -> p og w", og=OG),
            a_asc.rearrange("p (og w) -> p og w", og=OG)[:, :, 0:TS], reads=allk,
            writes=("D:num",), new_group=True, tag="spnum")
        dma("gpsimd", f"sp{par}", d_den.ap()[u].rearrange("p (og w) -> p og w", og=OG),
            a_bsc.rearrange("p (og w) -> p og w", og=OG)[:, :, 0:TS], reads=allkb,
            writes=("D:den",), tag="spden")
        dma("gpsimd", f"sp{par}", d_nh.ap()[u], a_np[par][0], reads=(f"nh{par}",),
            writes=("D:nh",), tag="spnh")
        dma("gpsimd", f"sp{par}", d_nl.ap()[u], a_np[par][1], reads=(f"nl{par}",),
            writes=("D:nl",), tag="spnl")
        dma("gpsimd", f"sp{par}", d_xrh.ap()[u], a_xrp[par][0], reads=(f"xr{par}h",),
            writes=("D:xrh",), tag="spxrh")
        dma("gpsimd", f"sp{par}", d_xrl.ap()[u], a_xrp[par][1], reads=(f"xr{par}l",),
            writes=("D:xrl",), tag="spxrl")

    p1a_A(0)
    p1a_A(1)
    for u in range(NSUP2):
        p1a_B(u)
        if u + 2 < NSUP2:
            p1a_A(u + 2)

    dma("gpsimd", "tot", d_tot.ap(), a_tots, reads=("tots",),
        writes=("D:tot",), new_group=True, tag="totst")

    def f_cc(e):
        return e.collective_compute(
            "AllGather", ALU.bypass, replica_groups=[list(range(NC))],
            ins=[d_tot.ap().opt()], outs=[d_gat.ap().opt()])
    kb.manual("gpsimd", f_cc, reads=("D:tot",), writes=("D:gat",), semname="cc",
              inc=1, tag="allgather")

    dma("gpsimd", "gat",
        a_gath.rearrange("p (j w) -> p j w", j=NC),
        d_gat.ap().rearrange("(j p) w -> p j w", p=128),
        reads=("D:gat",), writes=("gath",), new_group=True, slow=True, tag="gatld")

    memset_op(v_A, 0.0, writes=("A",), tag="A0")
    for j in range(NC - 1):
        ts_op(a_Atmp, v_ew2r, v_mask[:, j:j + 1], 1.0, ALU.mult, ALU.add,
              reads=("c:ew2r", "c:mask"), writes=("Atmp",), tag="dsel")
        tt_op(v_A, v_A, a_Atmp, ALU.mult, reads=("A", "Atmp"), writes=("A",),
              tag="Amul")
        ts_op(a_Atmp, a_gath[:, j * (2 * B) * OG:(j + 1) * (2 * B) * OG],
              v_mask[:, j:j + 1], None, ALU.mult, reads=("gath", "c:mask"),
              writes=("Atmp",), tag="Tmask")
        tt_op(v_A, v_A, a_Atmp, ALU.add, reads=("A", "Atmp"), writes=("A",),
              tag="Aadd")

    kb.barrier()
    phase_reset()
    b_num = [alloc(OG * TS) for _ in range(2)]
    b_den = [alloc(OG * TS) for _ in range(2)]
    b_x = [alloc(OG * TS) for _ in range(2)]
    b_sr = alloc(OG * TS)
    b_attn = alloc(OG * TS)
    b_xp = [alloc(OG * TS) for _ in range(2)]
    b_ewpb = alloc(OG * TS)
    b_ews = alloc(NSUP * OG)
    b_As = [[alloc(OG), alloc(OG)] for _ in range(2)]
    b_wrh = allocb(OG * C); b_wrl = allocb(OG * C)
    b_woh = allocb(OG * C); b_wol = allocb(OG * C)
    b_np = [[allocb(OG * TSH), allocb(OG * TSH)] for _ in range(2)]
    b_xrp = [[allocb(4 * TS), allocb(4 * TS)] for _ in range(2)]
    b_gp = [allocb(OG * TS), allocb(OG * TS)]

    weight_split(d_wr, b_wrh, b_wrl, "wrh", "wrl", [b_num[0], b_num[1]],
                 ["bnum0", "bnum1"], "wsp2", "wr")
    weight_split(d_wo, b_woh, b_wol, "woh", "wol", [b_num[0], b_num[1]],
                 ["bnum0", "bnum1"], "wsp2", "wo")

    for og in range(OG):
        ps = pb[og % 2][:, 0:TS]
        mm(ps, r_sd[0:1, og * 128:(og + 1) * 128], r_i256, True, True,
           reads=("c:sdrow", "c:i256"), writes=(f"pb{og % 2}",), tag="ewpmm")
        act_op(b_ewpb[:, og * TS:(og + 1) * TS], ps, AF.Exp, scale=cfg["WSC"],
               reads=(f"pb{og % 2}",), writes=("ewpb",), tag="ewpexp")
    for og in range(OG):
        ps = pb[og % 2][:, 0:NSUP]
        mm(ps, r_sd[0:1, og * 128:(og + 1) * 128], r_i8, True, True,
           reads=("c:sdrow", "c:i8"), writes=(f"pb{og % 2}",), tag="ewsmm")
        act_op(b_ews[:, og * NSUP:(og + 1) * NSUP], ps, AF.Exp,
               scale=float(TS) * cfg["WSC"], reads=(f"pb{og % 2}",), writes=("ews",),
               tag="ewsexp")
    ews_v = b_ews.rearrange("p (og s) -> p og s", og=OG)

    def p1b_Aload(u):
        b, s = divmod(u, NSUP)
        par = u % 2
        numv, denv = b_num[par], b_den[par]
        dma("sync", f"ld1b{par}", numv, d_num.ap()[u], reads=("D:num",),
            writes=(f"bnum{par}",), new_group=True, tag="ldnum")
        dma("sync", f"ld1b{par}", denv, d_den.ap()[u], reads=("D:den",),
            writes=(f"bden{par}",), tag="ldden")
        dma("sync", f"ld1b{par}", b_np[par][0], d_nh.ap()[u], reads=("D:nh",),
            writes=(f"bnh{par}",), tag="ldnh")
        dma("sync", f"ld1b{par}", b_np[par][1], d_nl.ap()[u], reads=("D:nl",),
            writes=(f"bnl{par}",), tag="ldnl")
        dma("sync", f"ld1b{par}", b_xrp[par][0], d_xrh.ap()[u], reads=("D:xrh",),
            writes=(f"bxrh{par}",), tag="ldxrh")
        dma("sync", f"ld1b{par}", b_xrp[par][1], d_xrl.ap()[u], reads=("D:xrl",),
            writes=(f"bxrl{par}",), tag="ldxrl")
        dma("sync", f"ld1b{par}", b_x[par].rearrange("p (g w) -> p g w", g=OG),
            d_x[b, :, s * TS + 1:s * TS + 1 + TS].rearrange("(g p) w -> p g w", p=128),
            reads=(), writes=(f"bx{par}",), tag="ldx")

    def p1b_Afix(u):
        b, s = divmod(u, NSUP)
        par = u % 2
        numv, denv = b_num[par], b_den[par]
        for kind in range(2):
            r = 2 * b + kind
            tt_op(b_As[par][kind], v_A[:, r * OG:(r + 1) * OG], ews_v[:, :, s],
                  ALU.mult, reads=("A", "ews"), writes=(f"As{par}{kind}",),
                  tag="Ascale")
        for og in range(OG):
            sl = slice(og * TS, (og + 1) * TS)
            stt_op(numv[:, sl], b_ewpb[:, sl], b_As[par][0][:, og:og + 1],
                   numv[:, sl], ALU.mult, ALU.add,
                   reads=("ewpb", f"As{par}0", f"bnum{par}"),
                   writes=(f"bnum{par}",), tag="fixn")
            stt_op(denv[:, sl], b_ewpb[:, sl], b_As[par][1][:, og:og + 1],
                   denv[:, sl], ALU.mult, ALU.add,
                   reads=("ewpb", f"As{par}1", f"bden{par}"),
                   writes=(f"bden{par}",), tag="fixd")
        act_op(denv, denv, AF.Ln, reads=(f"bden{par}",), writes=(f"bden{par}",),
               tag="lnden")
        act_op(denv, denv, AF.Exp, scale=-1.0, reads=(f"bden{par}",),
               writes=(f"bden{par}",), tag="rden")
        tt_op(numv, numv, denv, ALU.mult, reads=(f"bnum{par}", f"bden{par}"),
              writes=(f"bnum{par}",), tag="wkv")

    def p1b_B(u):
        b, s = divmod(u, NSUP)
        par = u % 2
        numv, xpv = b_num[par], b_xp[par]

        def rrhs(half):
            def f(cg, half=half):
                if cg < 4:
                    return b_xrp[par][half][:, cg * TS:(cg + 1) * TS]
                return b_np[par][half][:, cg * TSH + 1:cg * TSH + 1 + TS]
            return f

        rkeys = (f"bxrh{par}", f"bxrl{par}", f"bnh{par}", f"bnl{par}")
        for og in range(OG):
            bk = og % 4
            ps = pb[bk][:, 0:TS]
            mm3(ps,
                lambda cg, og=og: b_wrh[:, cg * C + og * 128:cg * C + (og + 1) * 128],
                lambda cg, og=og: b_wrl[:, cg * C + og * 128:cg * C + (og + 1) * 128],
                rrhs(0), rrhs(1), CG,
                reads=("wrh", "wrl") + rkeys, writes=(f"pb{bk}",), tag="rmm")
            act_op(b_sr[:, og * TS:(og + 1) * TS], ps, AF.Sigmoid,
                   reads=(f"pb{bk}",), writes=("sr",), tag="sr")
            sl = slice(og * TS, (og + 1) * TS)
            tt_op(numv[:, sl], numv[:, sl], b_sr[:, sl], ALU.mult,
                  reads=(f"bnum{par}", "sr"), writes=(f"bnum{par}",), tag="gated")
            split(b_gp[0][:, sl], b_gp[1][:, sl], numv[:, sl], f"bnum{par}",
                  "gph", "gpl", tag="gsp")
        for dg in range(DG):
            bk = 4 + dg % 2
            ps = pb[bk][:, 0:TS]
            mm3(ps,
                lambda og, dg=dg: b_woh[:, og * C + dg * 128:og * C + (dg + 1) * 128],
                lambda og, dg=dg: b_wol[:, og * C + dg * 128:og * C + (dg + 1) * 128],
                lambda og: b_gp[0][:, og * TS:(og + 1) * TS],
                lambda og: b_gp[1][:, og * TS:(og + 1) * TS], OG,
                reads=("woh", "wol", "gph", "gpl"), writes=(f"pb{bk}",), tag="womm")
            act_op(b_attn[:, dg * TS:(dg + 1) * TS], ps, AF.Copy,
                   reads=(f"pb{bk}",), writes=("attn",), tag="attnev")
        mrep, rrep = ln_stats(b_attn, TS, xpv, f"bxp{par}", "attn", "l2",
                              packed=True)
        at3 = b_attn.rearrange("p (g w) -> p g w", g=OG)
        tt_op(at3, at3, mrep.unsqueeze(1).broadcast_to([128, OG, TS]), ALU.subtract,
              reads=("attn", "pb6"), writes=("attn",), tag="pcent")
        tt_op(at3, at3, rrep.unsqueeze(1).broadcast_to([128, OG, TS]), ALU.mult,
              reads=("attn", "pb7"), writes=("attn",), tag="pmul")
        tt_op(xpv, b_attn, b_x[par], ALU.add,
              reads=("attn", f"bx{par}"), writes=(f"bxp{par}",), tag="resid")
        dma("gpsimd", f"spx{par}", d_xp.ap()[u], xpv, reads=(f"bxp{par}",),
            writes=("D:xp",), new_group=True, tag="spxp")

    p1b_Aload(0)
    p1b_Aload(1)
    p1b_Afix(0)
    for u in range(NSUP2):
        p1b_B(u)
        if u + 1 < NSUP2:
            p1b_Afix(u + 1)
        if u + 2 < NSUP2:
            p1b_Aload(u + 2)

    kb.barrier()
    phase_reset()
    c_q = alloc(GRP2 * TS)
    c_ws = alloc(2 * C)
    c_xp = [alloc(OG * TS) for _ in range(4)]
    c_n2 = alloc(OG * TS)
    c_out = alloc(OG * TS)
    c_hb = [alloc(GRP2 * TS) for _ in range(2)]
    c_sh = alloc(GRP2 * TS)
    c_rows2 = alloc(2 * TS)
    c_w1 = [allocb(2 * C) for _ in range(2)]
    c_w2s = [allocb(2 * C) for _ in range(2)]
    c_actp = [[allocb(FG * TS), allocb(FG * TS)] for _ in range(2)]
    c_n2p = [[allocb(OG * TS), allocb(OG * TS)] for _ in range(2)]

    def p2_Pload(u):
        par = u % 4
        dma("sync", f"xpin{par}", c_xp[par], d_xp.ap()[u], reads=("D:xp",),
            writes=(f"cxp{par}",), new_group=True, tag="ldxp")

    def p2_Pstats(u):
        par = u % 4
        mem = u % 2
        xpv = c_xp[par]
        xkey = f"cxp{par}"
        tt_op(c_n2, xpv, xpv, ALU.mult, reads=(xkey,), writes=("n2f",),
              tag="l3sq")
        rsum = pb[4][0:1, mem * TS:(mem + 1) * TS]
        rsq = pb[5][0:1, mem * TS:(mem + 1) * TS]
        for cg in range(CG):
            mm(rsum, v_onesP, xpv[:, cg * TS:(cg + 1) * TS], cg == 0,
               cg == CG - 1, reads=("c:onesP", xkey), writes=("pb4",),
               tag="l3summ")
        for cg in range(CG):
            mm(rsq, v_onesP, c_n2[:, cg * TS:(cg + 1) * TS], cg == 0,
               cg == CG - 1, reads=("c:onesP", "n2f"), writes=("pb5",),
               tag="l3sqmm")
        if mem == 0:
            m_row = r_rows[0:1, 0:TS]
            s_row = r_rows[0:1, 2 * TSH:2 * TSH + TS]
            mkey, skey = "rows_m", "rows_s"
        else:
            m_row = c_rows2[0:1, 0:TS]
            s_row = c_rows2[0:1, TS:2 * TS]
            mkey, skey = "r2m", "r2s"
        v_row = r_rows[0:1, TSH:TSH + TS]
        ts_op(m_row, rsum, invC, None, ALU.mult, reads=("pb4",), writes=(mkey,),
              tag="l3m")
        tt_op(v_row, m_row, m_row, ALU.mult, reads=(mkey,), writes=("rows_v",),
              tag="l3msq")
        stt_op(v_row, rsq, invC, v_row, ALU.mult, ALU.subtract,
               reads=("pb5", "rows_v"), writes=("rows_v",), tag="l3var")
        act_op(v_row, v_row, AF.Ln, bias=v_eps[0:1, 0:1],
               reads=("rows_v", "c:eps"), writes=("rows_v",), tag="l3lnv")
        act_op(s_row, v_row, AF.Exp, scale=-0.5, reads=("rows_v",),
               writes=(skey,), tag="l3rstd")
        mm(pb[6][:, mem * TS:(mem + 1) * TS], r_ones1, m_row, mem == 0,
           mem == 1, reads=("c:ones1", mkey), writes=("pb6",), tag="l3mrep")
        mm(pb[7][:, mem * TS:(mem + 1) * TS], r_ones1, s_row, mem == 0,
           mem == 1, reads=("c:ones1", skey), writes=("pb7",), tag="l3rrep")

    def p2_Papply(u):
        par = u % 4
        mem = u % 2
        xpv = c_xp[par]
        xkey = f"cxp{par}"
        mrep = pb[6][:, mem * TS:(mem + 1) * TS]
        rrep = pb[7][:, mem * TS:(mem + 1) * TS]
        n23 = c_n2.rearrange("p (g w) -> p g w", g=OG)
        tt_op(n23, xpv.rearrange("p (g w) -> p g w", g=OG),
              mrep.unsqueeze(1).broadcast_to([128, OG, TS]), ALU.subtract,
              reads=(xkey, "pb6"), writes=("n2f",), tag="n2c")
        tt_op(n23, n23, rrep.unsqueeze(1).broadcast_to([128, OG, TS]), ALU.mult,
              reads=("n2f", "pb7"), writes=("n2f",), tag="n2m")
        split(c_n2p[mem][0], c_n2p[mem][1], c_n2, "n2f", f"n2h{mem}",
              f"n2l{mem}", tag="n2sp")

    def p2_FG(u):
        for grp in range(NGRP2):
            for fgl in range(GRP2):
                fg = grp * GRP2 + fgl
                wpar = fg % 2
                wpb = c_w1[wpar]
                if u == 0:
                    dma("sync", f"w1in{wpar}",
                        c_ws[:, 0:C].rearrange("p (cg f) -> p cg f", cg=CG),
                        d_w1[:, fg * 128:(fg + 1) * 128].rearrange(
                            "(cg p) f -> p cg f", p=128),
                        reads=("ws",), writes=("ws",), new_group=True, tag="w1ldf")
                    split(wpb[:, 0:C], wpb[:, C:2 * C], c_ws[:, 0:C], "ws",
                          f"w1p{wpar}", f"w1p{wpar}", tag="w1sp")
                    dma("gpsimd", "wst", d_w1s.ap()[fg], wpb,
                        reads=(f"w1p{wpar}",), writes=("D:w1s",),
                        new_group=True, tag="w1st")
                else:
                    dma("sync", f"w1in{wpar}", wpb, d_w1s.ap()[fg],
                        reads=("D:w1s",), writes=(f"w1p{wpar}",),
                        new_group=True, tag="w1ldp")
                hbk = fg % 6
                first = True
                for mem in range(2):
                    ps = pb[hbk][:, mem * TS:(mem + 1) * TS]
                    for pi, (lo_w, lo_r) in enumerate(((0, 0), (C, 0), (0, 1))):
                        for cg in range(CG):
                            mm(ps, wpb[:, lo_w + cg * 128:lo_w + (cg + 1) * 128],
                               c_n2p[mem][lo_r][:, cg * TS:(cg + 1) * TS],
                               first, mem == 1 and pi == 2 and cg == CG - 1,
                               reads=(f"w1p{wpar}", f"n2h{mem}", f"n2l{mem}"),
                               writes=(f"pb{hbk}",), tag="w1mm")
                            first = False
                for mem in range(2):
                    act_op(c_hb[mem][:, fgl * TS:(fgl + 1) * TS],
                           pb[hbk][:, mem * TS:(mem + 1) * TS], AF.Identity,
                           bias=v_b1[:, fg:fg + 1], reads=(f"pb{hbk}", "c:b1"),
                           writes=(f"hb{mem}f{fgl}",), tag="hbev")
            Ec = v_E[:, grp * GRP2:(grp + 1) * GRP2]
            Fc = v_F[:, grp * GRP2:(grp + 1) * GRP2]
            qb = (c_q, c_out)
            qk = ("q", "cout")
            shb = (c_sh, c_ws)
            shk = ("sh", "ws")
            for mem in range(2):
                hball = tuple(f"hb{mem}f{i}" for i in range(GRP2))
                ts_op(shb[mem], c_hb[mem], P1H, None, ALU.mult,
                      reads=hball, writes=(shk[mem],), tag="hh1")
                stt_op(shb[mem], c_hb[mem], P1L, shb[mem], ALU.mult, ALU.add,
                       reads=hball + (shk[mem],), writes=(shk[mem],), tag="hh2")
            for mem in range(2):
                act_op(qb[mem], shb[mem], AF.Sin, reads=(shk[mem],),
                       writes=(qk[mem],), tag="sinb")
            for mem in range(2):
                act_op(shb[mem], shb[mem], AF.Sin, scale=0.5, reads=(shk[mem],),
                       writes=(shk[mem],), tag="sinh2")
            for mem in range(2):
                act_op(shb[mem], shb[mem], AF.Square, reads=(shk[mem],),
                       writes=(shk[mem],), tag="shsq")
            for mem in range(2):
                act_op(shb[mem], shb[mem], AF.Copy, scale=-2.0, bias=1.0,
                       reads=(shk[mem],), writes=(shk[mem],), tag="cosb")
            for mem in range(2):
                for fgl in range(GRP2):
                    sl = slice(fgl * TS, (fgl + 1) * TS)
                    stt_op(qb[mem][:, sl], qb[mem][:, sl], Fc[:, fgl:fgl + 1],
                           c_hb[mem][:, sl], ALU.mult, ALU.add,
                           reads=(qk[mem], "c:F", f"hb{mem}f{fgl}"),
                           writes=(qk[mem],), tag="fsb")
            for mem in range(2):
                for fgl in range(GRP2):
                    sl = slice(fgl * TS, (fgl + 1) * TS)
                    stt_op(qb[mem][:, sl], shb[mem][:, sl], Ec[:, fgl:fgl + 1],
                           qb[mem][:, sl], ALU.mult, ALU.add,
                           reads=(qk[mem], "c:E", shk[mem]),
                           writes=(qk[mem],), tag="eq")
            for mem in range(2):
                akey = f"act{mem}g{grp}"
                a_h = c_actp[mem][0][:, grp * GRP2 * TS:(grp + 1) * GRP2 * TS]
                a_l = c_actp[mem][1][:, grp * GRP2 * TS:(grp + 1) * GRP2 * TS]
                act_op(a_h, qb[mem], AF.Relu, reads=(qk[mem],),
                       writes=(akey + "h",), tag="asphi")
                stt_op(a_l, qb[mem], 0.0, a_h, ALU.max, ALU.subtract,
                       reads=(qk[mem], akey + "h"), writes=(akey + "l",),
                       tag="asplo")

    def p2_DG(u):
        for fgl in range(FG):
            wpar = fgl % 2
            wpb = c_w2s[wpar]
            if u == 0:
                dma("sync", f"w2in{wpar}",
                    c_ws[:, 0:C], d_w2[fgl * 128:(fgl + 1) * 128, :].rearrange(
                        "(o p) d -> p (o d)", p=128),
                    reads=("ws",), writes=("ws",), new_group=True, tag="w2ldf")
                split(wpb[:, 0:C], wpb[:, C:2 * C], c_ws[:, 0:C], "ws",
                      f"w2p{wpar}", f"w2p{wpar}", tag="w2sp")
                dma("gpsimd", "wst", d_w2s.ap()[fgl], wpb,
                    reads=(f"w2p{wpar}",), writes=("D:w2s",),
                    new_group=True, tag="w2st")
            else:
                dma("sync", f"w2in{wpar}", wpb, d_w2s.ap()[fgl],
                    reads=("D:w2s",), writes=(f"w2p{wpar}",),
                    new_group=True, tag="w2ldp")
            for mem in range(2):
                akey = f"act{mem}g{fgl // GRP2}"
                ah = c_actp[mem][0][:, fgl * TS:(fgl + 1) * TS]
                al = c_actp[mem][1][:, fgl * TS:(fgl + 1) * TS]
                for dg in range(DG):
                    base = dg * 128
                    for pi, (off, ra) in enumerate(((0, ah), (C, ah), (0, al))):
                        mm(pb[dg][:, mem * TS:(mem + 1) * TS],
                           wpb[:, off + base:off + base + 128], ra,
                           fgl == 0 and mem == 0 and pi == 0,
                           fgl == FG - 1 and mem == 1 and pi == 2,
                           reads=(f"w2p{wpar}", akey + "h", akey + "l"),
                           writes=(f"pb{dg}",), tag="w2mm")

    def p2_Lpair(u):
        obuf = (c_out, c_q)
        okey = ("cout", "q")
        for mem in range(2):
            for dg in range(DG):
                ts_op(obuf[mem][:, dg * TS:(dg + 1) * TS],
                      pb[dg][:, mem * TS:(mem + 1) * TS],
                      v_b2[:, dg:dg + 1], None, ALU.add, reads=(f"pb{dg}", "c:b2"),
                      writes=(okey[mem],), tag="outev")
        for mem in range(2):
            um = u + mem
            par = um % 4
            b, s = divmod(um, NSUP)
            xkey = f"cxp{par}"
            ov = obuf[mem]
            ok = okey[mem]
            mrep, rrep = ln_stats(ov, TS, c_ws, "ws", ok, "l4", packed4=True)
            o3 = ov.rearrange("p (g w) -> p g w", g=OG)
            tt_op(o3, o3, mrep.unsqueeze(1).broadcast_to([128, OG, TS]),
                  ALU.subtract, reads=(ok, "pb4"), writes=(ok,), tag="fcent")
            tt_op(o3, o3, rrep.unsqueeze(1).broadcast_to([128, OG, TS]), ALU.mult,
                  reads=(ok, "pb5"), writes=(ok,), tag="fmul")
            tt_op(ov, ov, c_xp[par], ALU.add,
                  reads=(ok, xkey), writes=(ok,), tag="fresid")
            dma("gpsimd", "outst",
                d_out[b, :, s * TS:(s + 1) * TS].rearrange("(g p) w -> p g w", p=128),
                ov.rearrange("p (g w) -> p g w", g=OG), reads=(ok,),
                writes=("D:out",), new_group=True, tag="outst")

    NP2 = NSUP2 // 2
    p2_Pload(0)
    p2_Pload(1)
    p2_Pstats(0)
    p2_Pstats(1)
    p2_Papply(0)
    p2_Papply(1)
    for p in range(NP2):
        u = 2 * p
        if p > 0:
            p2_DG(u - 2)
            p2_Lpair(u - 2)
        if u + 2 < NSUP2:
            p2_Pload(u + 2)
            p2_Pload(u + 3)
            p2_Pstats(u + 2)
            p2_Pstats(u + 3)
        p2_FG(u)
        if u + 2 < NSUP2:
            p2_Papply(u + 2)
            p2_Papply(u + 3)
    p2_DG(NSUP2 - 2)
    p2_Lpair(NSUP2 - 2)

    kb.barrier()

    kb.resolve()
    blk = es.enter_context(nc.Block())
    with es:
        for name in ENGS:
            getattr(blk, name)(lambda eng, name=name: kb.replay(name, eng))
    return nc


def _fm_vec(v, groups):
    return np.ascontiguousarray(np.asarray(v, np.float32).reshape(groups, 128).T)


def stage_inputs(cfg, inputs):
    B, T, C, DFF, NC, TLOC = (cfg["B"], cfg["T"], cfg["C"], cfg["DFF"],
                              cfg["NC"], cfg["TLOC"])
    x = np.asarray(inputs["L0_attn_x"], np.float32)
    xt = np.transpose(x, (0, 2, 1))
    xpad = np.zeros((B, C, T + 2), np.float32)
    xpad[:, :, 1:T + 1] = xt
    shared = {
        "wkT": np.ascontiguousarray(np.asarray(inputs["L0_attn_Wk"], np.float32).T),
        "wvT": np.ascontiguousarray(np.asarray(inputs["L0_attn_Wv"], np.float32).T),
        "wrT": np.ascontiguousarray(np.asarray(inputs["L0_attn_Wr"], np.float32).T),
        "woT": np.ascontiguousarray(np.asarray(inputs["L0_attn_Wo"], np.float32).T),
        "w1": np.ascontiguousarray(np.asarray(inputs["L0_ffn_W1"], np.float32)),
        "w2": np.ascontiguousarray(np.asarray(inputs["L0_ffn_W2"], np.float32)),
        "sd_fm": _fm_vec(inputs["L0_attn_spatial_decay"], C // 128),
        "sf_fm": _fm_vec(inputs["L0_attn_spatial_first"], C // 128),
        "mixk_fm": _fm_vec(inputs["L0_attn_mix_k"], C // 128),
        "mixv_fm": _fm_vec(inputs["L0_attn_mix_v"], C // 128),
        "mixr_fm": _fm_vec(inputs["L0_attn_mix_r"], C // 128),
        "b1_fm": _fm_vec(inputs["L0_ffn_b1"], DFF // 128),
        "theta_fm": _fm_vec(inputs["L0_ffn_quantum_phases"], DFF // 128),
        "qwr_fm": _fm_vec(inputs["L0_ffn_quantum_weights_real"], DFF // 128),
        "qwi_fm": _fm_vec(inputs["L0_ffn_quantum_weights_imag"], DFF // 128),
        "b2_fm": _fm_vec(inputs["L0_ffn_b2"], C // 128),
        "sd_row": np.ascontiguousarray(
            np.asarray(inputs["L0_attn_spatial_decay"], np.float32).reshape(1, C)),
        "iota256": np.arange(cfg["TS"], dtype=np.float32).reshape(1, -1),
        "iota8": np.arange(cfg["NSUP"], dtype=np.float32).reshape(1, -1),
        "ones1": np.ones((1, 128), np.float32),
        "invC_row": np.full((1, 128), 1.0 / C, np.float32),
        "onesP": np.ones((128, 1), np.float32),
    }
    in_maps = []
    for i in range(NC):
        m = dict(shared)
        m["x_fm"] = np.ascontiguousarray(xpad[:, :, i * TLOC:(i + 1) * TLOC + 2])
        mask = np.zeros((128, 8), np.float32)
        mask[:, :i] = 1.0
        m["mask"] = mask
        in_maps.append(m)
    return in_maps


def assemble_output(cfg, results):
    outs = [np.transpose(np.asarray(r["out_fm"]), (0, 2, 1)) for r in results]
    return np.ascontiguousarray(np.concatenate(outs, axis=1))


def kernel(**inputs):
    x = np.asarray(inputs["L0_attn_x"])
    B, T, C = x.shape
    DFF = np.asarray(inputs["L0_ffn_W1"]).shape[1]
    cfg = make_cfg(B, T, C, DFF)
    nc_prog = build_program(cfg)
    in_maps = stage_inputs(cfg, inputs)
    res = run_bass_kernel_spmd(nc_prog, in_maps, core_ids=list(range(cfg["NC"])))
    return assemble_output(cfg, res.results)


# version 24 (final)
# speedup vs baseline: 1.0239x; 1.0047x over previous
import numpy as np
from contextlib import ExitStack

import concourse.bass as bass
import concourse.mybir as mybir
from concourse.bass_utils import run_bass_kernel_spmd

F32 = mybir.dt.float32
BF16 = mybir.dt.bfloat16
AF = mybir.ActivationFunctionType
ALU = mybir.AluOpType

ENGS = ("tensor", "vector", "scalar", "sync", "gpsimd")


def make_cfg(B, T, C, DFF, ncores=8, TS=256):
    TLOC = T // ncores
    cfg = dict(B=B, T=T, C=C, DFF=DFF, NC=ncores, TS=TS, TLOC=TLOC,
               NSUP=TLOC // TS, OG=C // 128, CG=C // 128, FG=DFF // 128,
               DG=C // 128, WSC=1.0 / T)
    assert TLOC % TS == 0 and C % 128 == 0 and DFF % 128 == 0
    return cfg


class _Node:
    def __init__(self, eng, fn, reads, writes, is_dma=False, slot=None, tag=""):
        self.eng = eng
        self.fn = fn
        self.reads = reads
        self.writes = writes
        self.deps = []
        self.marked = False
        self.semname = None
        self.semval = None
        self.clock = None
        self.waits = ()
        self.is_dma = is_dma
        self.slot = slot
        self.tag = tag
        self.new_group = False
        self.manual_sem = None
        self.issue_wait = None
        self.barrier_targets = None


class _Slot:
    def __init__(self, name):
        self.name = name
        self.count = 0
        self.open_group = []
        self.last_group_end = None


class KB:
    def __init__(self, nc, es):
        self.nc = nc
        self.nodes = []
        self.last_w = {}
        self.readers = {}
        self.sems = {}
        self.es = es
        for e in ENGS:
            self.sems["e_" + e] = es.enter_context(nc.semaphore("se_" + e))
        self.slots = {}

    def slot(self, name):
        if name not in self.slots:
            self.slots[name] = _Slot(name)
            self.sems["s_" + name] = self.es.enter_context(
                self.nc.semaphore("ss_" + name))
        return self.slots[name]

    def _dep(self, node):
        deps = set()
        for k in node.reads:
            w = self.last_w.get(k)
            if w is not None:
                deps.add(w)
        for k in node.writes:
            w = self.last_w.get(k)
            if w is not None:
                deps.add(w)
            for r in self.readers.get(k, ()):
                deps.add(r)
        node.deps = [d for d in deps if d is not node]
        for k in node.reads:
            self.readers.setdefault(k, []).append(node)
        for k in node.writes:
            self.last_w[k] = node
            self.readers[k] = []

    def op(self, eng, fn, reads=(), writes=(), tag=""):
        n = _Node(eng, fn, tuple(reads), tuple(writes), tag=tag)
        self._dep(n)
        self.nodes.append(n)
        return n

    def dma(self, eng, slot_name, fn, reads=(), writes=(), new_group=False, tag=""):
        sl = self.slot(slot_name)
        n = _Node(eng, fn, tuple(reads), tuple(writes), is_dma=True, slot=sl, tag=tag)
        n.new_group = new_group
        self._dep(n)
        self.nodes.append(n)
        return n

    def manual(self, eng, fn, reads=(), writes=(), semname=None, inc=1, tag=""):
        n = _Node(eng, fn, tuple(reads), tuple(writes), tag=tag)
        n.manual_sem = (semname, inc)
        self._dep(n)
        self.nodes.append(n)
        return n

    def add_sem(self, name):
        if name not in self.sems:
            self.sems[name] = self.es.enter_context(self.nc.semaphore("sx_" + name))
        return self.sems[name]

    def barrier(self):
        n = _Node("barrier", None, (), (), tag="barrier")
        self.nodes.append(n)
        self.last_w = {k: v for k, v in self.last_w.items() if k.startswith("D:")}
        self.readers = {k: v for k, v in self.readers.items() if k.startswith("D:")}
        return n

    def resolve(self):
        def implicit(d, n):
            return (d.eng == "tensor" and n.eng == "tensor" and not d.is_dma
                    and d.manual_sem is None)

        last_per_eng = {}
        for n in self.nodes:
            if n.eng == "barrier":
                for e, ln in last_per_eng.items():
                    if not ln.is_dma and ln.manual_sem is None:
                        ln.marked = True
                continue
            for d in n.deps:
                if not d.is_dma and d.manual_sem is None and not implicit(d, n):
                    d.marked = True
            last_per_eng[n.eng] = n
        for ln in last_per_eng.values():
            if not ln.is_dma and ln.manual_sem is None:
                ln.marked = True

        nincs = {e: 0 for e in ENGS}
        eclock = {e: {} for e in ENGS}

        def merge(dst, src):
            for k, v in src.items():
                if dst.get(k, -1) < v:
                    dst[k] = v

        def close_slot(sl):
            if sl.open_group:
                v = 16 * sl.count
                for g in sl.open_group:
                    g.semval = v
                    g.clock = dict(g.clock or {})
                    g.clock["s_" + sl.name] = v
                sl.last_group_end = sl.open_group[-1]
                sl.open_group = []

        for n in self.nodes:
            if n.eng == "barrier":
                for sl in self.slots.values():
                    close_slot(sl)
                mc = {}
                for e in ENGS:
                    merge(mc, eclock[e])
                    if nincs[e] > 0:
                        mc["e_" + e] = max(mc.get("e_" + e, 0), nincs[e])
                for name, sl in self.slots.items():
                    if sl.count > 0:
                        mc["s_" + name] = max(mc.get("s_" + name, 0), 16 * sl.count)
                n.barrier_targets = dict(mc)
                for e in ENGS:
                    eclock[e] = dict(mc)
                continue
            e = n.eng
            if n.is_dma:
                sl = n.slot
                n.semname = "s_" + sl.name
                if n.new_group:
                    close_slot(sl)
                    if sl.count > 0:
                        n.issue_wait = ("s_" + sl.name, 16 * sl.count,
                                        sl.last_group_end)
            elif n.manual_sem is not None:
                n.semname, n.semval = n.manual_sem
            elif n.marked:
                nincs[e] += 1
                n.semname = "e_" + e
                n.semval = nincs[e]
            waits = []
            clk = eclock[e]
            for d in n.deps:
                if implicit(d, n) and not d.marked:
                    continue
                if d.is_dma and d.semval is None:
                    close_slot(d.slot)
                if implicit(d, n):
                    continue
                assert d.semname is not None and d.semval is not None, (
                    f"unsynced dep: {d.tag} <- {n.tag}")
                if clk.get(d.semname, 0) >= d.semval:
                    continue
                waits.append((d.semname, d.semval))
                merge(clk, d.clock or {})
                clk[d.semname] = max(clk.get(d.semname, 0), d.semval)
            if n.issue_wait is not None:
                sname, v, ref = n.issue_wait
                if clk.get(sname, 0) < v:
                    waits.append((sname, v))
                    if ref is not None:
                        merge(clk, ref.clock or {})
                    clk[sname] = max(clk.get(sname, 0), v)
            n.waits = tuple(waits)
            n.clock = dict(clk)
            if n.is_dma:
                sl = n.slot
                sl.count += 1
                sl.open_group.append(n)
            elif n.semname is not None and n.semval is not None:
                n.clock[n.semname] = n.semval
        for sl in self.slots.values():
            close_slot(sl)

    def replay(self, engine_name, eng):
        for n in self.nodes:
            if n.eng == "barrier":
                for sname, v in sorted(n.barrier_targets.items()):
                    eng.wait_ge(self.sems[sname], v)
                continue
            if n.eng != engine_name:
                continue
            for sname, v in n.waits:
                eng.wait_ge(self.sems[sname], v)
            instr = n.fn(eng)
            if instr is None:
                continue
            if n.is_dma:
                instr.then_inc(self.sems[n.semname], 16)
            elif n.manual_sem is not None:
                instr.then_inc(self.sems[n.semname], n.manual_sem[1])
            elif n.marked:
                instr.then_inc(self.sems[n.semname], 1)


def build_program(cfg):
    B, T, C, DFF = cfg["B"], cfg["T"], cfg["C"], cfg["DFF"]
    TS, TLOC, NSUP = cfg["TS"], cfg["TLOC"], cfg["NSUP"]
    OG, CG, FG, DG = cfg["OG"], cfg["CG"], cfg["FG"], cfg["DG"]
    NC = cfg["NC"]
    TSH = TS + 2
    NSUP2 = B * NSUP
    GRP2 = 8
    NGRP2 = FG // GRP2
    invC = 1.0 / C
    eps = 1e-5

    nc = bass.Bass("TRN2", target_bir_lowering=False, debug=False)

    d_x = nc.dram_tensor("x_fm", [B, C, TLOC + 2], F32, kind="ExternalInput").ap()
    d_wk = nc.dram_tensor("wkT", [C, C], F32, kind="ExternalInput").ap()
    d_wv = nc.dram_tensor("wvT", [C, C], F32, kind="ExternalInput").ap()
    d_wr = nc.dram_tensor("wrT", [C, C], F32, kind="ExternalInput").ap()
    d_wo = nc.dram_tensor("woT", [C, C], F32, kind="ExternalInput").ap()
    d_w1 = nc.dram_tensor("w1", [C, DFF], F32, kind="ExternalInput").ap()
    d_w2 = nc.dram_tensor("w2", [DFF, C], F32, kind="ExternalInput").ap()
    d_sdf = nc.dram_tensor("sd_fm", [128, OG], F32, kind="ExternalInput").ap()
    d_sff = nc.dram_tensor("sf_fm", [128, OG], F32, kind="ExternalInput").ap()
    d_mk = nc.dram_tensor("mixk_fm", [128, OG], F32, kind="ExternalInput").ap()
    d_mv = nc.dram_tensor("mixv_fm", [128, OG], F32, kind="ExternalInput").ap()
    d_mr = nc.dram_tensor("mixr_fm", [128, OG], F32, kind="ExternalInput").ap()
    d_b1 = nc.dram_tensor("b1_fm", [128, FG], F32, kind="ExternalInput").ap()
    d_th = nc.dram_tensor("theta_fm", [128, FG], F32, kind="ExternalInput").ap()
    d_qr = nc.dram_tensor("qwr_fm", [128, FG], F32, kind="ExternalInput").ap()
    d_qi = nc.dram_tensor("qwi_fm", [128, FG], F32, kind="ExternalInput").ap()
    d_b2 = nc.dram_tensor("b2_fm", [128, OG], F32, kind="ExternalInput").ap()
    d_sdr = nc.dram_tensor("sd_row", [1, C], F32, kind="ExternalInput").ap()
    d_i256 = nc.dram_tensor("iota256", [1, TS], F32, kind="ExternalInput").ap()
    d_i8 = nc.dram_tensor("iota8", [1, NSUP], F32, kind="ExternalInput").ap()
    d_ones1 = nc.dram_tensor("ones1", [1, 128], F32, kind="ExternalInput").ap()
    d_invC = nc.dram_tensor("invC_row", [1, 128], F32, kind="ExternalInput").ap()
    d_onesP = nc.dram_tensor("onesP", [128, 1], F32, kind="ExternalInput").ap()
    d_mask = nc.dram_tensor("mask", [128, 8], F32, kind="ExternalInput").ap()
    d_out = nc.dram_tensor("out_fm", [B, C, TLOC], F32, kind="ExternalOutput").ap()
    d_num = nc.dram_tensor("sp_num", [NSUP2, 128, OG * TS], F32)
    d_den = nc.dram_tensor("sp_den", [NSUP2, 128, OG * TS], F32)
    d_nh = nc.dram_tensor("sp_nh", [NSUP2, 128, OG * TSH], BF16)
    d_nl = nc.dram_tensor("sp_nl", [NSUP2, 128, OG * TSH], BF16)
    d_xrh = nc.dram_tensor("sp_xrh", [NSUP2, 128, 4 * TS], BF16)
    d_xrl = nc.dram_tensor("sp_xrl", [NSUP2, 128, 4 * TS], BF16)
    d_xp = nc.dram_tensor("sp_xp", [NSUP2, 128, OG * TS], F32)
    d_tot = nc.dram_tensor("sp_tot", [128, 2 * B * OG], F32)
    d_gat = nc.dram_tensor("sp_gat", [NC * 128, 2 * B * OG], F32)
    d_w1s = nc.dram_tensor("sp_w1s", [DFF // 128, 128, 2 * C], BF16)
    d_w2s = nc.dram_tensor("sp_w2s", [DFF // 128, 128, 2 * C], BF16)

    es = ExitStack()
    CONST_F32 = 2944
    PHF = 23100
    PHB = 53600
    arena = es.enter_context(nc.sbuf_tensor("arena", [128, CONST_F32 + PHF], F32))
    arenab = es.enter_context(nc.sbuf_tensor("arenab", [128, PHB], BF16))
    _off = [0]
    _offb = [0]

    def alloc(nelem):
        o = _off[0]
        _off[0] += nelem
        assert _off[0] <= CONST_F32 + PHF, f"arena overflow {_off[0]}"
        return arena[:, o:o + nelem]

    def allocb(nelem):
        o = _offb[0]
        _offb[0] += nelem
        assert _offb[0] <= PHB, f"arenab overflow {_offb[0]}"
        return arenab[:, o:o + nelem]

    v_sd = alloc(OG); v_sf = alloc(OG)
    v_mk = alloc(OG); v_mv = alloc(OG); v_mr = alloc(OG)
    v_omk = alloc(OG); v_omv = alloc(OG); v_omr = alloc(OG)
    v_eu = alloc(OG); v_ewcol = alloc(OG)
    v_ew2 = alloc(OG)
    v_ew2r = alloc(4 * OG)
    v_mask = alloc(8)
    v_b1 = alloc(FG); v_th = alloc(FG); v_qr = alloc(FG); v_qi = alloc(FG)
    v_sc = alloc(FG); v_cth = alloc(FG); v_aqh = alloc(FG); v_aql = alloc(FG)
    v_E = alloc(FG); v_F = alloc(FG)
    v_b2 = alloc(OG)
    v_tmp32 = alloc(FG)
    v_A = alloc(4 * OG)
    v_onesP = alloc(1)
    v_eps = alloc(1)
    r_sd = alloc(C)[0:1, :]
    r_i256 = alloc(TS)[0:1, :]
    r_i8 = alloc(NSUP)[0:1, :]
    r_ones1 = alloc(128)[0:1, :]
    r_invC = alloc(128)[0:1, :]
    r_rows = alloc(TSH * 3)
    assert _off[0] <= CONST_F32, f"const region overflow: {_off[0]}"
    _off[0] = CONST_F32

    def phase_reset():
        _off[0] = CONST_F32
        _offb[0] = 0

    kb = KB(nc, es)
    kb.add_sem("cc")
    pb = [es.enter_context(nc.psum_tensor(f"pb{i}", [128, 512], F32)) for i in range(8)]

    TAG = [0]

    def t(s):
        TAG[0] += 1
        return f"{s}#{TAG[0]}"

    def ts_op(out, in0, s1, s2, op0, op1=None, reads=(), writes=(), tag=""):
        def f(e, out=out, in0=in0, s1=s1, s2=s2, op0=op0, op1=op1):
            if op1 is None:
                return e.tensor_scalar(out=out, in0=in0, scalar1=s1, scalar2=None,
                                       op0=op0)
            return e.tensor_scalar(out=out, in0=in0, scalar1=s1, scalar2=s2,
                                   op0=op0, op1=op1)
        return kb.op("vector", f, reads, writes, tag=t(tag or "ts"))

    def tt_op(out, in0, in1, op, reads=(), writes=(), tag=""):
        def f(e, out=out, in0=in0, in1=in1, op=op):
            return e.tensor_tensor(out=out, in0=in0, in1=in1, op=op)
        return kb.op("vector", f, reads, writes, tag=t(tag or "tt"))

    def stt_op(out, in0, sc, in1, op0, op1, reads=(), writes=(), tag=""):
        def f(e, out=out, in0=in0, sc=sc, in1=in1, op0=op0, op1=op1):
            return e.scalar_tensor_tensor(out=out, in0=in0, scalar=sc, in1=in1,
                                          op0=op0, op1=op1)
        return kb.op("vector", f, reads, writes, tag=t(tag or "stt"))

    def act_op(out, in_, func, bias=None, scale=None, reads=(), writes=(), tag=""):
        def f(e, out=out, in_=in_, func=func, bias=bias, scale=scale):
            kw = {}
            if bias is not None:
                kw["bias"] = bias
            if scale is not None:
                kw["scale"] = scale
            return e.activation(out=out, in_=in_, func=func, **kw)
        return kb.op("scalar", f, reads, writes, tag=t(tag or "act"))

    def copy_op(out, in_, reads=(), writes=(), tag=""):
        def f(e, out=out, in_=in_):
            return e.tensor_copy(out=out, in_=in_)
        return kb.op("vector", f, reads, writes, tag=t(tag or "cp"))

    def memset_op(ap, val, writes=(), tag=""):
        def f(e, ap=ap, val=val):
            return e.memset(ap, val)
        return kb.op("vector", f, (), writes, tag=t(tag or "ms"))

    def mm(out, lhsT, rhs, start, stop, reads=(), writes=(), tag=""):
        def f(e, out=out, lhsT=lhsT, rhs=rhs, start=start, stop=stop):
            return e.matmul(out, lhsT, rhs, start=start, stop=stop)
        return kb.op("tensor", f, reads, writes, tag=t(tag or "mm"))

    def mm3(ps, lhs_h, lhs_l, rhs_h, rhs_l, nk, reads, writes, tag=""):
        passes = ((lhs_h, rhs_h), (lhs_l, rhs_h), (lhs_h, rhs_l))
        for pi, (lf, rf) in enumerate(passes):
            for i in range(nk):
                mm(ps, lf(i), rf(i), pi == 0 and i == 0,
                   pi == 2 and i == nk - 1, reads=reads, writes=writes, tag=tag)

    def scan_op(out, data0, data1, initial, reads=(), writes=(), tag=""):
        def f(e, out=out, d0=data0, d1=data1, init=initial):
            return e.tensor_tensor_scan(out=out, data0=d0, data1=d1,
                                        initial=init, op0=ALU.mult, op1=ALU.add)
        return kb.op("vector", f, reads, writes, tag=t(tag or "scan"))

    def dma(eng, slot, out, in_, reads=(), writes=(), new_group=False, slow=False,
            tag=""):
        def f(e, out=out, in_=in_, slow=slow):
            if slow:
                return e.dma_start(out=out, in_=in_, allow_slow_non_contiguous=True)
            return e.dma_start(out=out, in_=in_)
        return kb.dma(eng, slot, f, reads, writes, new_group=new_group,
                      tag=t(tag or ("dma_" + slot)))

    def split(hi, lo, src, skey, hkey, lkey, tag=""):
        act_op(hi, src, AF.Copy, reads=(skey,), writes=(hkey,), tag=tag + "hi")
        tt_op(lo, src, hi, ALU.subtract, reads=(skey, hkey), writes=(lkey,),
              tag=tag + "lo")

    for ap_, d_, key in ((v_sd, d_sdf, "c:sd"), (v_sf, d_sff, "c:sf"),
                         (v_mk, d_mk, "c:mk"), (v_mv, d_mv, "c:mv"),
                         (v_mr, d_mr, "c:mr"), (v_b1, d_b1, "c:b1"),
                         (v_th, d_th, "c:th"), (v_qr, d_qr, "c:qr"),
                         (v_qi, d_qi, "c:qi"), (v_b2, d_b2, "c:b2"),
                         (r_sd, d_sdr, "c:sdrow"), (r_i256, d_i256, "c:i256"),
                         (r_i8, d_i8, "c:i8"), (r_ones1, d_ones1, "c:ones1"),
                         (r_invC, d_invC, "c:invC"), (v_onesP, d_onesP, "c:onesP"),
                         (v_mask, d_mask, "c:mask")):
        dma("sync", "cin", ap_, d_, writes=(key,), tag="cld")

    memset_op(v_eps, eps, writes=("c:eps",), tag="epsfill")
    for v_m, v_om, km, ko in ((v_mk, v_omk, "c:mk", "c:omk"),
                              (v_mv, v_omv, "c:mv", "c:omv"),
                              (v_mr, v_omr, "c:mr", "c:omr")):
        ts_op(v_om, v_m, -1.0, 1.0, ALU.mult, ALU.add, reads=(km,), writes=(ko,),
              tag="om")
    act_op(v_eu, v_sf, AF.Exp, scale=cfg["WSC"], reads=("c:sf",), writes=("c:eu",),
           tag="eu")
    act_op(v_ewcol, v_sd, AF.Exp, scale=cfg["WSC"], reads=("c:sd",), writes=("c:ew",),
           tag="ew")
    act_op(v_ew2, v_sd, AF.Exp, scale=float(TLOC) * cfg["WSC"], reads=("c:sd",),
           writes=("c:ew2",), tag="ew2")
    ts_op(v_ew2, v_ew2, -1.0, None, ALU.add, reads=("c:ew2",), writes=("c:ew2",),
          tag="ew2m1")
    for r in range(4):
        copy_op(v_ew2r[:, r * OG:(r + 1) * OG], v_ew2, reads=("c:ew2",),
                writes=("c:ew2r",), tag="ew2rep")
    PI = float(np.float32(np.pi))
    TWO_PI = 2.0 * np.pi
    c1 = 6.28125
    c2 = float(np.float32(TWO_PI - c1))
    c3 = float(np.float32(TWO_PI - c1 - float(np.float32(TWO_PI - c1))))
    for _ in range(7):
        ts_op(v_tmp32, v_th, PI, None, ALU.is_gt, reads=("c:th",), writes=("c:g",),
              tag="rrgt")
        for cc_ in (c1, c2, c3):
            stt_op(v_th, v_tmp32, -cc_, v_th, ALU.mult, ALU.add,
                   reads=("c:g", "c:th"), writes=("c:th",), tag="rrsub")
        ts_op(v_tmp32, v_th, -PI, None, ALU.is_lt, reads=("c:th",), writes=("c:g",),
              tag="rrlt")
        for cc_ in (c1, c2, c3):
            stt_op(v_th, v_tmp32, cc_, v_th, ALU.mult, ALU.add,
                   reads=("c:g", "c:th"), writes=("c:th",), tag="rradd")
    act_op(v_sc, v_th, AF.Sin, reads=("c:th",), writes=("c:sc",), tag="sinth")
    act_op(v_cth, v_th, AF.Sin, scale=0.5, reads=("c:th",), writes=("c:cth",),
           tag="sinth2")
    tt_op(v_cth, v_cth, v_cth, ALU.mult, reads=("c:cth",), writes=("c:cth",), tag="csq")
    ts_op(v_cth, v_cth, -2.0, 1.0, ALU.mult, ALU.add, reads=("c:cth",),
          writes=("c:cth",), tag="costh")
    P1H = float(np.float32(0.1))
    P1L = float(0.1 - P1H)
    tt_op(v_aqh, v_cth, v_qr, ALU.mult, reads=("c:cth", "c:qr"), writes=("c:aqh",),
          tag="qp1")
    tt_op(v_aql, v_sc, v_qi, ALU.mult, reads=("c:sc", "c:qi"), writes=("c:aql",),
          tag="qp2")
    tt_op(v_aqh, v_aqh, v_aql, ALU.add, reads=("c:aqh", "c:aql"), writes=("c:aqh",),
          tag="qp3")
    tt_op(v_tmp32, v_aqh, v_qi, ALU.mult, reads=("c:aqh", "c:qi"), writes=("c:t3",),
          tag="qp4")
    tt_op(v_aqh, v_tmp32, v_sc, ALU.mult, reads=("c:t3", "c:sc"), writes=("c:e1",),
          tag="e1")
    ts_op(v_E, v_aqh, P1H, None, ALU.mult, reads=("c:e1",), writes=("c:E",),
          tag="Eh")
    stt_op(v_E, v_aqh, P1L, v_E, ALU.mult, ALU.add, reads=("c:e1", "c:E"),
           writes=("c:E",), tag="El")
    tt_op(v_aqh, v_tmp32, v_cth, ALU.mult, reads=("c:t3", "c:cth"), writes=("c:e1",),
          tag="f1")
    ts_op(v_F, v_aqh, P1H, None, ALU.mult, reads=("c:e1",), writes=("c:F",),
          tag="Fh")
    stt_op(v_F, v_aqh, P1L, v_F, ALU.mult, ALU.add, reads=("c:e1", "c:F"),
           writes=("c:F",), tag="Fl")

    def ln_stats(xv, width, sqv, sq_key, x_key, tagp, packed=False, packed4=False,
                 sq_vec=False):
        if sq_vec:
            tt_op(sqv[:, 0:OG * width], xv[:, 0:OG * width], xv[:, 0:OG * width],
                  ALU.mult, reads=(x_key,), writes=(sq_key,), tag=tagp + "sq")
        else:
            act_op(sqv[:, 0:OG * width], xv[:, 0:OG * width], AF.Square,
                   reads=(x_key,), writes=(sq_key,), tag=tagp + "sq")
        if packed4:
            assert width <= 256
            rsum = pb[4][0:1, 0:width]
            rsq = pb[5][0:1, 0:width]
            ksum, ksq = "pb4", "pb5"
            mrep = pb[4][:, 256:256 + width]
            rrep = pb[5][:, 256:256 + width]
        elif packed:
            assert width <= 256
            rsum = pb[6][0:1, 0:width]
            rsq = pb[7][0:1, 0:width]
            ksum, ksq = "pb6", "pb7"
            mrep = pb[6][:, 256:256 + width]
            rrep = pb[7][:, 256:256 + width]
        else:
            rsum = pb[4][0:1, 0:width]
            rsq = pb[5][0:1, 0:width]
            ksum, ksq = "pb4", "pb5"
            mrep = pb[6][:, 0:width]
            rrep = pb[7][:, 0:width]
        for cg in range(CG):
            mm(rsum, v_onesP, xv[:, cg * width:(cg + 1) * width],
               cg == 0, cg == CG - 1, reads=("c:onesP", x_key), writes=(ksum,),
               tag=tagp + "summ")
        for cg in range(CG):
            mm(rsq, v_onesP, sqv[:, cg * width:(cg + 1) * width],
               cg == 0, cg == CG - 1, reads=("c:onesP", sq_key), writes=(ksq,),
               tag=tagp + "sqmm")
        m_row = r_rows[0:1, 0:width]
        v_row = r_rows[0:1, TSH:TSH + width]
        s_row = r_rows[0:1, 2 * TSH:2 * TSH + width]
        ts_op(m_row, rsum, invC, None, ALU.mult, reads=(ksum,), writes=("rows_m",),
              tag=tagp + "m")
        tt_op(v_row, m_row, m_row, ALU.mult, reads=("rows_m",), writes=("rows_v",),
              tag=tagp + "msq")
        stt_op(v_row, rsq, invC, v_row, ALU.mult, ALU.subtract,
               reads=(ksq, "rows_v"), writes=("rows_v",), tag=tagp + "var")
        act_op(v_row, v_row, AF.Ln, bias=v_eps[0:1, 0:1], reads=("rows_v", "c:eps"),
               writes=("rows_v",), tag=tagp + "lnv")
        act_op(s_row, v_row, AF.Exp, scale=-0.5, reads=("rows_v",), writes=("rows_s",),
               tag=tagp + "rstd")
        mm(mrep, r_ones1, m_row, True, True,
           reads=("c:ones1", "rows_m"), writes=(ksum if (packed or packed4) else "pb6",),
           tag=tagp + "mrep")
        mm(rrep, r_ones1, s_row, True, True,
           reads=("c:ones1", "rows_s"), writes=(ksq if (packed or packed4) else "pb7",),
           tag=tagp + "rrep")
        return mrep, rrep

    def weight_split(d_src, hi, lo, hkey, lkey, scr_list, scr_keys, slot, tag):
        nch = CG // 2
        for i in range(nch):
            scr = scr_list[i % len(scr_list)][:, 0:2 * C]
            skey = scr_keys[i % len(scr_keys)]
            dma("sync", slot, scr.rearrange("p (cg o) -> p cg o", cg=2),
                d_src[i * 256:(i + 1) * 256, :].rearrange("(cg p) o -> p cg o", p=128),
                reads=(), writes=(skey,), new_group=True, tag=tag + "ld")
            split(hi[:, i * 2 * C:(i + 1) * 2 * C], lo[:, i * 2 * C:(i + 1) * 2 * C],
                  scr, skey, hkey, lkey, tag=tag + "sp")

    phase_reset()
    a_x = [alloc(OG * TSH) for _ in range(2)]
    a_sq = alloc(OG * TSH)
    a_n = [alloc(OG * TSH) for _ in range(2)]
    a_d = alloc(4 * TS)
    a_mscr = alloc(4 * TS)
    a_ek = alloc(OG * TS)
    a_ekv = alloc(OG * TS)
    a_asc = alloc(OG * (TS + 1))
    a_bsc = alloc(OG * (TS + 1))
    a_ewrep = alloc(OG * TS)
    a_tots = alloc(4 * OG)
    a_gath = alloc(2 * B * NC * OG)
    a_Atmp = alloc(4 * OG)
    a_wkh = allocb(OG * C); a_wkl = allocb(OG * C)
    a_wvh = allocb(OG * C); a_wvl = allocb(OG * C)
    a_xkp = [[allocb(4 * TS), allocb(4 * TS)] for _ in range(2)]
    a_xvp = [[allocb(4 * TS), allocb(4 * TS)] for _ in range(2)]
    a_xrp = [[allocb(4 * TS), allocb(4 * TS)] for _ in range(2)]
    a_np = [[allocb(OG * TSH), allocb(OG * TSH)] for _ in range(2)]

    weight_split(d_wk, a_wkh, a_wkl, "wkh", "wkl", [a_x[0], a_x[1]], ["x0", "x1"],
                 "wsp", "wk")
    weight_split(d_wv, a_wvh, a_wvl, "wvh", "wvl", [a_x[0], a_x[1]], ["x0", "x1"],
                 "wsp", "wv")

    memset_op(a_ewrep, 1.0, writes=("ewrep",), tag="ewrep0")
    for og in range(OG):
        ts_op(a_ewrep[:, og * TS:(og + 1) * TS], a_ewrep[:, og * TS:(og + 1) * TS],
              v_ewcol[:, og:og + 1], None, ALU.mult,
              reads=("ewrep", "c:ew"), writes=("ewrep",), tag="ewrep")

    def p1a_A(u):
        b, s = divmod(u, NSUP)
        par = u % 2
        xv, nv = a_x[par], a_n[par]
        xkey, nkey = f"x{par}", f"n{par}"
        dma("sync", f"xin{par}", xv.rearrange("p (g w) -> p g w", g=OG),
            d_x[b, :, s * TS:s * TS + TSH].rearrange("(g p) w -> p g w", p=128),
            writes=(xkey,), new_group=True, tag="xload")
        mrep, rrep = ln_stats(xv, TSH, a_sq, "sq", xkey, "l1")
        nv3 = nv.rearrange("p (g w) -> p g w", g=OG)
        tt_op(nv3, xv.rearrange("p (g w) -> p g w", g=OG),
              mrep.unsqueeze(1).broadcast_to([128, OG, TSH]), ALU.subtract,
              reads=(xkey, "pb6"), writes=(nkey,), tag="ncent")
        tt_op(nv3, nv3, rrep.unsqueeze(1).broadcast_to([128, OG, TSH]), ALU.mult,
              reads=(nkey, "pb7"), writes=(nkey,), tag="nmul")
        split(a_np[par][0], a_np[par][1], nv, nkey, f"nh{par}", f"nl{par}", tag="nsp")
        for cg in range(4):
            off = cg * TSH
            sh = nv[:, off:off + TS] if cg < 2 else nv[:, off + 2:off + TS + 2]
            mid = nv[:, off + 1:off + TS + 1]
            dsl = a_d[:, cg * TS:(cg + 1) * TS]
            tt_op(dsl, sh, mid, ALU.subtract, reads=(nkey,), writes=("d",),
                  tag="dshift")
        for om_, pr, kk in ((v_omk, a_xkp[par], f"xk{par}"),
                            (v_omv, a_xvp[par], f"xv{par}"),
                            (v_omr, a_xrp[par], f"xr{par}")):
            ko = {id(v_omk): "c:omk", id(v_omv): "c:omv", id(v_omr): "c:omr"}[id(om_)]
            for cg in range(4):
                off = cg * TSH
                mid = nv[:, off + 1:off + TS + 1]
                stt_op(a_mscr[:, cg * TS:(cg + 1) * TS],
                       a_d[:, cg * TS:(cg + 1) * TS], om_[:, cg:cg + 1], mid,
                       ALU.mult, ALU.add, reads=("d", nkey, ko), writes=("mscr",),
                       tag="mix")
            split(pr[0], pr[1], a_mscr, "mscr", kk + "h", kk + "l", tag="xsp")

    def p1a_B(u):
        b, s = divmod(u, NSUP)
        par = u % 2
        nkey = f"n{par}"

        def rhs_pair(pref, half):
            def f(cg, pref=pref, half=half):
                if cg < 4:
                    src = (a_xkp if pref == "k" else a_xvp)[par][half]
                    return src[:, cg * TS:(cg + 1) * TS]
                nph = a_np[par][half]
                return nph[:, cg * TSH + 1:cg * TSH + 1 + TS]
            return f

        rk_h = (f"xk{par}h", f"nh{par}")
        rk_l = (f"xk{par}l", f"nl{par}")
        rv_h = (f"xv{par}h", f"nh{par}")
        rv_l = (f"xv{par}l", f"nl{par}")
        for og in range(OG):
            bk = (2 * og) % 6
            bkv = (2 * og + 1) % 6
            psk = pb[bk][:, 0:TS]
            psv = pb[bkv][:, 0:TS]
            mm3(psk,
                lambda cg, og=og: a_wkh[:, cg * C + og * 128:cg * C + (og + 1) * 128],
                lambda cg, og=og: a_wkl[:, cg * C + og * 128:cg * C + (og + 1) * 128],
                rhs_pair("k", 0), rhs_pair("k", 1), CG,
                reads=("wkh", "wkl") + rk_h + rk_l, writes=(f"pb{bk}",), tag="kmm")
            mm3(psv,
                lambda cg, og=og: a_wvh[:, cg * C + og * 128:cg * C + (og + 1) * 128],
                lambda cg, og=og: a_wvl[:, cg * C + og * 128:cg * C + (og + 1) * 128],
                rhs_pair("v", 0), rhs_pair("v", 1), CG,
                reads=("wvh", "wvl") + rv_h + rv_l, writes=(f"pb{bkv}",), tag="vmm")
            eksl = a_ek[:, og * TS:(og + 1) * TS]
            ekvsl = a_ekv[:, og * TS:(og + 1) * TS]
            kek, kekv = f"ek{og}", f"ekv{og}"
            act_op(eksl, psk, AF.Exp, reads=(f"pb{bk}",), writes=(kek,), tag="ek")
            tt_op(ekvsl, eksl, psv, ALU.mult, reads=(kek, f"pb{bkv}"),
                  writes=(kekv,), tag="ekv")
            W1p = TS + 1
            asl = a_asc[:, og * W1p:(og + 1) * W1p]
            bsl = a_bsc[:, og * W1p:(og + 1) * W1p]
            ka, kbk = f"asc{og}", f"bsc{og}"
            if s == 0:
                memset_op(asl[:, 0:1], 0.0, writes=(ka,), tag="acar0")
                memset_op(bsl[:, 0:1], 0.0, writes=(kbk,), tag="bcar0")
            else:
                copy_op(asl[:, 0:1], asl[:, TS:TS + 1], reads=(ka,), writes=(ka,),
                        tag="acar")
                copy_op(bsl[:, 0:1], bsl[:, TS:TS + 1], reads=(kbk,), writes=(kbk,),
                        tag="bcar")
            scan_op(asl[:, 1:TS + 1], a_ewrep[:, og * TS:(og + 1) * TS], ekvsl,
                    asl[:, 0:1], reads=("ewrep", kekv, ka), writes=(ka,), tag="ascan")
            scan_op(bsl[:, 1:TS + 1], a_ewrep[:, og * TS:(og + 1) * TS], eksl,
                    bsl[:, 0:1], reads=("ewrep", kek, kbk), writes=(kbk,), tag="bscan")
            stt_op(asl[:, 0:TS], ekvsl, v_eu[:, og:og + 1],
                   asl[:, 0:TS], ALU.mult, ALU.add,
                   reads=(kekv, "c:eu", ka), writes=(ka,), tag="numloc")
            stt_op(bsl[:, 0:TS], eksl, v_eu[:, og:og + 1],
                   bsl[:, 0:TS], ALU.mult, ALU.add,
                   reads=(kek, "c:eu", kbk), writes=(kbk,), tag="denloc")
            if s == NSUP - 1:
                copy_op(a_tots[:, (2 * b) * OG + og:(2 * b) * OG + og + 1],
                        asl[:, TS:TS + 1], reads=(ka,), writes=("tots",), tag="tota")
                copy_op(a_tots[:, (2 * b + 1) * OG + og:(2 * b + 1) * OG + og + 1],
                        bsl[:, TS:TS + 1], reads=(kbk,), writes=("tots",), tag="totb")
        allk = tuple(f"asc{og}" for og in range(OG))
        allkb = tuple(f"bsc{og}" for og in range(OG))
        dma("scalar", f"spnd{par}", d_num.ap()[u].rearrange("p (og w) -> p og w", og=OG),
            a_asc.rearrange("p (og w) -> p og w", og=OG)[:, :, 0:TS], reads=allk,
            writes=("D:num",), new_group=True, tag="spnum")
        dma("scalar", f"spnd{par}", d_den.ap()[u].rearrange("p (og w) -> p og w", og=OG),
            a_bsc.rearrange("p (og w) -> p og w", og=OG)[:, :, 0:TS], reads=allkb,
            writes=("D:den",), tag="spden")
        dma("gpsimd", f"sp{par}", d_nh.ap()[u], a_np[par][0], reads=(f"nh{par}",),
            new_group=True,
            writes=("D:nh",), tag="spnh")
        dma("gpsimd", f"sp{par}", d_nl.ap()[u], a_np[par][1], reads=(f"nl{par}",),
            writes=("D:nl",), tag="spnl")
        dma("gpsimd", f"sp{par}", d_xrh.ap()[u], a_xrp[par][0], reads=(f"xr{par}h",),
            writes=("D:xrh",), tag="spxrh")
        dma("gpsimd", f"sp{par}", d_xrl.ap()[u], a_xrp[par][1], reads=(f"xr{par}l",),
            writes=("D:xrl",), tag="spxrl")

    p1a_A(0)
    p1a_A(1)
    for u in range(NSUP2):
        p1a_B(u)
        if u + 2 < NSUP2:
            p1a_A(u + 2)

    dma("gpsimd", "tot", d_tot.ap(), a_tots, reads=("tots",),
        writes=("D:tot",), new_group=True, tag="totst")

    def f_cc(e):
        return e.collective_compute(
            "AllGather", ALU.bypass, replica_groups=[list(range(NC))],
            ins=[d_tot.ap().opt()], outs=[d_gat.ap().opt()])
    kb.manual("gpsimd", f_cc, reads=("D:tot",), writes=("D:gat",), semname="cc",
              inc=1, tag="allgather")

    dma("gpsimd", "gat",
        a_gath.rearrange("p (j w) -> p j w", j=NC),
        d_gat.ap().rearrange("(j p) w -> p j w", p=128),
        reads=("D:gat",), writes=("gath",), new_group=True, slow=True, tag="gatld")

    memset_op(v_A, 0.0, writes=("A",), tag="A0")
    for j in range(NC - 1):
        ts_op(a_Atmp, v_ew2r, v_mask[:, j:j + 1], 1.0, ALU.mult, ALU.add,
              reads=("c:ew2r", "c:mask"), writes=("Atmp",), tag="dsel")
        tt_op(v_A, v_A, a_Atmp, ALU.mult, reads=("A", "Atmp"), writes=("A",),
              tag="Amul")
        ts_op(a_Atmp, a_gath[:, j * (2 * B) * OG:(j + 1) * (2 * B) * OG],
              v_mask[:, j:j + 1], None, ALU.mult, reads=("gath", "c:mask"),
              writes=("Atmp",), tag="Tmask")
        tt_op(v_A, v_A, a_Atmp, ALU.add, reads=("A", "Atmp"), writes=("A",),
              tag="Aadd")

    kb.barrier()
    phase_reset()
    b_num = [alloc(OG * TS) for _ in range(2)]
    b_den = [alloc(OG * TS) for _ in range(2)]
    b_x = [alloc(OG * TS) for _ in range(2)]
    b_sr = alloc(OG * TS)
    b_attn = alloc(OG * TS)
    b_xp = [alloc(OG * TS) for _ in range(2)]
    b_ewpb = alloc(OG * TS)
    b_ews = alloc(NSUP * OG)
    b_As = [[alloc(OG), alloc(OG)] for _ in range(2)]
    b_wrh = allocb(OG * C); b_wrl = allocb(OG * C)
    b_woh = allocb(OG * C); b_wol = allocb(OG * C)
    b_np = [[allocb(OG * TSH), allocb(OG * TSH)] for _ in range(2)]
    b_xrp = [[allocb(4 * TS), allocb(4 * TS)] for _ in range(2)]
    b_gp = [allocb(OG * TS), allocb(OG * TS)]

    weight_split(d_wr, b_wrh, b_wrl, "wrh", "wrl", [b_num[0], b_num[1]],
                 ["bnum0", "bnum1"], "wsp2", "wr")
    weight_split(d_wo, b_woh, b_wol, "woh", "wol", [b_num[0], b_num[1]],
                 ["bnum0", "bnum1"], "wsp2", "wo")

    for og in range(OG):
        ps = pb[og % 2][:, 0:TS]
        mm(ps, r_sd[0:1, og * 128:(og + 1) * 128], r_i256, True, True,
           reads=("c:sdrow", "c:i256"), writes=(f"pb{og % 2}",), tag="ewpmm")
        act_op(b_ewpb[:, og * TS:(og + 1) * TS], ps, AF.Exp, scale=cfg["WSC"],
               reads=(f"pb{og % 2}",), writes=("ewpb",), tag="ewpexp")
    for og in range(OG):
        ps = pb[og % 2][:, 0:NSUP]
        mm(ps, r_sd[0:1, og * 128:(og + 1) * 128], r_i8, True, True,
           reads=("c:sdrow", "c:i8"), writes=(f"pb{og % 2}",), tag="ewsmm")
        act_op(b_ews[:, og * NSUP:(og + 1) * NSUP], ps, AF.Exp,
               scale=float(TS) * cfg["WSC"], reads=(f"pb{og % 2}",), writes=("ews",),
               tag="ewsexp")
    ews_v = b_ews.rearrange("p (og s) -> p og s", og=OG)

    def p1b_Aload(u):
        b, s = divmod(u, NSUP)
        par = u % 2
        numv, denv = b_num[par], b_den[par]
        dma("sync", f"ld1b{par}", numv, d_num.ap()[u], reads=("D:num",),
            writes=(f"bnum{par}",), new_group=True, tag="ldnum")
        dma("sync", f"ld1b{par}", denv, d_den.ap()[u], reads=("D:den",),
            writes=(f"bden{par}",), tag="ldden")
        dma("sync", f"ld1b{par}", b_np[par][0], d_nh.ap()[u], reads=("D:nh",),
            writes=(f"bnh{par}",), tag="ldnh")
        dma("sync", f"ld1b{par}", b_np[par][1], d_nl.ap()[u], reads=("D:nl",),
            writes=(f"bnl{par}",), tag="ldnl")
        dma("sync", f"ld1b{par}", b_xrp[par][0], d_xrh.ap()[u], reads=("D:xrh",),
            writes=(f"bxrh{par}",), tag="ldxrh")
        dma("sync", f"ld1b{par}", b_xrp[par][1], d_xrl.ap()[u], reads=("D:xrl",),
            writes=(f"bxrl{par}",), tag="ldxrl")
        dma("sync", f"ld1b{par}", b_x[par].rearrange("p (g w) -> p g w", g=OG),
            d_x[b, :, s * TS + 1:s * TS + 1 + TS].rearrange("(g p) w -> p g w", p=128),
            reads=(), writes=(f"bx{par}",), tag="ldx")

    def p1b_Afix(u):
        b, s = divmod(u, NSUP)
        par = u % 2
        numv, denv = b_num[par], b_den[par]
        for kind in range(2):
            r = 2 * b + kind
            tt_op(b_As[par][kind], v_A[:, r * OG:(r + 1) * OG], ews_v[:, :, s],
                  ALU.mult, reads=("A", "ews"), writes=(f"As{par}{kind}",),
                  tag="Ascale")
        for og in range(OG):
            sl = slice(og * TS, (og + 1) * TS)
            stt_op(numv[:, sl], b_ewpb[:, sl], b_As[par][0][:, og:og + 1],
                   numv[:, sl], ALU.mult, ALU.add,
                   reads=("ewpb", f"As{par}0", f"bnum{par}"),
                   writes=(f"bnum{par}",), tag="fixn")
            stt_op(denv[:, sl], b_ewpb[:, sl], b_As[par][1][:, og:og + 1],
                   denv[:, sl], ALU.mult, ALU.add,
                   reads=("ewpb", f"As{par}1", f"bden{par}"),
                   writes=(f"bden{par}",), tag="fixd")
        act_op(denv, denv, AF.Ln, reads=(f"bden{par}",), writes=(f"bden{par}",),
               tag="lnden")
        act_op(denv, denv, AF.Exp, scale=-1.0, reads=(f"bden{par}",),
               writes=(f"bden{par}",), tag="rden")
        tt_op(numv, numv, denv, ALU.mult, reads=(f"bnum{par}", f"bden{par}"),
              writes=(f"bnum{par}",), tag="wkv")

    def p1b_B(u):
        b, s = divmod(u, NSUP)
        par = u % 2
        numv, xpv = b_num[par], b_xp[par]

        def rrhs(half):
            def f(cg, half=half):
                if cg < 4:
                    return b_xrp[par][half][:, cg * TS:(cg + 1) * TS]
                return b_np[par][half][:, cg * TSH + 1:cg * TSH + 1 + TS]
            return f

        rkeys = (f"bxrh{par}", f"bxrl{par}", f"bnh{par}", f"bnl{par}")
        for og in range(OG):
            bk = og % 4
            ps = pb[bk][:, 0:TS]
            mm3(ps,
                lambda cg, og=og: b_wrh[:, cg * C + og * 128:cg * C + (og + 1) * 128],
                lambda cg, og=og: b_wrl[:, cg * C + og * 128:cg * C + (og + 1) * 128],
                rrhs(0), rrhs(1), CG,
                reads=("wrh", "wrl") + rkeys, writes=(f"pb{bk}",), tag="rmm")
            act_op(b_sr[:, og * TS:(og + 1) * TS], ps, AF.Sigmoid,
                   reads=(f"pb{bk}",), writes=("sr",), tag="sr")
            sl = slice(og * TS, (og + 1) * TS)
            tt_op(numv[:, sl], numv[:, sl], b_sr[:, sl], ALU.mult,
                  reads=(f"bnum{par}", "sr"), writes=(f"bnum{par}",), tag="gated")
            split(b_gp[0][:, sl], b_gp[1][:, sl], numv[:, sl], f"bnum{par}",
                  "gph", "gpl", tag="gsp")
        for dg in range(DG):
            bk = 4 + dg % 2
            ps = pb[bk][:, 0:TS]
            mm3(ps,
                lambda og, dg=dg: b_woh[:, og * C + dg * 128:og * C + (dg + 1) * 128],
                lambda og, dg=dg: b_wol[:, og * C + dg * 128:og * C + (dg + 1) * 128],
                lambda og: b_gp[0][:, og * TS:(og + 1) * TS],
                lambda og: b_gp[1][:, og * TS:(og + 1) * TS], OG,
                reads=("woh", "wol", "gph", "gpl"), writes=(f"pb{bk}",), tag="womm")
            act_op(b_attn[:, dg * TS:(dg + 1) * TS], ps, AF.Copy,
                   reads=(f"pb{bk}",), writes=("attn",), tag="attnev")
        mrep, rrep = ln_stats(b_attn, TS, xpv, f"bxp{par}", "attn", "l2",
                              packed=True)
        at3 = b_attn.rearrange("p (g w) -> p g w", g=OG)
        tt_op(at3, at3, mrep.unsqueeze(1).broadcast_to([128, OG, TS]), ALU.subtract,
              reads=("attn", "pb6"), writes=("attn",), tag="pcent")
        tt_op(at3, at3, rrep.unsqueeze(1).broadcast_to([128, OG, TS]), ALU.mult,
              reads=("attn", "pb7"), writes=("attn",), tag="pmul")
        tt_op(xpv, b_attn, b_x[par], ALU.add,
              reads=("attn", f"bx{par}"), writes=(f"bxp{par}",), tag="resid")
        dma("gpsimd", f"spx{par}", d_xp.ap()[u], xpv, reads=(f"bxp{par}",),
            writes=("D:xp",), new_group=True, tag="spxp")

    p1b_Aload(0)
    p1b_Aload(1)
    p1b_Afix(0)
    for u in range(NSUP2):
        p1b_B(u)
        if u + 1 < NSUP2:
            p1b_Afix(u + 1)
        if u + 2 < NSUP2:
            p1b_Aload(u + 2)

    kb.barrier()
    phase_reset()
    c_q = alloc(GRP2 * TS)
    c_ws = alloc(2 * C)
    c_xp = [alloc(OG * TS) for _ in range(4)]
    c_n2 = alloc(OG * TS)
    c_out = alloc(OG * TS)
    c_hb = [alloc(GRP2 * TS) for _ in range(2)]
    c_sh = alloc(GRP2 * TS)
    c_rows2 = alloc(2 * TS)
    c_w1 = [allocb(2 * C) for _ in range(2)]
    c_w2s = [allocb(2 * C) for _ in range(2)]
    c_actp = [[allocb(FG * TS), allocb(FG * TS)] for _ in range(2)]
    c_n2p = [[allocb(OG * TS), allocb(OG * TS)] for _ in range(2)]

    def p2_Pload(u):
        par = u % 4
        dma("sync", f"xpin{par}", c_xp[par], d_xp.ap()[u], reads=("D:xp",),
            writes=(f"cxp{par}",), new_group=True, tag="ldxp")

    def p2_Pstats(u):
        par = u % 4
        mem = u % 2
        xpv = c_xp[par]
        xkey = f"cxp{par}"
        tt_op(c_n2, xpv, xpv, ALU.mult, reads=(xkey,), writes=("n2f",),
              tag="l3sq")
        rsum = pb[4][0:1, mem * TS:(mem + 1) * TS]
        rsq = pb[5][0:1, mem * TS:(mem + 1) * TS]
        for cg in range(CG):
            mm(rsum, v_onesP, xpv[:, cg * TS:(cg + 1) * TS], cg == 0,
               cg == CG - 1, reads=("c:onesP", xkey), writes=("pb4",),
               tag="l3summ")
        for cg in range(CG):
            mm(rsq, v_onesP, c_n2[:, cg * TS:(cg + 1) * TS], cg == 0,
               cg == CG - 1, reads=("c:onesP", "n2f"), writes=("pb5",),
               tag="l3sqmm")
        if mem == 0:
            m_row = r_rows[0:1, 0:TS]
            s_row = r_rows[0:1, 2 * TSH:2 * TSH + TS]
            mkey, skey = "rows_m", "rows_s"
        else:
            m_row = c_rows2[0:1, 0:TS]
            s_row = c_rows2[0:1, TS:2 * TS]
            mkey, skey = "r2m", "r2s"
        v_row = r_rows[0:1, TSH:TSH + TS]
        ts_op(m_row, rsum, invC, None, ALU.mult, reads=("pb4",), writes=(mkey,),
              tag="l3m")
        tt_op(v_row, m_row, m_row, ALU.mult, reads=(mkey,), writes=("rows_v",),
              tag="l3msq")
        stt_op(v_row, rsq, invC, v_row, ALU.mult, ALU.subtract,
               reads=("pb5", "rows_v"), writes=("rows_v",), tag="l3var")
        act_op(v_row, v_row, AF.Ln, bias=v_eps[0:1, 0:1],
               reads=("rows_v", "c:eps"), writes=("rows_v",), tag="l3lnv")
        act_op(s_row, v_row, AF.Exp, scale=-0.5, reads=("rows_v",),
               writes=(skey,), tag="l3rstd")
        mm(pb[6][:, mem * TS:(mem + 1) * TS], r_ones1, m_row, mem == 0,
           mem == 1, reads=("c:ones1", mkey), writes=("pb6",), tag="l3mrep")
        mm(pb[7][:, mem * TS:(mem + 1) * TS], r_ones1, s_row, mem == 0,
           mem == 1, reads=("c:ones1", skey), writes=("pb7",), tag="l3rrep")

    def p2_Papply(u):
        par = u % 4
        mem = u % 2
        xpv = c_xp[par]
        xkey = f"cxp{par}"
        mrep = pb[6][:, mem * TS:(mem + 1) * TS]
        rrep = pb[7][:, mem * TS:(mem + 1) * TS]
        n23 = c_n2.rearrange("p (g w) -> p g w", g=OG)
        tt_op(n23, xpv.rearrange("p (g w) -> p g w", g=OG),
              mrep.unsqueeze(1).broadcast_to([128, OG, TS]), ALU.subtract,
              reads=(xkey, "pb6"), writes=("n2f",), tag="n2c")
        tt_op(n23, n23, rrep.unsqueeze(1).broadcast_to([128, OG, TS]), ALU.mult,
              reads=("n2f", "pb7"), writes=("n2f",), tag="n2m")
        split(c_n2p[mem][0], c_n2p[mem][1], c_n2, "n2f", f"n2h{mem}",
              f"n2l{mem}", tag="n2sp")

    def p2_FG(u):
        for grp in range(NGRP2):
            for fgl in range(GRP2):
                fg = grp * GRP2 + fgl
                wpar = fg % 2
                wpb = c_w1[wpar]
                if u == 0:
                    dma("sync", f"w1in{wpar}",
                        c_ws[:, 0:C].rearrange("p (cg f) -> p cg f", cg=CG),
                        d_w1[:, fg * 128:(fg + 1) * 128].rearrange(
                            "(cg p) f -> p cg f", p=128),
                        reads=("ws",), writes=("ws",), new_group=True, tag="w1ldf")
                    split(wpb[:, 0:C], wpb[:, C:2 * C], c_ws[:, 0:C], "ws",
                          f"w1p{wpar}", f"w1p{wpar}", tag="w1sp")
                    dma("gpsimd", "wst", d_w1s.ap()[fg], wpb,
                        reads=(f"w1p{wpar}",), writes=("D:w1s",),
                        new_group=True, tag="w1st")
                else:
                    dma("sync", f"w1in{wpar}", wpb, d_w1s.ap()[fg],
                        reads=("D:w1s",), writes=(f"w1p{wpar}",),
                        new_group=True, tag="w1ldp")
                hbk = fg % 6
                first = True
                for mem in range(2):
                    ps = pb[hbk][:, mem * TS:(mem + 1) * TS]
                    for pi, (lo_w, lo_r) in enumerate(((0, 0), (C, 0), (0, 1))):
                        for cg in range(CG):
                            mm(ps, wpb[:, lo_w + cg * 128:lo_w + (cg + 1) * 128],
                               c_n2p[mem][lo_r][:, cg * TS:(cg + 1) * TS],
                               first, mem == 1 and pi == 2 and cg == CG - 1,
                               reads=(f"w1p{wpar}", f"n2h{mem}", f"n2l{mem}"),
                               writes=(f"pb{hbk}",), tag="w1mm")
                            first = False
                for mem in range(2):
                    act_op(c_hb[mem][:, fgl * TS:(fgl + 1) * TS],
                           pb[hbk][:, mem * TS:(mem + 1) * TS], AF.Identity,
                           bias=v_b1[:, fg:fg + 1], reads=(f"pb{hbk}", "c:b1"),
                           writes=(f"hb{mem}f{fgl}",), tag="hbev")
            Ec = v_E[:, grp * GRP2:(grp + 1) * GRP2]
            Fc = v_F[:, grp * GRP2:(grp + 1) * GRP2]
            qb = (c_q, c_out)
            qk = ("q", "cout")
            shb = (c_sh, c_ws)
            shk = ("sh", "ws")
            for mem in range(2):
                hball = tuple(f"hb{mem}f{i}" for i in range(GRP2))
                ts_op(shb[mem], c_hb[mem], P1H, None, ALU.mult,
                      reads=hball, writes=(shk[mem],), tag="hh1")
                stt_op(shb[mem], c_hb[mem], P1L, shb[mem], ALU.mult, ALU.add,
                       reads=hball + (shk[mem],), writes=(shk[mem],), tag="hh2")
            for mem in range(2):
                act_op(qb[mem], shb[mem], AF.Sin, reads=(shk[mem],),
                       writes=(qk[mem],), tag="sinb")
            for mem in range(2):
                act_op(shb[mem], shb[mem], AF.Sin, scale=0.5, reads=(shk[mem],),
                       writes=(shk[mem],), tag="sinh2")
            for mem in range(2):
                act_op(shb[mem], shb[mem], AF.Square, reads=(shk[mem],),
                       writes=(shk[mem],), tag="shsq")
            for mem in range(2):
                act_op(shb[mem], shb[mem], AF.Copy, scale=-2.0, bias=1.0,
                       reads=(shk[mem],), writes=(shk[mem],), tag="cosb")
            for mem in range(2):
                for fgl in range(GRP2):
                    sl = slice(fgl * TS, (fgl + 1) * TS)
                    stt_op(qb[mem][:, sl], qb[mem][:, sl], Fc[:, fgl:fgl + 1],
                           c_hb[mem][:, sl], ALU.mult, ALU.add,
                           reads=(qk[mem], "c:F", f"hb{mem}f{fgl}"),
                           writes=(qk[mem],), tag="fsb")
            for mem in range(2):
                for fgl in range(GRP2):
                    sl = slice(fgl * TS, (fgl + 1) * TS)
                    stt_op(qb[mem][:, sl], shb[mem][:, sl], Ec[:, fgl:fgl + 1],
                           qb[mem][:, sl], ALU.mult, ALU.add,
                           reads=(qk[mem], "c:E", shk[mem]),
                           writes=(qk[mem],), tag="eq")
            for mem in range(2):
                akey = f"act{mem}g{grp}"
                a_h = c_actp[mem][0][:, grp * GRP2 * TS:(grp + 1) * GRP2 * TS]
                a_l = c_actp[mem][1][:, grp * GRP2 * TS:(grp + 1) * GRP2 * TS]
                act_op(a_h, qb[mem], AF.Relu, reads=(qk[mem],),
                       writes=(akey + "h",), tag="asphi")
                stt_op(a_l, qb[mem], 0.0, a_h, ALU.max, ALU.subtract,
                       reads=(qk[mem], akey + "h"), writes=(akey + "l",),
                       tag="asplo")

    def p2_DG(u):
        for fgl in range(FG):
            wpar = fgl % 2
            wpb = c_w2s[wpar]
            if u == 0:
                dma("sync", f"w2in{wpar}",
                    c_ws[:, 0:C], d_w2[fgl * 128:(fgl + 1) * 128, :].rearrange(
                        "(o p) d -> p (o d)", p=128),
                    reads=("ws",), writes=("ws",), new_group=True, tag="w2ldf")
                split(wpb[:, 0:C], wpb[:, C:2 * C], c_ws[:, 0:C], "ws",
                      f"w2p{wpar}", f"w2p{wpar}", tag="w2sp")
                dma("gpsimd", "wst", d_w2s.ap()[fgl], wpb,
                    reads=(f"w2p{wpar}",), writes=("D:w2s",),
                    new_group=True, tag="w2st")
            else:
                dma("sync", f"w2in{wpar}", wpb, d_w2s.ap()[fgl],
                    reads=("D:w2s",), writes=(f"w2p{wpar}",),
                    new_group=True, tag="w2ldp")
            for mem in range(2):
                akey = f"act{mem}g{fgl // GRP2}"
                ah = c_actp[mem][0][:, fgl * TS:(fgl + 1) * TS]
                al = c_actp[mem][1][:, fgl * TS:(fgl + 1) * TS]
                for dg in range(DG):
                    base = dg * 128
                    for pi, (off, ra) in enumerate(((0, ah), (C, ah), (0, al))):
                        mm(pb[dg][:, mem * TS:(mem + 1) * TS],
                           wpb[:, off + base:off + base + 128], ra,
                           fgl == 0 and mem == 0 and pi == 0,
                           fgl == FG - 1 and mem == 1 and pi == 2,
                           reads=(f"w2p{wpar}", akey + "h", akey + "l"),
                           writes=(f"pb{dg}",), tag="w2mm")

    def p2_Lpair(u):
        obuf = (c_out, c_q)
        okey = ("cout", "q")
        for mem in range(2):
            for dg in range(DG):
                ts_op(obuf[mem][:, dg * TS:(dg + 1) * TS],
                      pb[dg][:, mem * TS:(mem + 1) * TS],
                      v_b2[:, dg:dg + 1], None, ALU.add, reads=(f"pb{dg}", "c:b2"),
                      writes=(okey[mem],), tag="outev")
        for mem in range(2):
            um = u + mem
            par = um % 4
            b, s = divmod(um, NSUP)
            xkey = f"cxp{par}"
            ov = obuf[mem]
            ok = okey[mem]
            mrep, rrep = ln_stats(ov, TS, c_ws, "ws", ok, "l4", packed4=True)
            o3 = ov.rearrange("p (g w) -> p g w", g=OG)
            tt_op(o3, o3, mrep.unsqueeze(1).broadcast_to([128, OG, TS]),
                  ALU.subtract, reads=(ok, "pb4"), writes=(ok,), tag="fcent")
            tt_op(o3, o3, rrep.unsqueeze(1).broadcast_to([128, OG, TS]), ALU.mult,
                  reads=(ok, "pb5"), writes=(ok,), tag="fmul")
            tt_op(ov, ov, c_xp[par], ALU.add,
                  reads=(ok, xkey), writes=(ok,), tag="fresid")
            dma("gpsimd", "outst",
                d_out[b, :, s * TS:(s + 1) * TS].rearrange("(g p) w -> p g w", p=128),
                ov.rearrange("p (g w) -> p g w", g=OG), reads=(ok,),
                writes=("D:out",), new_group=True, tag="outst")

    NP2 = NSUP2 // 2
    p2_Pload(0)
    p2_Pload(1)
    p2_Pstats(0)
    p2_Pstats(1)
    p2_Papply(0)
    p2_Papply(1)
    for p in range(NP2):
        u = 2 * p
        if p > 0:
            p2_DG(u - 2)
            p2_Lpair(u - 2)
        if u + 2 < NSUP2:
            p2_Pload(u + 2)
            p2_Pload(u + 3)
            p2_Pstats(u + 2)
            p2_Pstats(u + 3)
        p2_FG(u)
        if u + 2 < NSUP2:
            p2_Papply(u + 2)
            p2_Papply(u + 3)
    p2_DG(NSUP2 - 2)
    p2_Lpair(NSUP2 - 2)

    kb.barrier()

    kb.resolve()
    blk = es.enter_context(nc.Block())
    with es:
        for name in ENGS:
            getattr(blk, name)(lambda eng, name=name: kb.replay(name, eng))
    return nc


def _fm_vec(v, groups):
    return np.ascontiguousarray(np.asarray(v, np.float32).reshape(groups, 128).T)


def stage_inputs(cfg, inputs):
    B, T, C, DFF, NC, TLOC = (cfg["B"], cfg["T"], cfg["C"], cfg["DFF"],
                              cfg["NC"], cfg["TLOC"])
    x = np.asarray(inputs["L0_attn_x"], np.float32)
    xt = np.transpose(x, (0, 2, 1))
    xpad = np.zeros((B, C, T + 2), np.float32)
    xpad[:, :, 1:T + 1] = xt
    shared = {
        "wkT": np.ascontiguousarray(np.asarray(inputs["L0_attn_Wk"], np.float32).T),
        "wvT": np.ascontiguousarray(np.asarray(inputs["L0_attn_Wv"], np.float32).T),
        "wrT": np.ascontiguousarray(np.asarray(inputs["L0_attn_Wr"], np.float32).T),
        "woT": np.ascontiguousarray(np.asarray(inputs["L0_attn_Wo"], np.float32).T),
        "w1": np.ascontiguousarray(np.asarray(inputs["L0_ffn_W1"], np.float32)),
        "w2": np.ascontiguousarray(np.asarray(inputs["L0_ffn_W2"], np.float32)),
        "sd_fm": _fm_vec(inputs["L0_attn_spatial_decay"], C // 128),
        "sf_fm": _fm_vec(inputs["L0_attn_spatial_first"], C // 128),
        "mixk_fm": _fm_vec(inputs["L0_attn_mix_k"], C // 128),
        "mixv_fm": _fm_vec(inputs["L0_attn_mix_v"], C // 128),
        "mixr_fm": _fm_vec(inputs["L0_attn_mix_r"], C // 128),
        "b1_fm": _fm_vec(inputs["L0_ffn_b1"], DFF // 128),
        "theta_fm": _fm_vec(inputs["L0_ffn_quantum_phases"], DFF // 128),
        "qwr_fm": _fm_vec(inputs["L0_ffn_quantum_weights_real"], DFF // 128),
        "qwi_fm": _fm_vec(inputs["L0_ffn_quantum_weights_imag"], DFF // 128),
        "b2_fm": _fm_vec(inputs["L0_ffn_b2"], C // 128),
        "sd_row": np.ascontiguousarray(
            np.asarray(inputs["L0_attn_spatial_decay"], np.float32).reshape(1, C)),
        "iota256": np.arange(cfg["TS"], dtype=np.float32).reshape(1, -1),
        "iota8": np.arange(cfg["NSUP"], dtype=np.float32).reshape(1, -1),
        "ones1": np.ones((1, 128), np.float32),
        "invC_row": np.full((1, 128), 1.0 / C, np.float32),
        "onesP": np.ones((128, 1), np.float32),
    }
    in_maps = []
    for i in range(NC):
        m = dict(shared)
        m["x_fm"] = np.ascontiguousarray(xpad[:, :, i * TLOC:(i + 1) * TLOC + 2])
        mask = np.zeros((128, 8), np.float32)
        mask[:, :i] = 1.0
        m["mask"] = mask
        in_maps.append(m)
    return in_maps


def assemble_output(cfg, results):
    outs = [np.transpose(np.asarray(r["out_fm"]), (0, 2, 1)) for r in results]
    return np.ascontiguousarray(np.concatenate(outs, axis=1))


def kernel(**inputs):
    x = np.asarray(inputs["L0_attn_x"])
    B, T, C = x.shape
    DFF = np.asarray(inputs["L0_ffn_W1"]).shape[1]
    cfg = make_cfg(B, T, C, DFF)
    nc_prog = build_program(cfg)
    in_maps = stage_inputs(cfg, inputs)
    res = run_bass_kernel_spmd(nc_prog, in_maps, core_ids=list(range(cfg["NC"])))
    return assemble_output(cfg, res.results)


# version 25
# speedup vs baseline: 1.0253x; 1.0014x over previous
import numpy as np
from contextlib import ExitStack

import concourse.bass as bass
import concourse.mybir as mybir
from concourse.bass_utils import run_bass_kernel_spmd

F32 = mybir.dt.float32
BF16 = mybir.dt.bfloat16
AF = mybir.ActivationFunctionType
ALU = mybir.AluOpType

ENGS = ("tensor", "vector", "scalar", "sync", "gpsimd")


def make_cfg(B, T, C, DFF, ncores=8, TS=256):
    TLOC = T // ncores
    cfg = dict(B=B, T=T, C=C, DFF=DFF, NC=ncores, TS=TS, TLOC=TLOC,
               NSUP=TLOC // TS, OG=C // 128, CG=C // 128, FG=DFF // 128,
               DG=C // 128, WSC=1.0 / T)
    assert TLOC % TS == 0 and C % 128 == 0 and DFF % 128 == 0
    return cfg


class _Node:
    def __init__(self, eng, fn, reads, writes, is_dma=False, slot=None, tag=""):
        self.eng = eng
        self.fn = fn
        self.reads = reads
        self.writes = writes
        self.deps = []
        self.marked = False
        self.semname = None
        self.semval = None
        self.clock = None
        self.waits = ()
        self.is_dma = is_dma
        self.slot = slot
        self.tag = tag
        self.new_group = False
        self.manual_sem = None
        self.issue_wait = None
        self.barrier_targets = None


class _Slot:
    def __init__(self, name):
        self.name = name
        self.count = 0
        self.open_group = []
        self.last_group_end = None


class KB:
    def __init__(self, nc, es):
        self.nc = nc
        self.nodes = []
        self.last_w = {}
        self.readers = {}
        self.sems = {}
        self.es = es
        for e in ENGS:
            self.sems["e_" + e] = es.enter_context(nc.semaphore("se_" + e))
        self.slots = {}

    def slot(self, name):
        if name not in self.slots:
            self.slots[name] = _Slot(name)
            self.sems["s_" + name] = self.es.enter_context(
                self.nc.semaphore("ss_" + name))
        return self.slots[name]

    def _dep(self, node):
        deps = set()
        for k in node.reads:
            w = self.last_w.get(k)
            if w is not None:
                deps.add(w)
        for k in node.writes:
            w = self.last_w.get(k)
            if w is not None:
                deps.add(w)
            for r in self.readers.get(k, ()):
                deps.add(r)
        node.deps = [d for d in deps if d is not node]
        for k in node.reads:
            self.readers.setdefault(k, []).append(node)
        for k in node.writes:
            self.last_w[k] = node
            self.readers[k] = []

    def op(self, eng, fn, reads=(), writes=(), tag=""):
        n = _Node(eng, fn, tuple(reads), tuple(writes), tag=tag)
        self._dep(n)
        self.nodes.append(n)
        return n

    def dma(self, eng, slot_name, fn, reads=(), writes=(), new_group=False, tag=""):
        sl = self.slot(slot_name)
        n = _Node(eng, fn, tuple(reads), tuple(writes), is_dma=True, slot=sl, tag=tag)
        n.new_group = new_group
        self._dep(n)
        self.nodes.append(n)
        return n

    def manual(self, eng, fn, reads=(), writes=(), semname=None, inc=1, tag=""):
        n = _Node(eng, fn, tuple(reads), tuple(writes), tag=tag)
        n.manual_sem = (semname, inc)
        self._dep(n)
        self.nodes.append(n)
        return n

    def add_sem(self, name):
        if name not in self.sems:
            self.sems[name] = self.es.enter_context(self.nc.semaphore("sx_" + name))
        return self.sems[name]

    def barrier(self):
        n = _Node("barrier", None, (), (), tag="barrier")
        self.nodes.append(n)
        self.last_w = {k: v for k, v in self.last_w.items() if k.startswith("D:")}
        self.readers = {k: v for k, v in self.readers.items() if k.startswith("D:")}
        return n

    def resolve(self):
        def implicit(d, n):
            return (d.eng == "tensor" and n.eng == "tensor" and not d.is_dma
                    and d.manual_sem is None)

        last_per_eng = {}
        for n in self.nodes:
            if n.eng == "barrier":
                for e, ln in last_per_eng.items():
                    if not ln.is_dma and ln.manual_sem is None:
                        ln.marked = True
                continue
            for d in n.deps:
                if not d.is_dma and d.manual_sem is None and not implicit(d, n):
                    d.marked = True
            last_per_eng[n.eng] = n
        for ln in last_per_eng.values():
            if not ln.is_dma and ln.manual_sem is None:
                ln.marked = True

        nincs = {e: 0 for e in ENGS}
        eclock = {e: {} for e in ENGS}

        def merge(dst, src):
            for k, v in src.items():
                if dst.get(k, -1) < v:
                    dst[k] = v

        def close_slot(sl):
            if sl.open_group:
                v = 16 * sl.count
                for g in sl.open_group:
                    g.semval = v
                    g.clock = dict(g.clock or {})
                    g.clock["s_" + sl.name] = v
                sl.last_group_end = sl.open_group[-1]
                sl.open_group = []

        for n in self.nodes:
            if n.eng == "barrier":
                for sl in self.slots.values():
                    close_slot(sl)
                mc = {}
                for e in ENGS:
                    merge(mc, eclock[e])
                    if nincs[e] > 0:
                        mc["e_" + e] = max(mc.get("e_" + e, 0), nincs[e])
                for name, sl in self.slots.items():
                    if sl.count > 0:
                        mc["s_" + name] = max(mc.get("s_" + name, 0), 16 * sl.count)
                n.barrier_targets = dict(mc)
                for e in ENGS:
                    eclock[e] = dict(mc)
                continue
            e = n.eng
            if n.is_dma:
                sl = n.slot
                n.semname = "s_" + sl.name
                if n.new_group:
                    close_slot(sl)
                    if sl.count > 0:
                        n.issue_wait = ("s_" + sl.name, 16 * sl.count,
                                        sl.last_group_end)
            elif n.manual_sem is not None:
                n.semname, n.semval = n.manual_sem
            elif n.marked:
                nincs[e] += 1
                n.semname = "e_" + e
                n.semval = nincs[e]
            waits = []
            clk = eclock[e]
            for d in n.deps:
                if implicit(d, n) and not d.marked:
                    continue
                if d.is_dma and d.semval is None:
                    close_slot(d.slot)
                if implicit(d, n):
                    continue
                assert d.semname is not None and d.semval is not None, (
                    f"unsynced dep: {d.tag} <- {n.tag}")
                if clk.get(d.semname, 0) >= d.semval:
                    continue
                waits.append((d.semname, d.semval))
                merge(clk, d.clock or {})
                clk[d.semname] = max(clk.get(d.semname, 0), d.semval)
            if n.issue_wait is not None:
                sname, v, ref = n.issue_wait
                if clk.get(sname, 0) < v:
                    waits.append((sname, v))
                    if ref is not None:
                        merge(clk, ref.clock or {})
                    clk[sname] = max(clk.get(sname, 0), v)
            n.waits = tuple(waits)
            n.clock = dict(clk)
            if n.is_dma:
                sl = n.slot
                sl.count += 1
                sl.open_group.append(n)
            elif n.semname is not None and n.semval is not None:
                n.clock[n.semname] = n.semval
        for sl in self.slots.values():
            close_slot(sl)

    def replay(self, engine_name, eng):
        for n in self.nodes:
            if n.eng == "barrier":
                for sname, v in sorted(n.barrier_targets.items()):
                    eng.wait_ge(self.sems[sname], v)
                continue
            if n.eng != engine_name:
                continue
            for sname, v in n.waits:
                eng.wait_ge(self.sems[sname], v)
            instr = n.fn(eng)
            if instr is None:
                continue
            if n.is_dma:
                instr.then_inc(self.sems[n.semname], 16)
            elif n.manual_sem is not None:
                instr.then_inc(self.sems[n.semname], n.manual_sem[1])
            elif n.marked:
                instr.then_inc(self.sems[n.semname], 1)


def build_program(cfg):
    B, T, C, DFF = cfg["B"], cfg["T"], cfg["C"], cfg["DFF"]
    TS, TLOC, NSUP = cfg["TS"], cfg["TLOC"], cfg["NSUP"]
    OG, CG, FG, DG = cfg["OG"], cfg["CG"], cfg["FG"], cfg["DG"]
    NC = cfg["NC"]
    TSH = TS + 2
    NSUP2 = B * NSUP
    GRP2 = 8
    NGRP2 = FG // GRP2
    invC = 1.0 / C
    eps = 1e-5

    nc = bass.Bass("TRN2", target_bir_lowering=False, debug=False)

    d_x = nc.dram_tensor("x_fm", [B, C, TLOC + 2], F32, kind="ExternalInput").ap()
    d_wk = nc.dram_tensor("wkT", [C, C], F32, kind="ExternalInput").ap()
    d_wv = nc.dram_tensor("wvT", [C, C], F32, kind="ExternalInput").ap()
    d_wr = nc.dram_tensor("wrT", [C, C], F32, kind="ExternalInput").ap()
    d_wo = nc.dram_tensor("woT", [C, C], F32, kind="ExternalInput").ap()
    d_w1 = nc.dram_tensor("w1", [C, DFF], F32, kind="ExternalInput").ap()
    d_w2 = nc.dram_tensor("w2", [DFF, C], F32, kind="ExternalInput").ap()
    d_sdf = nc.dram_tensor("sd_fm", [128, OG], F32, kind="ExternalInput").ap()
    d_sff = nc.dram_tensor("sf_fm", [128, OG], F32, kind="ExternalInput").ap()
    d_mk = nc.dram_tensor("mixk_fm", [128, OG], F32, kind="ExternalInput").ap()
    d_mv = nc.dram_tensor("mixv_fm", [128, OG], F32, kind="ExternalInput").ap()
    d_mr = nc.dram_tensor("mixr_fm", [128, OG], F32, kind="ExternalInput").ap()
    d_b1 = nc.dram_tensor("b1_fm", [128, FG], F32, kind="ExternalInput").ap()
    d_th = nc.dram_tensor("theta_fm", [128, FG], F32, kind="ExternalInput").ap()
    d_qr = nc.dram_tensor("qwr_fm", [128, FG], F32, kind="ExternalInput").ap()
    d_qi = nc.dram_tensor("qwi_fm", [128, FG], F32, kind="ExternalInput").ap()
    d_b2 = nc.dram_tensor("b2_fm", [128, OG], F32, kind="ExternalInput").ap()
    d_sdr = nc.dram_tensor("sd_row", [1, C], F32, kind="ExternalInput").ap()
    d_i256 = nc.dram_tensor("iota256", [1, TS], F32, kind="ExternalInput").ap()
    d_i8 = nc.dram_tensor("iota8", [1, NSUP], F32, kind="ExternalInput").ap()
    d_ones1 = nc.dram_tensor("ones1", [1, 128], F32, kind="ExternalInput").ap()
    d_invC = nc.dram_tensor("invC_row", [1, 128], F32, kind="ExternalInput").ap()
    d_onesP = nc.dram_tensor("onesP", [128, 1], F32, kind="ExternalInput").ap()
    d_mask = nc.dram_tensor("mask", [128, 8], F32, kind="ExternalInput").ap()
    d_out = nc.dram_tensor("out_fm", [B, C, TLOC], F32, kind="ExternalOutput").ap()
    d_num = nc.dram_tensor("sp_num", [NSUP2, 128, OG * TS], F32)
    d_den = nc.dram_tensor("sp_den", [NSUP2, 128, OG * TS], F32)
    d_nh = nc.dram_tensor("sp_nh", [NSUP2, 128, OG * TSH], BF16)
    d_nl = nc.dram_tensor("sp_nl", [NSUP2, 128, OG * TSH], BF16)
    d_xrh = nc.dram_tensor("sp_xrh", [NSUP2, 128, 4 * TS], BF16)
    d_xrl = nc.dram_tensor("sp_xrl", [NSUP2, 128, 4 * TS], BF16)
    d_xp = nc.dram_tensor("sp_xp", [NSUP2, 128, OG * TS], F32)
    d_tot = nc.dram_tensor("sp_tot", [128, 2 * B * OG], F32)
    d_gat = nc.dram_tensor("sp_gat", [NC * 128, 2 * B * OG], F32)
    d_w1s = nc.dram_tensor("sp_w1s", [DFF // 128, 128, 2 * C], BF16)
    d_w2s = nc.dram_tensor("sp_w2s", [DFF // 128, 128, 2 * C], BF16)

    es = ExitStack()
    CONST_F32 = 2944
    PHF = 23100
    PHB = 53600
    arena = es.enter_context(nc.sbuf_tensor("arena", [128, CONST_F32 + PHF], F32))
    arenab = es.enter_context(nc.sbuf_tensor("arenab", [128, PHB], BF16))
    _off = [0]
    _offb = [0]

    def alloc(nelem):
        o = _off[0]
        _off[0] += nelem
        assert _off[0] <= CONST_F32 + PHF, f"arena overflow {_off[0]}"
        return arena[:, o:o + nelem]

    def allocb(nelem):
        o = _offb[0]
        _offb[0] += nelem
        assert _offb[0] <= PHB, f"arenab overflow {_offb[0]}"
        return arenab[:, o:o + nelem]

    v_sd = alloc(OG); v_sf = alloc(OG)
    v_mk = alloc(OG); v_mv = alloc(OG); v_mr = alloc(OG)
    v_omk = alloc(OG); v_omv = alloc(OG); v_omr = alloc(OG)
    v_eu = alloc(OG); v_ewcol = alloc(OG)
    v_ew2 = alloc(OG)
    v_ew2r = alloc(4 * OG)
    v_mask = alloc(8)
    v_b1 = alloc(FG); v_th = alloc(FG); v_qr = alloc(FG); v_qi = alloc(FG)
    v_sc = alloc(FG); v_cth = alloc(FG); v_aqh = alloc(FG); v_aql = alloc(FG)
    v_E = alloc(FG); v_F = alloc(FG)
    v_b2 = alloc(OG)
    v_tmp32 = alloc(FG)
    v_A = alloc(4 * OG)
    v_onesP = alloc(1)
    v_eps = alloc(1)
    r_sd = alloc(C)[0:1, :]
    r_i256 = alloc(TS)[0:1, :]
    r_i8 = alloc(NSUP)[0:1, :]
    r_ones1 = alloc(128)[0:1, :]
    r_invC = alloc(128)[0:1, :]
    r_rows = alloc(TSH * 3)
    assert _off[0] <= CONST_F32, f"const region overflow: {_off[0]}"
    _off[0] = CONST_F32

    def phase_reset():
        _off[0] = CONST_F32
        _offb[0] = 0

    kb = KB(nc, es)
    kb.add_sem("cc")
    pb = [es.enter_context(nc.psum_tensor(f"pb{i}", [128, 512], F32)) for i in range(8)]

    TAG = [0]

    def t(s):
        TAG[0] += 1
        return f"{s}#{TAG[0]}"

    def ts_op(out, in0, s1, s2, op0, op1=None, reads=(), writes=(), tag=""):
        def f(e, out=out, in0=in0, s1=s1, s2=s2, op0=op0, op1=op1):
            if op1 is None:
                return e.tensor_scalar(out=out, in0=in0, scalar1=s1, scalar2=None,
                                       op0=op0)
            return e.tensor_scalar(out=out, in0=in0, scalar1=s1, scalar2=s2,
                                   op0=op0, op1=op1)
        return kb.op("vector", f, reads, writes, tag=t(tag or "ts"))

    def tt_op(out, in0, in1, op, reads=(), writes=(), tag=""):
        def f(e, out=out, in0=in0, in1=in1, op=op):
            return e.tensor_tensor(out=out, in0=in0, in1=in1, op=op)
        return kb.op("vector", f, reads, writes, tag=t(tag or "tt"))

    def stt_op(out, in0, sc, in1, op0, op1, reads=(), writes=(), tag=""):
        def f(e, out=out, in0=in0, sc=sc, in1=in1, op0=op0, op1=op1):
            return e.scalar_tensor_tensor(out=out, in0=in0, scalar=sc, in1=in1,
                                          op0=op0, op1=op1)
        return kb.op("vector", f, reads, writes, tag=t(tag or "stt"))

    def act_op(out, in_, func, bias=None, scale=None, reads=(), writes=(), tag=""):
        def f(e, out=out, in_=in_, func=func, bias=bias, scale=scale):
            kw = {}
            if bias is not None:
                kw["bias"] = bias
            if scale is not None:
                kw["scale"] = scale
            return e.activation(out=out, in_=in_, func=func, **kw)
        return kb.op("scalar", f, reads, writes, tag=t(tag or "act"))

    def copy_op(out, in_, reads=(), writes=(), tag=""):
        def f(e, out=out, in_=in_):
            return e.tensor_copy(out=out, in_=in_)
        return kb.op("vector", f, reads, writes, tag=t(tag or "cp"))

    def memset_op(ap, val, writes=(), tag=""):
        def f(e, ap=ap, val=val):
            return e.memset(ap, val)
        return kb.op("vector", f, (), writes, tag=t(tag or "ms"))

    def mm(out, lhsT, rhs, start, stop, reads=(), writes=(), tag=""):
        def f(e, out=out, lhsT=lhsT, rhs=rhs, start=start, stop=stop):
            return e.matmul(out, lhsT, rhs, start=start, stop=stop)
        return kb.op("tensor", f, reads, writes, tag=t(tag or "mm"))

    def mm3(ps, lhs_h, lhs_l, rhs_h, rhs_l, nk, reads, writes, tag=""):
        passes = ((lhs_h, rhs_h), (lhs_l, rhs_h), (lhs_h, rhs_l))
        for pi, (lf, rf) in enumerate(passes):
            for i in range(nk):
                mm(ps, lf(i), rf(i), pi == 0 and i == 0,
                   pi == 2 and i == nk - 1, reads=reads, writes=writes, tag=tag)

    def scan_op(out, data0, data1, initial, reads=(), writes=(), tag=""):
        def f(e, out=out, d0=data0, d1=data1, init=initial):
            return e.tensor_tensor_scan(out=out, data0=d0, data1=d1,
                                        initial=init, op0=ALU.mult, op1=ALU.add)
        return kb.op("vector", f, reads, writes, tag=t(tag or "scan"))

    def dma(eng, slot, out, in_, reads=(), writes=(), new_group=False, slow=False,
            tag=""):
        def f(e, out=out, in_=in_, slow=slow):
            if slow:
                return e.dma_start(out=out, in_=in_, allow_slow_non_contiguous=True)
            return e.dma_start(out=out, in_=in_)
        return kb.dma(eng, slot, f, reads, writes, new_group=new_group,
                      tag=t(tag or ("dma_" + slot)))

    def split(hi, lo, src, skey, hkey, lkey, tag=""):
        act_op(hi, src, AF.Copy, reads=(skey,), writes=(hkey,), tag=tag + "hi")
        tt_op(lo, src, hi, ALU.subtract, reads=(skey, hkey), writes=(lkey,),
              tag=tag + "lo")

    for ap_, d_, key in ((v_sd, d_sdf, "c:sd"), (v_sf, d_sff, "c:sf"),
                         (v_mk, d_mk, "c:mk"), (v_mv, d_mv, "c:mv"),
                         (v_mr, d_mr, "c:mr"), (v_b1, d_b1, "c:b1"),
                         (v_th, d_th, "c:th"), (v_qr, d_qr, "c:qr"),
                         (v_qi, d_qi, "c:qi"), (v_b2, d_b2, "c:b2"),
                         (r_sd, d_sdr, "c:sdrow"), (r_i256, d_i256, "c:i256"),
                         (r_i8, d_i8, "c:i8"), (r_ones1, d_ones1, "c:ones1"),
                         (r_invC, d_invC, "c:invC"), (v_onesP, d_onesP, "c:onesP"),
                         (v_mask, d_mask, "c:mask")):
        dma("sync", "cin", ap_, d_, writes=(key,), tag="cld")

    memset_op(v_eps, eps, writes=("c:eps",), tag="epsfill")
    for v_m, v_om, km, ko in ((v_mk, v_omk, "c:mk", "c:omk"),
                              (v_mv, v_omv, "c:mv", "c:omv"),
                              (v_mr, v_omr, "c:mr", "c:omr")):
        ts_op(v_om, v_m, -1.0, 1.0, ALU.mult, ALU.add, reads=(km,), writes=(ko,),
              tag="om")
    act_op(v_eu, v_sf, AF.Exp, scale=cfg["WSC"], reads=("c:sf",), writes=("c:eu",),
           tag="eu")
    act_op(v_ewcol, v_sd, AF.Exp, scale=cfg["WSC"], reads=("c:sd",), writes=("c:ew",),
           tag="ew")
    act_op(v_ew2, v_sd, AF.Exp, scale=float(TLOC) * cfg["WSC"], reads=("c:sd",),
           writes=("c:ew2",), tag="ew2")
    ts_op(v_ew2, v_ew2, -1.0, None, ALU.add, reads=("c:ew2",), writes=("c:ew2",),
          tag="ew2m1")
    for r in range(4):
        copy_op(v_ew2r[:, r * OG:(r + 1) * OG], v_ew2, reads=("c:ew2",),
                writes=("c:ew2r",), tag="ew2rep")
    PI = float(np.float32(np.pi))
    TWO_PI = 2.0 * np.pi
    c1 = 6.28125
    c2 = float(np.float32(TWO_PI - c1))
    c3 = float(np.float32(TWO_PI - c1 - float(np.float32(TWO_PI - c1))))
    for _ in range(7):
        ts_op(v_tmp32, v_th, PI, None, ALU.is_gt, reads=("c:th",), writes=("c:g",),
              tag="rrgt")
        for cc_ in (c1, c2, c3):
            stt_op(v_th, v_tmp32, -cc_, v_th, ALU.mult, ALU.add,
                   reads=("c:g", "c:th"), writes=("c:th",), tag="rrsub")
        ts_op(v_tmp32, v_th, -PI, None, ALU.is_lt, reads=("c:th",), writes=("c:g",),
              tag="rrlt")
        for cc_ in (c1, c2, c3):
            stt_op(v_th, v_tmp32, cc_, v_th, ALU.mult, ALU.add,
                   reads=("c:g", "c:th"), writes=("c:th",), tag="rradd")
    act_op(v_sc, v_th, AF.Sin, reads=("c:th",), writes=("c:sc",), tag="sinth")
    act_op(v_cth, v_th, AF.Sin, scale=0.5, reads=("c:th",), writes=("c:cth",),
           tag="sinth2")
    tt_op(v_cth, v_cth, v_cth, ALU.mult, reads=("c:cth",), writes=("c:cth",), tag="csq")
    ts_op(v_cth, v_cth, -2.0, 1.0, ALU.mult, ALU.add, reads=("c:cth",),
          writes=("c:cth",), tag="costh")
    P1H = float(np.float32(0.1))
    P1L = float(0.1 - P1H)
    tt_op(v_aqh, v_cth, v_qr, ALU.mult, reads=("c:cth", "c:qr"), writes=("c:aqh",),
          tag="qp1")
    tt_op(v_aql, v_sc, v_qi, ALU.mult, reads=("c:sc", "c:qi"), writes=("c:aql",),
          tag="qp2")
    tt_op(v_aqh, v_aqh, v_aql, ALU.add, reads=("c:aqh", "c:aql"), writes=("c:aqh",),
          tag="qp3")
    tt_op(v_tmp32, v_aqh, v_qi, ALU.mult, reads=("c:aqh", "c:qi"), writes=("c:t3",),
          tag="qp4")
    tt_op(v_aqh, v_tmp32, v_sc, ALU.mult, reads=("c:t3", "c:sc"), writes=("c:e1",),
          tag="e1")
    ts_op(v_E, v_aqh, P1H, None, ALU.mult, reads=("c:e1",), writes=("c:E",),
          tag="Eh")
    stt_op(v_E, v_aqh, P1L, v_E, ALU.mult, ALU.add, reads=("c:e1", "c:E"),
           writes=("c:E",), tag="El")
    tt_op(v_aqh, v_tmp32, v_cth, ALU.mult, reads=("c:t3", "c:cth"), writes=("c:e1",),
          tag="f1")
    ts_op(v_F, v_aqh, P1H, None, ALU.mult, reads=("c:e1",), writes=("c:F",),
          tag="Fh")
    stt_op(v_F, v_aqh, P1L, v_F, ALU.mult, ALU.add, reads=("c:e1", "c:F"),
           writes=("c:F",), tag="Fl")

    def ln_stats(xv, width, sqv, sq_key, x_key, tagp, packed=False, packed4=False,
                 sq_vec=False):
        if sq_vec:
            tt_op(sqv[:, 0:OG * width], xv[:, 0:OG * width], xv[:, 0:OG * width],
                  ALU.mult, reads=(x_key,), writes=(sq_key,), tag=tagp + "sq")
        else:
            act_op(sqv[:, 0:OG * width], xv[:, 0:OG * width], AF.Square,
                   reads=(x_key,), writes=(sq_key,), tag=tagp + "sq")
        if packed4:
            assert width <= 256
            rsum = pb[4][0:1, 0:width]
            rsq = pb[5][0:1, 0:width]
            ksum, ksq = "pb4", "pb5"
            mrep = pb[4][:, 256:256 + width]
            rrep = pb[5][:, 256:256 + width]
        elif packed:
            assert width <= 256
            rsum = pb[6][0:1, 0:width]
            rsq = pb[7][0:1, 0:width]
            ksum, ksq = "pb6", "pb7"
            mrep = pb[6][:, 256:256 + width]
            rrep = pb[7][:, 256:256 + width]
        else:
            rsum = pb[4][0:1, 0:width]
            rsq = pb[5][0:1, 0:width]
            ksum, ksq = "pb4", "pb5"
            mrep = pb[6][:, 0:width]
            rrep = pb[7][:, 0:width]
        for cg in range(CG):
            mm(rsum, v_onesP, xv[:, cg * width:(cg + 1) * width],
               cg == 0, cg == CG - 1, reads=("c:onesP", x_key), writes=(ksum,),
               tag=tagp + "summ")
        for cg in range(CG):
            mm(rsq, v_onesP, sqv[:, cg * width:(cg + 1) * width],
               cg == 0, cg == CG - 1, reads=("c:onesP", sq_key), writes=(ksq,),
               tag=tagp + "sqmm")
        m_row = r_rows[0:1, 0:width]
        v_row = r_rows[0:1, TSH:TSH + width]
        s_row = r_rows[0:1, 2 * TSH:2 * TSH + width]
        ts_op(m_row, rsum, invC, None, ALU.mult, reads=(ksum,), writes=("rows_m",),
              tag=tagp + "m")
        tt_op(v_row, m_row, m_row, ALU.mult, reads=("rows_m",), writes=("rows_v",),
              tag=tagp + "msq")
        stt_op(v_row, rsq, invC, v_row, ALU.mult, ALU.subtract,
               reads=(ksq, "rows_v"), writes=("rows_v",), tag=tagp + "var")
        act_op(v_row, v_row, AF.Ln, bias=v_eps[0:1, 0:1], reads=("rows_v", "c:eps"),
               writes=("rows_v",), tag=tagp + "lnv")
        act_op(s_row, v_row, AF.Exp, scale=-0.5, reads=("rows_v",), writes=("rows_s",),
               tag=tagp + "rstd")
        mm(mrep, r_ones1, m_row, True, True,
           reads=("c:ones1", "rows_m"), writes=(ksum if (packed or packed4) else "pb6",),
           tag=tagp + "mrep")
        mm(rrep, r_ones1, s_row, True, True,
           reads=("c:ones1", "rows_s"), writes=(ksq if (packed or packed4) else "pb7",),
           tag=tagp + "rrep")
        return mrep, rrep

    def weight_split(d_src, hi, lo, hkey, lkey, scr_list, scr_keys, slot, tag):
        nch = CG // 2
        for i in range(nch):
            scr = scr_list[i % len(scr_list)][:, 0:2 * C]
            skey = scr_keys[i % len(scr_keys)]
            dma("sync", slot, scr.rearrange("p (cg o) -> p cg o", cg=2),
                d_src[i * 256:(i + 1) * 256, :].rearrange("(cg p) o -> p cg o", p=128),
                reads=(), writes=(skey,), new_group=True, tag=tag + "ld")
            split(hi[:, i * 2 * C:(i + 1) * 2 * C], lo[:, i * 2 * C:(i + 1) * 2 * C],
                  scr, skey, hkey, lkey, tag=tag + "sp")

    phase_reset()
    a_x = [alloc(OG * TSH) for _ in range(2)]
    a_sq = alloc(OG * TSH)
    a_n = [alloc(OG * TSH) for _ in range(2)]
    a_d = alloc(4 * TS)
    a_mscr = alloc(4 * TS)
    a_ek = alloc(OG * TS)
    a_ekv = alloc(OG * TS)
    a_asc = alloc(OG * (TS + 1))
    a_bsc = alloc(OG * (TS + 1))
    a_ewrep = alloc(OG * TS)
    a_tots = alloc(4 * OG)
    a_gath = alloc(2 * B * NC * OG)
    a_Atmp = alloc(4 * OG)
    a_wkh = allocb(OG * C); a_wkl = allocb(OG * C)
    a_wvh = allocb(OG * C); a_wvl = allocb(OG * C)
    a_xkp = [[allocb(4 * TS), allocb(4 * TS)] for _ in range(2)]
    a_xvp = [[allocb(4 * TS), allocb(4 * TS)] for _ in range(2)]
    a_xrp = [[allocb(4 * TS), allocb(4 * TS)] for _ in range(2)]
    a_np = [[allocb(OG * TSH), allocb(OG * TSH)] for _ in range(2)]

    weight_split(d_wk, a_wkh, a_wkl, "wkh", "wkl", [a_x[0], a_x[1]], ["x0", "x1"],
                 "wsp", "wk")
    weight_split(d_wv, a_wvh, a_wvl, "wvh", "wvl", [a_x[0], a_x[1]], ["x0", "x1"],
                 "wsp", "wv")

    memset_op(a_ewrep, 1.0, writes=("ewrep",), tag="ewrep0")
    for og in range(OG):
        ts_op(a_ewrep[:, og * TS:(og + 1) * TS], a_ewrep[:, og * TS:(og + 1) * TS],
              v_ewcol[:, og:og + 1], None, ALU.mult,
              reads=("ewrep", "c:ew"), writes=("ewrep",), tag="ewrep")

    def p1a_A(u):
        b, s = divmod(u, NSUP)
        par = u % 2
        xv, nv = a_x[par], a_n[par]
        xkey, nkey = f"x{par}", f"n{par}"
        dma("sync", f"xin{par}", xv.rearrange("p (g w) -> p g w", g=OG),
            d_x[b, :, s * TS:s * TS + TSH].rearrange("(g p) w -> p g w", p=128),
            writes=(xkey,), new_group=True, tag="xload")
        mrep, rrep = ln_stats(xv, TSH, a_sq, "sq", xkey, "l1")
        nv3 = nv.rearrange("p (g w) -> p g w", g=OG)
        tt_op(nv3, xv.rearrange("p (g w) -> p g w", g=OG),
              mrep.unsqueeze(1).broadcast_to([128, OG, TSH]), ALU.subtract,
              reads=(xkey, "pb6"), writes=(nkey,), tag="ncent")
        tt_op(nv3, nv3, rrep.unsqueeze(1).broadcast_to([128, OG, TSH]), ALU.mult,
              reads=(nkey, "pb7"), writes=(nkey,), tag="nmul")
        split(a_np[par][0], a_np[par][1], nv, nkey, f"nh{par}", f"nl{par}", tag="nsp")
        for cg in range(4):
            off = cg * TSH
            sh = nv[:, off:off + TS] if cg < 2 else nv[:, off + 2:off + TS + 2]
            mid = nv[:, off + 1:off + TS + 1]
            dsl = a_d[:, cg * TS:(cg + 1) * TS]
            tt_op(dsl, sh, mid, ALU.subtract, reads=(nkey,), writes=("d",),
                  tag="dshift")
        for om_, pr, kk in ((v_omk, a_xkp[par], f"xk{par}"),
                            (v_omv, a_xvp[par], f"xv{par}"),
                            (v_omr, a_xrp[par], f"xr{par}")):
            ko = {id(v_omk): "c:omk", id(v_omv): "c:omv", id(v_omr): "c:omr"}[id(om_)]
            for cg in range(4):
                off = cg * TSH
                mid = nv[:, off + 1:off + TS + 1]
                stt_op(a_mscr[:, cg * TS:(cg + 1) * TS],
                       a_d[:, cg * TS:(cg + 1) * TS], om_[:, cg:cg + 1], mid,
                       ALU.mult, ALU.add, reads=("d", nkey, ko), writes=("mscr",),
                       tag="mix")
            split(pr[0], pr[1], a_mscr, "mscr", kk + "h", kk + "l", tag="xsp")

    def p1a_B(u):
        b, s = divmod(u, NSUP)
        par = u % 2
        nkey = f"n{par}"

        def rhs_pair(pref, half):
            def f(cg, pref=pref, half=half):
                if cg < 4:
                    src = (a_xkp if pref == "k" else a_xvp)[par][half]
                    return src[:, cg * TS:(cg + 1) * TS]
                nph = a_np[par][half]
                return nph[:, cg * TSH + 1:cg * TSH + 1 + TS]
            return f

        rk_h = (f"xk{par}h", f"nh{par}")
        rk_l = (f"xk{par}l", f"nl{par}")
        rv_h = (f"xv{par}h", f"nh{par}")
        rv_l = (f"xv{par}l", f"nl{par}")
        for og in range(OG):
            bk = (2 * og) % 6
            bkv = (2 * og + 1) % 6
            psk = pb[bk][:, 0:TS]
            psv = pb[bkv][:, 0:TS]
            mm3(psk,
                lambda cg, og=og: a_wkh[:, cg * C + og * 128:cg * C + (og + 1) * 128],
                lambda cg, og=og: a_wkl[:, cg * C + og * 128:cg * C + (og + 1) * 128],
                rhs_pair("k", 0), rhs_pair("k", 1), CG,
                reads=("wkh", "wkl") + rk_h + rk_l, writes=(f"pb{bk}",), tag="kmm")
            mm3(psv,
                lambda cg, og=og: a_wvh[:, cg * C + og * 128:cg * C + (og + 1) * 128],
                lambda cg, og=og: a_wvl[:, cg * C + og * 128:cg * C + (og + 1) * 128],
                rhs_pair("v", 0), rhs_pair("v", 1), CG,
                reads=("wvh", "wvl") + rv_h + rv_l, writes=(f"pb{bkv}",), tag="vmm")
            eksl = a_ek[:, og * TS:(og + 1) * TS]
            ekvsl = a_ekv[:, og * TS:(og + 1) * TS]
            kek, kekv = f"ek{og}", f"ekv{og}"
            act_op(eksl, psk, AF.Exp, reads=(f"pb{bk}",), writes=(kek,), tag="ek")
            tt_op(ekvsl, eksl, psv, ALU.mult, reads=(kek, f"pb{bkv}"),
                  writes=(kekv,), tag="ekv")
            W1p = TS + 1
            asl = a_asc[:, og * W1p:(og + 1) * W1p]
            bsl = a_bsc[:, og * W1p:(og + 1) * W1p]
            ka, kbk = f"asc{og}", f"bsc{og}"
            if s == 0:
                memset_op(asl[:, 0:1], 0.0, writes=(ka,), tag="acar0")
                memset_op(bsl[:, 0:1], 0.0, writes=(kbk,), tag="bcar0")
            else:
                copy_op(asl[:, 0:1], asl[:, TS:TS + 1], reads=(ka,), writes=(ka,),
                        tag="acar")
                copy_op(bsl[:, 0:1], bsl[:, TS:TS + 1], reads=(kbk,), writes=(kbk,),
                        tag="bcar")
            scan_op(asl[:, 1:TS + 1], a_ewrep[:, og * TS:(og + 1) * TS], ekvsl,
                    asl[:, 0:1], reads=("ewrep", kekv, ka), writes=(ka,), tag="ascan")
            scan_op(bsl[:, 1:TS + 1], a_ewrep[:, og * TS:(og + 1) * TS], eksl,
                    bsl[:, 0:1], reads=("ewrep", kek, kbk), writes=(kbk,), tag="bscan")
            stt_op(asl[:, 0:TS], ekvsl, v_eu[:, og:og + 1],
                   asl[:, 0:TS], ALU.mult, ALU.add,
                   reads=(kekv, "c:eu", ka), writes=(ka,), tag="numloc")
            stt_op(bsl[:, 0:TS], eksl, v_eu[:, og:og + 1],
                   bsl[:, 0:TS], ALU.mult, ALU.add,
                   reads=(kek, "c:eu", kbk), writes=(kbk,), tag="denloc")
            if s == NSUP - 1:
                copy_op(a_tots[:, (2 * b) * OG + og:(2 * b) * OG + og + 1],
                        asl[:, TS:TS + 1], reads=(ka,), writes=("tots",), tag="tota")
                copy_op(a_tots[:, (2 * b + 1) * OG + og:(2 * b + 1) * OG + og + 1],
                        bsl[:, TS:TS + 1], reads=(kbk,), writes=("tots",), tag="totb")
        allk = tuple(f"asc{og}" for og in range(OG))
        allkb = tuple(f"bsc{og}" for og in range(OG))
        dma("scalar", f"spnd{par}", d_num.ap()[u].rearrange("p (og w) -> p og w", og=OG),
            a_asc.rearrange("p (og w) -> p og w", og=OG)[:, :, 0:TS], reads=allk,
            writes=("D:num",), new_group=True, tag="spnum")
        dma("scalar", f"spnd{par}", d_den.ap()[u].rearrange("p (og w) -> p og w", og=OG),
            a_bsc.rearrange("p (og w) -> p og w", og=OG)[:, :, 0:TS], reads=allkb,
            writes=("D:den",), tag="spden")
        dma("gpsimd", f"sp{par}", d_nh.ap()[u], a_np[par][0], reads=(f"nh{par}",),
            new_group=True,
            writes=("D:nh",), tag="spnh")
        dma("gpsimd", f"sp{par}", d_nl.ap()[u], a_np[par][1], reads=(f"nl{par}",),
            writes=("D:nl",), tag="spnl")
        dma("gpsimd", f"sp{par}", d_xrh.ap()[u], a_xrp[par][0], reads=(f"xr{par}h",),
            writes=("D:xrh",), tag="spxrh")
        dma("gpsimd", f"sp{par}", d_xrl.ap()[u], a_xrp[par][1], reads=(f"xr{par}l",),
            writes=("D:xrl",), tag="spxrl")

    p1a_A(0)
    p1a_A(1)
    for u in range(NSUP2):
        p1a_B(u)
        if u + 2 < NSUP2:
            p1a_A(u + 2)

    dma("gpsimd", "tot", d_tot.ap(), a_tots, reads=("tots",),
        writes=("D:tot",), new_group=True, tag="totst")

    def f_cc(e):
        return e.collective_compute(
            "AllGather", ALU.bypass, replica_groups=[list(range(NC))],
            ins=[d_tot.ap().opt()], outs=[d_gat.ap().opt()])
    kb.manual("gpsimd", f_cc, reads=("D:tot",), writes=("D:gat",), semname="cc",
              inc=1, tag="allgather")

    dma("gpsimd", "gat",
        a_gath.rearrange("p (j w) -> p j w", j=NC),
        d_gat.ap().rearrange("(j p) w -> p j w", p=128),
        reads=("D:gat",), writes=("gath",), new_group=True, slow=True, tag="gatld")

    memset_op(v_A, 0.0, writes=("A",), tag="A0")
    for j in range(NC - 1):
        ts_op(a_Atmp, v_ew2r, v_mask[:, j:j + 1], 1.0, ALU.mult, ALU.add,
              reads=("c:ew2r", "c:mask"), writes=("Atmp",), tag="dsel")
        tt_op(v_A, v_A, a_Atmp, ALU.mult, reads=("A", "Atmp"), writes=("A",),
              tag="Amul")
        ts_op(a_Atmp, a_gath[:, j * (2 * B) * OG:(j + 1) * (2 * B) * OG],
              v_mask[:, j:j + 1], None, ALU.mult, reads=("gath", "c:mask"),
              writes=("Atmp",), tag="Tmask")
        tt_op(v_A, v_A, a_Atmp, ALU.add, reads=("A", "Atmp"), writes=("A",),
              tag="Aadd")

    kb.barrier()
    phase_reset()
    b_num = [alloc(OG * TS) for _ in range(2)]
    b_den = [alloc(OG * TS) for _ in range(2)]
    b_x = [alloc(OG * TS) for _ in range(2)]
    b_sr = alloc(OG * TS)
    b_attn = alloc(OG * TS)
    b_xp = [alloc(OG * TS) for _ in range(2)]
    b_ewpb = alloc(OG * TS)
    b_ews = alloc(NSUP * OG)
    b_As = [[alloc(OG), alloc(OG)] for _ in range(2)]
    b_wrh = allocb(OG * C); b_wrl = allocb(OG * C)
    b_woh = allocb(OG * C); b_wol = allocb(OG * C)
    b_np = [[allocb(OG * TSH), allocb(OG * TSH)] for _ in range(2)]
    b_xrp = [[allocb(4 * TS), allocb(4 * TS)] for _ in range(2)]
    b_gp = [allocb(OG * TS), allocb(OG * TS)]

    weight_split(d_wr, b_wrh, b_wrl, "wrh", "wrl", [b_num[0], b_num[1]],
                 ["bnum0", "bnum1"], "wsp2", "wr")
    weight_split(d_wo, b_woh, b_wol, "woh", "wol", [b_num[0], b_num[1]],
                 ["bnum0", "bnum1"], "wsp2", "wo")

    for og in range(OG):
        ps = pb[og % 2][:, 0:TS]
        mm(ps, r_sd[0:1, og * 128:(og + 1) * 128], r_i256, True, True,
           reads=("c:sdrow", "c:i256"), writes=(f"pb{og % 2}",), tag="ewpmm")
        act_op(b_ewpb[:, og * TS:(og + 1) * TS], ps, AF.Exp, scale=cfg["WSC"],
               reads=(f"pb{og % 2}",), writes=("ewpb",), tag="ewpexp")
    for og in range(OG):
        ps = pb[og % 2][:, 0:NSUP]
        mm(ps, r_sd[0:1, og * 128:(og + 1) * 128], r_i8, True, True,
           reads=("c:sdrow", "c:i8"), writes=(f"pb{og % 2}",), tag="ewsmm")
        act_op(b_ews[:, og * NSUP:(og + 1) * NSUP], ps, AF.Exp,
               scale=float(TS) * cfg["WSC"], reads=(f"pb{og % 2}",), writes=("ews",),
               tag="ewsexp")
    ews_v = b_ews.rearrange("p (og s) -> p og s", og=OG)

    def p1b_Aload(u):
        b, s = divmod(u, NSUP)
        par = u % 2
        numv, denv = b_num[par], b_den[par]
        dma("sync", f"ld1b{par}", numv, d_num.ap()[u], reads=("D:num",),
            writes=(f"bnum{par}",), new_group=True, tag="ldnum")
        dma("sync", f"ld1b{par}", denv, d_den.ap()[u], reads=("D:den",),
            writes=(f"bden{par}",), tag="ldden")
        dma("sync", f"ld1b{par}", b_np[par][0], d_nh.ap()[u], reads=("D:nh",),
            writes=(f"bnh{par}",), tag="ldnh")
        dma("sync", f"ld1b{par}", b_np[par][1], d_nl.ap()[u], reads=("D:nl",),
            writes=(f"bnl{par}",), tag="ldnl")
        dma("sync", f"ld1b{par}", b_xrp[par][0], d_xrh.ap()[u], reads=("D:xrh",),
            writes=(f"bxrh{par}",), tag="ldxrh")
        dma("sync", f"ld1b{par}", b_xrp[par][1], d_xrl.ap()[u], reads=("D:xrl",),
            writes=(f"bxrl{par}",), tag="ldxrl")
        dma("sync", f"ld1b{par}", b_x[par].rearrange("p (g w) -> p g w", g=OG),
            d_x[b, :, s * TS + 1:s * TS + 1 + TS].rearrange("(g p) w -> p g w", p=128),
            reads=(), writes=(f"bx{par}",), tag="ldx")

    def p1b_Afix(u):
        b, s = divmod(u, NSUP)
        par = u % 2
        numv, denv = b_num[par], b_den[par]
        for kind in range(2):
            r = 2 * b + kind
            tt_op(b_As[par][kind], v_A[:, r * OG:(r + 1) * OG], ews_v[:, :, s],
                  ALU.mult, reads=("A", "ews"), writes=(f"As{par}{kind}",),
                  tag="Ascale")
        for og in range(OG):
            sl = slice(og * TS, (og + 1) * TS)
            stt_op(numv[:, sl], b_ewpb[:, sl], b_As[par][0][:, og:og + 1],
                   numv[:, sl], ALU.mult, ALU.add,
                   reads=("ewpb", f"As{par}0", f"bnum{par}"),
                   writes=(f"bnum{par}",), tag="fixn")
            stt_op(denv[:, sl], b_ewpb[:, sl], b_As[par][1][:, og:og + 1],
                   denv[:, sl], ALU.mult, ALU.add,
                   reads=("ewpb", f"As{par}1", f"bden{par}"),
                   writes=(f"bden{par}",), tag="fixd")
        act_op(denv, denv, AF.Ln, reads=(f"bden{par}",), writes=(f"bden{par}",),
               tag="lnden")
        act_op(denv, denv, AF.Exp, scale=-1.0, reads=(f"bden{par}",),
               writes=(f"bden{par}",), tag="rden")
        tt_op(numv, numv, denv, ALU.mult, reads=(f"bnum{par}", f"bden{par}"),
              writes=(f"bnum{par}",), tag="wkv")

    def p1b_B(u):
        b, s = divmod(u, NSUP)
        par = u % 2
        numv, xpv = b_num[par], b_xp[par]

        def rrhs(half):
            def f(cg, half=half):
                if cg < 4:
                    return b_xrp[par][half][:, cg * TS:(cg + 1) * TS]
                return b_np[par][half][:, cg * TSH + 1:cg * TSH + 1 + TS]
            return f

        rkeys = (f"bxrh{par}", f"bxrl{par}", f"bnh{par}", f"bnl{par}")
        for og in range(OG):
            bk = og % 4
            ps = pb[bk][:, 0:TS]
            mm3(ps,
                lambda cg, og=og: b_wrh[:, cg * C + og * 128:cg * C + (og + 1) * 128],
                lambda cg, og=og: b_wrl[:, cg * C + og * 128:cg * C + (og + 1) * 128],
                rrhs(0), rrhs(1), CG,
                reads=("wrh", "wrl") + rkeys, writes=(f"pb{bk}",), tag="rmm")
            act_op(b_sr[:, og * TS:(og + 1) * TS], ps, AF.Sigmoid,
                   reads=(f"pb{bk}",), writes=("sr",), tag="sr")
            sl = slice(og * TS, (og + 1) * TS)
            tt_op(numv[:, sl], numv[:, sl], b_sr[:, sl], ALU.mult,
                  reads=(f"bnum{par}", "sr"), writes=(f"bnum{par}",), tag="gated")
            split(b_gp[0][:, sl], b_gp[1][:, sl], numv[:, sl], f"bnum{par}",
                  "gph", "gpl", tag="gsp")
        for dg in range(DG):
            bk = 4 + dg % 2
            ps = pb[bk][:, 0:TS]
            mm3(ps,
                lambda og, dg=dg: b_woh[:, og * C + dg * 128:og * C + (dg + 1) * 128],
                lambda og, dg=dg: b_wol[:, og * C + dg * 128:og * C + (dg + 1) * 128],
                lambda og: b_gp[0][:, og * TS:(og + 1) * TS],
                lambda og: b_gp[1][:, og * TS:(og + 1) * TS], OG,
                reads=("woh", "wol", "gph", "gpl"), writes=(f"pb{bk}",), tag="womm")
            act_op(b_attn[:, dg * TS:(dg + 1) * TS], ps, AF.Copy,
                   reads=(f"pb{bk}",), writes=("attn",), tag="attnev")
        mrep, rrep = ln_stats(b_attn, TS, xpv, f"bxp{par}", "attn", "l2",
                              packed=True)
        at3 = b_attn.rearrange("p (g w) -> p g w", g=OG)
        tt_op(at3, at3, mrep.unsqueeze(1).broadcast_to([128, OG, TS]), ALU.subtract,
              reads=("attn", "pb6"), writes=("attn",), tag="pcent")
        tt_op(at3, at3, rrep.unsqueeze(1).broadcast_to([128, OG, TS]), ALU.mult,
              reads=("attn", "pb7"), writes=("attn",), tag="pmul")
        tt_op(xpv, b_attn, b_x[par], ALU.add,
              reads=("attn", f"bx{par}"), writes=(f"bxp{par}",), tag="resid")
        dma("scalar", f"spx{par}", d_xp.ap()[u], xpv, reads=(f"bxp{par}",),
            writes=("D:xp",), new_group=True, tag="spxp")

    p1b_Aload(0)
    p1b_Aload(1)
    p1b_Afix(0)
    for u in range(NSUP2):
        p1b_B(u)
        if u + 1 < NSUP2:
            p1b_Afix(u + 1)
        if u + 2 < NSUP2:
            p1b_Aload(u + 2)

    kb.barrier()
    phase_reset()
    c_q = alloc(GRP2 * TS)
    c_ws = alloc(2 * C)
    c_xp = [alloc(OG * TS) for _ in range(4)]
    c_n2 = alloc(OG * TS)
    c_out = alloc(OG * TS)
    c_hb = [alloc(GRP2 * TS) for _ in range(2)]
    c_sh = alloc(GRP2 * TS)
    c_rows2 = alloc(2 * TS)
    c_w1 = [allocb(2 * C) for _ in range(2)]
    c_w2s = [allocb(2 * C) for _ in range(2)]
    c_actp = [[allocb(FG * TS), allocb(FG * TS)] for _ in range(2)]
    c_n2p = [[allocb(OG * TS), allocb(OG * TS)] for _ in range(2)]

    def p2_Pload(u):
        par = u % 4
        dma("sync", f"xpin{par}", c_xp[par], d_xp.ap()[u], reads=("D:xp",),
            writes=(f"cxp{par}",), new_group=True, tag="ldxp")

    def p2_Pstats(u):
        par = u % 4
        mem = u % 2
        xpv = c_xp[par]
        xkey = f"cxp{par}"
        tt_op(c_n2, xpv, xpv, ALU.mult, reads=(xkey,), writes=("n2f",),
              tag="l3sq")
        rsum = pb[4][0:1, mem * TS:(mem + 1) * TS]
        rsq = pb[5][0:1, mem * TS:(mem + 1) * TS]
        for cg in range(CG):
            mm(rsum, v_onesP, xpv[:, cg * TS:(cg + 1) * TS], cg == 0,
               cg == CG - 1, reads=("c:onesP", xkey), writes=("pb4",),
               tag="l3summ")
        for cg in range(CG):
            mm(rsq, v_onesP, c_n2[:, cg * TS:(cg + 1) * TS], cg == 0,
               cg == CG - 1, reads=("c:onesP", "n2f"), writes=("pb5",),
               tag="l3sqmm")
        if mem == 0:
            m_row = r_rows[0:1, 0:TS]
            s_row = r_rows[0:1, 2 * TSH:2 * TSH + TS]
            mkey, skey = "rows_m", "rows_s"
        else:
            m_row = c_rows2[0:1, 0:TS]
            s_row = c_rows2[0:1, TS:2 * TS]
            mkey, skey = "r2m", "r2s"
        v_row = r_rows[0:1, TSH:TSH + TS]
        ts_op(m_row, rsum, invC, None, ALU.mult, reads=("pb4",), writes=(mkey,),
              tag="l3m")
        tt_op(v_row, m_row, m_row, ALU.mult, reads=(mkey,), writes=("rows_v",),
              tag="l3msq")
        stt_op(v_row, rsq, invC, v_row, ALU.mult, ALU.subtract,
               reads=("pb5", "rows_v"), writes=("rows_v",), tag="l3var")
        act_op(v_row, v_row, AF.Ln, bias=v_eps[0:1, 0:1],
               reads=("rows_v", "c:eps"), writes=("rows_v",), tag="l3lnv")
        act_op(s_row, v_row, AF.Exp, scale=-0.5, reads=("rows_v",),
               writes=(skey,), tag="l3rstd")
        mm(pb[6][:, mem * TS:(mem + 1) * TS], r_ones1, m_row, mem == 0,
           mem == 1, reads=("c:ones1", mkey), writes=("pb6",), tag="l3mrep")
        mm(pb[7][:, mem * TS:(mem + 1) * TS], r_ones1, s_row, mem == 0,
           mem == 1, reads=("c:ones1", skey), writes=("pb7",), tag="l3rrep")

    def p2_Papply(u):
        par = u % 4
        mem = u % 2
        xpv = c_xp[par]
        xkey = f"cxp{par}"
        mrep = pb[6][:, mem * TS:(mem + 1) * TS]
        rrep = pb[7][:, mem * TS:(mem + 1) * TS]
        n23 = c_n2.rearrange("p (g w) -> p g w", g=OG)
        tt_op(n23, xpv.rearrange("p (g w) -> p g w", g=OG),
              mrep.unsqueeze(1).broadcast_to([128, OG, TS]), ALU.subtract,
              reads=(xkey, "pb6"), writes=("n2f",), tag="n2c")
        tt_op(n23, n23, rrep.unsqueeze(1).broadcast_to([128, OG, TS]), ALU.mult,
              reads=("n2f", "pb7"), writes=("n2f",), tag="n2m")
        split(c_n2p[mem][0], c_n2p[mem][1], c_n2, "n2f", f"n2h{mem}",
              f"n2l{mem}", tag="n2sp")

    def p2_FG(u):
        for grp in range(NGRP2):
            for fgl in range(GRP2):
                fg = grp * GRP2 + fgl
                wpar = fg % 2
                wpb = c_w1[wpar]
                if u == 0:
                    dma("sync", f"w1in{wpar}",
                        c_ws[:, 0:C].rearrange("p (cg f) -> p cg f", cg=CG),
                        d_w1[:, fg * 128:(fg + 1) * 128].rearrange(
                            "(cg p) f -> p cg f", p=128),
                        reads=("ws",), writes=("ws",), new_group=True, tag="w1ldf")
                    split(wpb[:, 0:C], wpb[:, C:2 * C], c_ws[:, 0:C], "ws",
                          f"w1p{wpar}", f"w1p{wpar}", tag="w1sp")
                    dma("gpsimd", "wst", d_w1s.ap()[fg], wpb,
                        reads=(f"w1p{wpar}",), writes=("D:w1s",),
                        new_group=True, tag="w1st")
                else:
                    dma("sync", f"w1in{wpar}", wpb, d_w1s.ap()[fg],
                        reads=("D:w1s",), writes=(f"w1p{wpar}",),
                        new_group=True, tag="w1ldp")
                hbk = fg % 6
                first = True
                for mem in range(2):
                    ps = pb[hbk][:, mem * TS:(mem + 1) * TS]
                    for pi, (lo_w, lo_r) in enumerate(((0, 0), (C, 0), (0, 1))):
                        for cg in range(CG):
                            mm(ps, wpb[:, lo_w + cg * 128:lo_w + (cg + 1) * 128],
                               c_n2p[mem][lo_r][:, cg * TS:(cg + 1) * TS],
                               first, mem == 1 and pi == 2 and cg == CG - 1,
                               reads=(f"w1p{wpar}", f"n2h{mem}", f"n2l{mem}"),
                               writes=(f"pb{hbk}",), tag="w1mm")
                            first = False
                for mem in range(2):
                    act_op(c_hb[mem][:, fgl * TS:(fgl + 1) * TS],
                           pb[hbk][:, mem * TS:(mem + 1) * TS], AF.Identity,
                           bias=v_b1[:, fg:fg + 1], reads=(f"pb{hbk}", "c:b1"),
                           writes=(f"hb{mem}f{fgl}",), tag="hbev")
            Ec = v_E[:, grp * GRP2:(grp + 1) * GRP2]
            Fc = v_F[:, grp * GRP2:(grp + 1) * GRP2]
            qb = (c_q, c_out)
            qk = ("q", "cout")
            shb = (c_sh, c_ws)
            shk = ("sh", "ws")
            for mem in range(2):
                hball = tuple(f"hb{mem}f{i}" for i in range(GRP2))
                ts_op(shb[mem], c_hb[mem], P1H, None, ALU.mult,
                      reads=hball, writes=(shk[mem],), tag="hh1")
                stt_op(shb[mem], c_hb[mem], P1L, shb[mem], ALU.mult, ALU.add,
                       reads=hball + (shk[mem],), writes=(shk[mem],), tag="hh2")
            for mem in range(2):
                act_op(qb[mem], shb[mem], AF.Sin, reads=(shk[mem],),
                       writes=(qk[mem],), tag="sinb")
            for mem in range(2):
                act_op(shb[mem], shb[mem], AF.Sin, scale=0.5, reads=(shk[mem],),
                       writes=(shk[mem],), tag="sinh2")
            for mem in range(2):
                act_op(shb[mem], shb[mem], AF.Square, reads=(shk[mem],),
                       writes=(shk[mem],), tag="shsq")
            for mem in range(2):
                act_op(shb[mem], shb[mem], AF.Copy, scale=-2.0, bias=1.0,
                       reads=(shk[mem],), writes=(shk[mem],), tag="cosb")
            for mem in range(2):
                for fgl in range(GRP2):
                    sl = slice(fgl * TS, (fgl + 1) * TS)
                    stt_op(qb[mem][:, sl], qb[mem][:, sl], Fc[:, fgl:fgl + 1],
                           c_hb[mem][:, sl], ALU.mult, ALU.add,
                           reads=(qk[mem], "c:F", f"hb{mem}f{fgl}"),
                           writes=(qk[mem],), tag="fsb")
            for mem in range(2):
                for fgl in range(GRP2):
                    sl = slice(fgl * TS, (fgl + 1) * TS)
                    stt_op(qb[mem][:, sl], shb[mem][:, sl], Ec[:, fgl:fgl + 1],
                           qb[mem][:, sl], ALU.mult, ALU.add,
                           reads=(qk[mem], "c:E", shk[mem]),
                           writes=(qk[mem],), tag="eq")
            for mem in range(2):
                akey = f"act{mem}g{grp}"
                a_h = c_actp[mem][0][:, grp * GRP2 * TS:(grp + 1) * GRP2 * TS]
                a_l = c_actp[mem][1][:, grp * GRP2 * TS:(grp + 1) * GRP2 * TS]
                act_op(a_h, qb[mem], AF.Relu, reads=(qk[mem],),
                       writes=(akey + "h",), tag="asphi")
                stt_op(a_l, qb[mem], 0.0, a_h, ALU.max, ALU.subtract,
                       reads=(qk[mem], akey + "h"), writes=(akey + "l",),
                       tag="asplo")

    def p2_DG(u):
        for fgl in range(FG):
            wpar = fgl % 2
            wpb = c_w2s[wpar]
            if u == 0:
                dma("sync", f"w2in{wpar}",
                    c_ws[:, 0:C], d_w2[fgl * 128:(fgl + 1) * 128, :].rearrange(
                        "(o p) d -> p (o d)", p=128),
                    reads=("ws",), writes=("ws",), new_group=True, tag="w2ldf")
                split(wpb[:, 0:C], wpb[:, C:2 * C], c_ws[:, 0:C], "ws",
                      f"w2p{wpar}", f"w2p{wpar}", tag="w2sp")
                dma("gpsimd", "wst", d_w2s.ap()[fgl], wpb,
                    reads=(f"w2p{wpar}",), writes=("D:w2s",),
                    new_group=True, tag="w2st")
            else:
                dma("sync", f"w2in{wpar}", wpb, d_w2s.ap()[fgl],
                    reads=("D:w2s",), writes=(f"w2p{wpar}",),
                    new_group=True, tag="w2ldp")
            for mem in range(2):
                akey = f"act{mem}g{fgl // GRP2}"
                ah = c_actp[mem][0][:, fgl * TS:(fgl + 1) * TS]
                al = c_actp[mem][1][:, fgl * TS:(fgl + 1) * TS]
                for dg in range(DG):
                    base = dg * 128
                    for pi, (off, ra) in enumerate(((0, ah), (C, ah), (0, al))):
                        mm(pb[dg][:, mem * TS:(mem + 1) * TS],
                           wpb[:, off + base:off + base + 128], ra,
                           fgl == 0 and mem == 0 and pi == 0,
                           fgl == FG - 1 and mem == 1 and pi == 2,
                           reads=(f"w2p{wpar}", akey + "h", akey + "l"),
                           writes=(f"pb{dg}",), tag="w2mm")

    def p2_Lpair(u):
        obuf = (c_out, c_q)
        okey = ("cout", "q")
        for mem in range(2):
            for dg in range(DG):
                ts_op(obuf[mem][:, dg * TS:(dg + 1) * TS],
                      pb[dg][:, mem * TS:(mem + 1) * TS],
                      v_b2[:, dg:dg + 1], None, ALU.add, reads=(f"pb{dg}", "c:b2"),
                      writes=(okey[mem],), tag="outev")
        for mem in range(2):
            um = u + mem
            par = um % 4
            b, s = divmod(um, NSUP)
            xkey = f"cxp{par}"
            ov = obuf[mem]
            ok = okey[mem]
            mrep, rrep = ln_stats(ov, TS, c_ws, "ws", ok, "l4", packed4=True)
            o3 = ov.rearrange("p (g w) -> p g w", g=OG)
            tt_op(o3, o3, mrep.unsqueeze(1).broadcast_to([128, OG, TS]),
                  ALU.subtract, reads=(ok, "pb4"), writes=(ok,), tag="fcent")
            tt_op(o3, o3, rrep.unsqueeze(1).broadcast_to([128, OG, TS]), ALU.mult,
                  reads=(ok, "pb5"), writes=(ok,), tag="fmul")
            tt_op(ov, ov, c_xp[par], ALU.add,
                  reads=(ok, xkey), writes=(ok,), tag="fresid")
            dma("gpsimd", "outst",
                d_out[b, :, s * TS:(s + 1) * TS].rearrange("(g p) w -> p g w", p=128),
                ov.rearrange("p (g w) -> p g w", g=OG), reads=(ok,),
                writes=("D:out",), new_group=True, tag="outst")

    NP2 = NSUP2 // 2
    p2_Pload(0)
    p2_Pload(1)
    p2_Pstats(0)
    p2_Pstats(1)
    p2_Papply(0)
    p2_Papply(1)
    for p in range(NP2):
        u = 2 * p
        if p > 0:
            p2_DG(u - 2)
            p2_Lpair(u - 2)
        if u + 2 < NSUP2:
            p2_Pload(u + 2)
            p2_Pload(u + 3)
            p2_Pstats(u + 2)
            p2_Pstats(u + 3)
        p2_FG(u)
        if u + 2 < NSUP2:
            p2_Papply(u + 2)
            p2_Papply(u + 3)
    p2_DG(NSUP2 - 2)
    p2_Lpair(NSUP2 - 2)

    kb.barrier()

    kb.resolve()
    blk = es.enter_context(nc.Block())
    with es:
        for name in ENGS:
            getattr(blk, name)(lambda eng, name=name: kb.replay(name, eng))
    return nc


def _fm_vec(v, groups):
    return np.ascontiguousarray(np.asarray(v, np.float32).reshape(groups, 128).T)


def stage_inputs(cfg, inputs):
    B, T, C, DFF, NC, TLOC = (cfg["B"], cfg["T"], cfg["C"], cfg["DFF"],
                              cfg["NC"], cfg["TLOC"])
    x = np.asarray(inputs["L0_attn_x"], np.float32)
    xt = np.transpose(x, (0, 2, 1))
    xpad = np.zeros((B, C, T + 2), np.float32)
    xpad[:, :, 1:T + 1] = xt
    shared = {
        "wkT": np.ascontiguousarray(np.asarray(inputs["L0_attn_Wk"], np.float32).T),
        "wvT": np.ascontiguousarray(np.asarray(inputs["L0_attn_Wv"], np.float32).T),
        "wrT": np.ascontiguousarray(np.asarray(inputs["L0_attn_Wr"], np.float32).T),
        "woT": np.ascontiguousarray(np.asarray(inputs["L0_attn_Wo"], np.float32).T),
        "w1": np.ascontiguousarray(np.asarray(inputs["L0_ffn_W1"], np.float32)),
        "w2": np.ascontiguousarray(np.asarray(inputs["L0_ffn_W2"], np.float32)),
        "sd_fm": _fm_vec(inputs["L0_attn_spatial_decay"], C // 128),
        "sf_fm": _fm_vec(inputs["L0_attn_spatial_first"], C // 128),
        "mixk_fm": _fm_vec(inputs["L0_attn_mix_k"], C // 128),
        "mixv_fm": _fm_vec(inputs["L0_attn_mix_v"], C // 128),
        "mixr_fm": _fm_vec(inputs["L0_attn_mix_r"], C // 128),
        "b1_fm": _fm_vec(inputs["L0_ffn_b1"], DFF // 128),
        "theta_fm": _fm_vec(inputs["L0_ffn_quantum_phases"], DFF // 128),
        "qwr_fm": _fm_vec(inputs["L0_ffn_quantum_weights_real"], DFF // 128),
        "qwi_fm": _fm_vec(inputs["L0_ffn_quantum_weights_imag"], DFF // 128),
        "b2_fm": _fm_vec(inputs["L0_ffn_b2"], C // 128),
        "sd_row": np.ascontiguousarray(
            np.asarray(inputs["L0_attn_spatial_decay"], np.float32).reshape(1, C)),
        "iota256": np.arange(cfg["TS"], dtype=np.float32).reshape(1, -1),
        "iota8": np.arange(cfg["NSUP"], dtype=np.float32).reshape(1, -1),
        "ones1": np.ones((1, 128), np.float32),
        "invC_row": np.full((1, 128), 1.0 / C, np.float32),
        "onesP": np.ones((128, 1), np.float32),
    }
    in_maps = []
    for i in range(NC):
        m = dict(shared)
        m["x_fm"] = np.ascontiguousarray(xpad[:, :, i * TLOC:(i + 1) * TLOC + 2])
        mask = np.zeros((128, 8), np.float32)
        mask[:, :i] = 1.0
        m["mask"] = mask
        in_maps.append(m)
    return in_maps


def assemble_output(cfg, results):
    outs = [np.transpose(np.asarray(r["out_fm"]), (0, 2, 1)) for r in results]
    return np.ascontiguousarray(np.concatenate(outs, axis=1))


def kernel(**inputs):
    x = np.asarray(inputs["L0_attn_x"])
    B, T, C = x.shape
    DFF = np.asarray(inputs["L0_ffn_W1"]).shape[1]
    cfg = make_cfg(B, T, C, DFF)
    nc_prog = build_program(cfg)
    in_maps = stage_inputs(cfg, inputs)
    res = run_bass_kernel_spmd(nc_prog, in_maps, core_ids=list(range(cfg["NC"])))
    return assemble_output(cfg, res.results)


# version 26
# speedup vs baseline: 1.0325x; 1.0070x over previous
import numpy as np
from contextlib import ExitStack

import concourse.bass as bass
import concourse.mybir as mybir
from concourse.bass_utils import run_bass_kernel_spmd

F32 = mybir.dt.float32
BF16 = mybir.dt.bfloat16
AF = mybir.ActivationFunctionType
ALU = mybir.AluOpType

ENGS = ("tensor", "vector", "scalar", "sync", "gpsimd")


def make_cfg(B, T, C, DFF, ncores=8, TS=256):
    TLOC = T // ncores
    cfg = dict(B=B, T=T, C=C, DFF=DFF, NC=ncores, TS=TS, TLOC=TLOC,
               NSUP=TLOC // TS, OG=C // 128, CG=C // 128, FG=DFF // 128,
               DG=C // 128, WSC=1.0 / T)
    assert TLOC % TS == 0 and C % 128 == 0 and DFF % 128 == 0
    return cfg


class _Node:
    def __init__(self, eng, fn, reads, writes, is_dma=False, slot=None, tag=""):
        self.eng = eng
        self.fn = fn
        self.reads = reads
        self.writes = writes
        self.deps = []
        self.marked = False
        self.semname = None
        self.semval = None
        self.clock = None
        self.waits = ()
        self.is_dma = is_dma
        self.slot = slot
        self.tag = tag
        self.new_group = False
        self.manual_sem = None
        self.issue_wait = None
        self.barrier_targets = None


class _Slot:
    def __init__(self, name):
        self.name = name
        self.count = 0
        self.open_group = []
        self.last_group_end = None


class KB:
    def __init__(self, nc, es):
        self.nc = nc
        self.nodes = []
        self.last_w = {}
        self.readers = {}
        self.sems = {}
        self.es = es
        for e in ENGS:
            self.sems["e_" + e] = es.enter_context(nc.semaphore("se_" + e))
        self.slots = {}

    def slot(self, name):
        if name not in self.slots:
            self.slots[name] = _Slot(name)
            self.sems["s_" + name] = self.es.enter_context(
                self.nc.semaphore("ss_" + name))
        return self.slots[name]

    def _dep(self, node):
        deps = set()
        for k in node.reads:
            w = self.last_w.get(k)
            if w is not None:
                deps.add(w)
        for k in node.writes:
            w = self.last_w.get(k)
            if w is not None:
                deps.add(w)
            for r in self.readers.get(k, ()):
                deps.add(r)
        node.deps = [d for d in deps if d is not node]
        for k in node.reads:
            self.readers.setdefault(k, []).append(node)
        for k in node.writes:
            self.last_w[k] = node
            self.readers[k] = []

    def op(self, eng, fn, reads=(), writes=(), tag=""):
        n = _Node(eng, fn, tuple(reads), tuple(writes), tag=tag)
        self._dep(n)
        self.nodes.append(n)
        return n

    def dma(self, eng, slot_name, fn, reads=(), writes=(), new_group=False, tag=""):
        sl = self.slot(slot_name)
        n = _Node(eng, fn, tuple(reads), tuple(writes), is_dma=True, slot=sl, tag=tag)
        n.new_group = new_group
        self._dep(n)
        self.nodes.append(n)
        return n

    def manual(self, eng, fn, reads=(), writes=(), semname=None, inc=1, tag=""):
        n = _Node(eng, fn, tuple(reads), tuple(writes), tag=tag)
        n.manual_sem = (semname, inc)
        self._dep(n)
        self.nodes.append(n)
        return n

    def add_sem(self, name):
        if name not in self.sems:
            self.sems[name] = self.es.enter_context(self.nc.semaphore("sx_" + name))
        return self.sems[name]

    def barrier(self):
        n = _Node("barrier", None, (), (), tag="barrier")
        self.nodes.append(n)
        self.last_w = {k: v for k, v in self.last_w.items() if k.startswith("D:")}
        self.readers = {k: v for k, v in self.readers.items() if k.startswith("D:")}
        return n

    def resolve(self):
        def implicit(d, n):
            return (d.eng == "tensor" and n.eng == "tensor" and not d.is_dma
                    and d.manual_sem is None)

        last_per_eng = {}
        for n in self.nodes:
            if n.eng == "barrier":
                for e, ln in last_per_eng.items():
                    if not ln.is_dma and ln.manual_sem is None:
                        ln.marked = True
                continue
            for d in n.deps:
                if not d.is_dma and d.manual_sem is None and not implicit(d, n):
                    d.marked = True
            last_per_eng[n.eng] = n
        for ln in last_per_eng.values():
            if not ln.is_dma and ln.manual_sem is None:
                ln.marked = True

        nincs = {e: 0 for e in ENGS}
        eclock = {e: {} for e in ENGS}

        def merge(dst, src):
            for k, v in src.items():
                if dst.get(k, -1) < v:
                    dst[k] = v

        def close_slot(sl):
            if sl.open_group:
                v = 16 * sl.count
                for g in sl.open_group:
                    g.semval = v
                    g.clock = dict(g.clock or {})
                    g.clock["s_" + sl.name] = v
                sl.last_group_end = sl.open_group[-1]
                sl.open_group = []

        for n in self.nodes:
            if n.eng == "barrier":
                for sl in self.slots.values():
                    close_slot(sl)
                mc = {}
                for e in ENGS:
                    merge(mc, eclock[e])
                    if nincs[e] > 0:
                        mc["e_" + e] = max(mc.get("e_" + e, 0), nincs[e])
                for name, sl in self.slots.items():
                    if sl.count > 0:
                        mc["s_" + name] = max(mc.get("s_" + name, 0), 16 * sl.count)
                n.barrier_targets = dict(mc)
                for e in ENGS:
                    eclock[e] = dict(mc)
                continue
            e = n.eng
            if n.is_dma:
                sl = n.slot
                n.semname = "s_" + sl.name
                if n.new_group:
                    close_slot(sl)
                    if sl.count > 0:
                        n.issue_wait = ("s_" + sl.name, 16 * sl.count,
                                        sl.last_group_end)
            elif n.manual_sem is not None:
                n.semname, n.semval = n.manual_sem
            elif n.marked:
                nincs[e] += 1
                n.semname = "e_" + e
                n.semval = nincs[e]
            waits = []
            clk = eclock[e]
            for d in n.deps:
                if implicit(d, n) and not d.marked:
                    continue
                if d.is_dma and d.semval is None:
                    close_slot(d.slot)
                if implicit(d, n):
                    continue
                assert d.semname is not None and d.semval is not None, (
                    f"unsynced dep: {d.tag} <- {n.tag}")
                if clk.get(d.semname, 0) >= d.semval:
                    continue
                waits.append((d.semname, d.semval))
                merge(clk, d.clock or {})
                clk[d.semname] = max(clk.get(d.semname, 0), d.semval)
            if n.issue_wait is not None:
                sname, v, ref = n.issue_wait
                if clk.get(sname, 0) < v:
                    waits.append((sname, v))
                    if ref is not None:
                        merge(clk, ref.clock or {})
                    clk[sname] = max(clk.get(sname, 0), v)
            n.waits = tuple(waits)
            n.clock = dict(clk)
            if n.is_dma:
                sl = n.slot
                sl.count += 1
                sl.open_group.append(n)
            elif n.semname is not None and n.semval is not None:
                n.clock[n.semname] = n.semval
        for sl in self.slots.values():
            close_slot(sl)

    def replay(self, engine_name, eng):
        for n in self.nodes:
            if n.eng == "barrier":
                for sname, v in sorted(n.barrier_targets.items()):
                    eng.wait_ge(self.sems[sname], v)
                continue
            if n.eng != engine_name:
                continue
            for sname, v in n.waits:
                eng.wait_ge(self.sems[sname], v)
            instr = n.fn(eng)
            if instr is None:
                continue
            if n.is_dma:
                instr.then_inc(self.sems[n.semname], 16)
            elif n.manual_sem is not None:
                instr.then_inc(self.sems[n.semname], n.manual_sem[1])
            elif n.marked:
                instr.then_inc(self.sems[n.semname], 1)


def build_program(cfg):
    B, T, C, DFF = cfg["B"], cfg["T"], cfg["C"], cfg["DFF"]
    TS, TLOC, NSUP = cfg["TS"], cfg["TLOC"], cfg["NSUP"]
    OG, CG, FG, DG = cfg["OG"], cfg["CG"], cfg["FG"], cfg["DG"]
    NC = cfg["NC"]
    TSH = TS + 2
    NSUP2 = B * NSUP
    GRP2 = 8
    NGRP2 = FG // GRP2
    invC = 1.0 / C
    eps = 1e-5

    nc = bass.Bass("TRN2", target_bir_lowering=False, debug=False)

    d_x = nc.dram_tensor("x_fm", [B, C, TLOC + 2], F32, kind="ExternalInput").ap()
    d_wk = nc.dram_tensor("wkT", [C, C], F32, kind="ExternalInput").ap()
    d_wv = nc.dram_tensor("wvT", [C, C], F32, kind="ExternalInput").ap()
    d_wr = nc.dram_tensor("wrT", [C, C], F32, kind="ExternalInput").ap()
    d_wo = nc.dram_tensor("woT", [C, C], F32, kind="ExternalInput").ap()
    d_w1 = nc.dram_tensor("w1", [C, DFF], F32, kind="ExternalInput").ap()
    d_w2 = nc.dram_tensor("w2", [DFF, C], F32, kind="ExternalInput").ap()
    d_sdf = nc.dram_tensor("sd_fm", [128, OG], F32, kind="ExternalInput").ap()
    d_sff = nc.dram_tensor("sf_fm", [128, OG], F32, kind="ExternalInput").ap()
    d_mk = nc.dram_tensor("mixk_fm", [128, OG], F32, kind="ExternalInput").ap()
    d_mv = nc.dram_tensor("mixv_fm", [128, OG], F32, kind="ExternalInput").ap()
    d_mr = nc.dram_tensor("mixr_fm", [128, OG], F32, kind="ExternalInput").ap()
    d_b1 = nc.dram_tensor("b1_fm", [128, FG], F32, kind="ExternalInput").ap()
    d_th = nc.dram_tensor("theta_fm", [128, FG], F32, kind="ExternalInput").ap()
    d_qr = nc.dram_tensor("qwr_fm", [128, FG], F32, kind="ExternalInput").ap()
    d_qi = nc.dram_tensor("qwi_fm", [128, FG], F32, kind="ExternalInput").ap()
    d_b2 = nc.dram_tensor("b2_fm", [128, OG], F32, kind="ExternalInput").ap()
    d_sdr = nc.dram_tensor("sd_row", [1, C], F32, kind="ExternalInput").ap()
    d_i256 = nc.dram_tensor("iota256", [1, TS], F32, kind="ExternalInput").ap()
    d_i8 = nc.dram_tensor("iota8", [1, NSUP], F32, kind="ExternalInput").ap()
    d_ones1 = nc.dram_tensor("ones1", [1, 128], F32, kind="ExternalInput").ap()
    d_invC = nc.dram_tensor("invC_row", [1, 128], F32, kind="ExternalInput").ap()
    d_onesP = nc.dram_tensor("onesP", [128, 1], F32, kind="ExternalInput").ap()
    d_mask = nc.dram_tensor("mask", [128, 8], F32, kind="ExternalInput").ap()
    d_out = nc.dram_tensor("out_fm", [B, C, TLOC], F32, kind="ExternalOutput").ap()
    d_num = nc.dram_tensor("sp_num", [NSUP2, 128, OG * TS], F32)
    d_den = nc.dram_tensor("sp_den", [NSUP2, 128, OG * TS], F32)
    d_nh = nc.dram_tensor("sp_nh", [NSUP2, 128, OG * TSH], BF16)
    d_nl = nc.dram_tensor("sp_nl", [NSUP2, 128, OG * TSH], BF16)
    d_xrh = nc.dram_tensor("sp_xrh", [NSUP2, 128, 4 * TS], BF16)
    d_xrl = nc.dram_tensor("sp_xrl", [NSUP2, 128, 4 * TS], BF16)
    d_xp = nc.dram_tensor("sp_xp", [NSUP2, 128, OG * TS], F32)
    d_tot = nc.dram_tensor("sp_tot", [128, 2 * B * OG], F32)
    d_gat = nc.dram_tensor("sp_gat", [NC * 128, 2 * B * OG], F32)
    d_w1s = nc.dram_tensor("sp_w1s", [DFF // 128, 128, 2 * C], BF16)
    d_w2s = nc.dram_tensor("sp_w2s", [DFF // 128, 128, 2 * C], BF16)

    es = ExitStack()
    CONST_F32 = 2944
    PHF = 23100
    PHB = 53600
    arena = es.enter_context(nc.sbuf_tensor("arena", [128, CONST_F32 + PHF], F32))
    arenab = es.enter_context(nc.sbuf_tensor("arenab", [128, PHB], BF16))
    _off = [0]
    _offb = [0]

    def alloc(nelem):
        o = _off[0]
        _off[0] += nelem
        assert _off[0] <= CONST_F32 + PHF, f"arena overflow {_off[0]}"
        return arena[:, o:o + nelem]

    def allocb(nelem):
        o = _offb[0]
        _offb[0] += nelem
        assert _offb[0] <= PHB, f"arenab overflow {_offb[0]}"
        return arenab[:, o:o + nelem]

    v_sd = alloc(OG); v_sf = alloc(OG)
    v_mk = alloc(OG); v_mv = alloc(OG); v_mr = alloc(OG)
    v_omk = alloc(OG); v_omv = alloc(OG); v_omr = alloc(OG)
    v_eu = alloc(OG); v_ewcol = alloc(OG)
    v_ew2 = alloc(OG)
    v_ew2r = alloc(4 * OG)
    v_mask = alloc(8)
    v_b1 = alloc(FG); v_th = alloc(FG); v_qr = alloc(FG); v_qi = alloc(FG)
    v_sc = alloc(FG); v_cth = alloc(FG); v_aqh = alloc(FG); v_aql = alloc(FG)
    v_E = alloc(FG); v_F = alloc(FG)
    v_b2 = alloc(OG)
    v_tmp32 = alloc(FG)
    v_A = alloc(4 * OG)
    v_onesP = alloc(1)
    v_eps = alloc(1)
    r_sd = alloc(C)[0:1, :]
    r_i256 = alloc(TS)[0:1, :]
    r_i8 = alloc(NSUP)[0:1, :]
    r_ones1 = alloc(128)[0:1, :]
    r_invC = alloc(128)[0:1, :]
    r_rows = alloc(TSH * 3)
    assert _off[0] <= CONST_F32, f"const region overflow: {_off[0]}"
    _off[0] = CONST_F32

    def phase_reset():
        _off[0] = CONST_F32
        _offb[0] = 0

    kb = KB(nc, es)
    kb.add_sem("cc")
    pb = [es.enter_context(nc.psum_tensor(f"pb{i}", [128, 512], F32)) for i in range(8)]

    TAG = [0]

    def t(s):
        TAG[0] += 1
        return f"{s}#{TAG[0]}"

    def ts_op(out, in0, s1, s2, op0, op1=None, reads=(), writes=(), tag=""):
        def f(e, out=out, in0=in0, s1=s1, s2=s2, op0=op0, op1=op1):
            if op1 is None:
                return e.tensor_scalar(out=out, in0=in0, scalar1=s1, scalar2=None,
                                       op0=op0)
            return e.tensor_scalar(out=out, in0=in0, scalar1=s1, scalar2=s2,
                                   op0=op0, op1=op1)
        return kb.op("vector", f, reads, writes, tag=t(tag or "ts"))

    def tt_op(out, in0, in1, op, reads=(), writes=(), tag=""):
        def f(e, out=out, in0=in0, in1=in1, op=op):
            return e.tensor_tensor(out=out, in0=in0, in1=in1, op=op)
        return kb.op("vector", f, reads, writes, tag=t(tag or "tt"))

    def stt_op(out, in0, sc, in1, op0, op1, reads=(), writes=(), tag=""):
        def f(e, out=out, in0=in0, sc=sc, in1=in1, op0=op0, op1=op1):
            return e.scalar_tensor_tensor(out=out, in0=in0, scalar=sc, in1=in1,
                                          op0=op0, op1=op1)
        return kb.op("vector", f, reads, writes, tag=t(tag or "stt"))

    def act_op(out, in_, func, bias=None, scale=None, reads=(), writes=(), tag=""):
        def f(e, out=out, in_=in_, func=func, bias=bias, scale=scale):
            kw = {}
            if bias is not None:
                kw["bias"] = bias
            if scale is not None:
                kw["scale"] = scale
            return e.activation(out=out, in_=in_, func=func, **kw)
        return kb.op("scalar", f, reads, writes, tag=t(tag or "act"))

    def copy_op(out, in_, reads=(), writes=(), tag=""):
        def f(e, out=out, in_=in_):
            return e.tensor_copy(out=out, in_=in_)
        return kb.op("vector", f, reads, writes, tag=t(tag or "cp"))

    def memset_op(ap, val, writes=(), tag=""):
        def f(e, ap=ap, val=val):
            return e.memset(ap, val)
        return kb.op("vector", f, (), writes, tag=t(tag or "ms"))

    def mm(out, lhsT, rhs, start, stop, reads=(), writes=(), tag=""):
        def f(e, out=out, lhsT=lhsT, rhs=rhs, start=start, stop=stop):
            return e.matmul(out, lhsT, rhs, start=start, stop=stop)
        return kb.op("tensor", f, reads, writes, tag=t(tag or "mm"))

    def mm3(ps, lhs_h, lhs_l, rhs_h, rhs_l, nk, reads, writes, tag=""):
        passes = ((lhs_h, rhs_h), (lhs_l, rhs_h), (lhs_h, rhs_l))
        for pi, (lf, rf) in enumerate(passes):
            for i in range(nk):
                mm(ps, lf(i), rf(i), pi == 0 and i == 0,
                   pi == 2 and i == nk - 1, reads=reads, writes=writes, tag=tag)

    def scan_op(out, data0, data1, initial, reads=(), writes=(), tag=""):
        def f(e, out=out, d0=data0, d1=data1, init=initial):
            return e.tensor_tensor_scan(out=out, data0=d0, data1=d1,
                                        initial=init, op0=ALU.mult, op1=ALU.add)
        return kb.op("vector", f, reads, writes, tag=t(tag or "scan"))

    def dma(eng, slot, out, in_, reads=(), writes=(), new_group=False, slow=False,
            tag=""):
        def f(e, out=out, in_=in_, slow=slow):
            if slow:
                return e.dma_start(out=out, in_=in_, allow_slow_non_contiguous=True)
            return e.dma_start(out=out, in_=in_)
        return kb.dma(eng, slot, f, reads, writes, new_group=new_group,
                      tag=t(tag or ("dma_" + slot)))

    def split(hi, lo, src, skey, hkey, lkey, tag=""):
        act_op(hi, src, AF.Copy, reads=(skey,), writes=(hkey,), tag=tag + "hi")
        tt_op(lo, src, hi, ALU.subtract, reads=(skey, hkey), writes=(lkey,),
              tag=tag + "lo")

    for ap_, d_, key in ((v_sd, d_sdf, "c:sd"), (v_sf, d_sff, "c:sf"),
                         (v_mk, d_mk, "c:mk"), (v_mv, d_mv, "c:mv"),
                         (v_mr, d_mr, "c:mr"), (v_b1, d_b1, "c:b1"),
                         (v_th, d_th, "c:th"), (v_qr, d_qr, "c:qr"),
                         (v_qi, d_qi, "c:qi"), (v_b2, d_b2, "c:b2"),
                         (r_sd, d_sdr, "c:sdrow"), (r_i256, d_i256, "c:i256"),
                         (r_i8, d_i8, "c:i8"), (r_ones1, d_ones1, "c:ones1"),
                         (r_invC, d_invC, "c:invC"), (v_onesP, d_onesP, "c:onesP"),
                         (v_mask, d_mask, "c:mask")):
        dma("sync", "cin", ap_, d_, writes=(key,), tag="cld")

    memset_op(v_eps, eps, writes=("c:eps",), tag="epsfill")
    for v_m, v_om, km, ko in ((v_mk, v_omk, "c:mk", "c:omk"),
                              (v_mv, v_omv, "c:mv", "c:omv"),
                              (v_mr, v_omr, "c:mr", "c:omr")):
        ts_op(v_om, v_m, -1.0, 1.0, ALU.mult, ALU.add, reads=(km,), writes=(ko,),
              tag="om")
    act_op(v_eu, v_sf, AF.Exp, scale=cfg["WSC"], reads=("c:sf",), writes=("c:eu",),
           tag="eu")
    act_op(v_ewcol, v_sd, AF.Exp, scale=cfg["WSC"], reads=("c:sd",), writes=("c:ew",),
           tag="ew")
    act_op(v_ew2, v_sd, AF.Exp, scale=float(TLOC) * cfg["WSC"], reads=("c:sd",),
           writes=("c:ew2",), tag="ew2")
    ts_op(v_ew2, v_ew2, -1.0, None, ALU.add, reads=("c:ew2",), writes=("c:ew2",),
          tag="ew2m1")
    for r in range(4):
        copy_op(v_ew2r[:, r * OG:(r + 1) * OG], v_ew2, reads=("c:ew2",),
                writes=("c:ew2r",), tag="ew2rep")
    PI = float(np.float32(np.pi))
    TWO_PI = 2.0 * np.pi
    c1 = 6.28125
    c2 = float(np.float32(TWO_PI - c1))
    c3 = float(np.float32(TWO_PI - c1 - float(np.float32(TWO_PI - c1))))
    for _ in range(7):
        ts_op(v_tmp32, v_th, PI, None, ALU.is_gt, reads=("c:th",), writes=("c:g",),
              tag="rrgt")
        for cc_ in (c1, c2, c3):
            stt_op(v_th, v_tmp32, -cc_, v_th, ALU.mult, ALU.add,
                   reads=("c:g", "c:th"), writes=("c:th",), tag="rrsub")
        ts_op(v_tmp32, v_th, -PI, None, ALU.is_lt, reads=("c:th",), writes=("c:g",),
              tag="rrlt")
        for cc_ in (c1, c2, c3):
            stt_op(v_th, v_tmp32, cc_, v_th, ALU.mult, ALU.add,
                   reads=("c:g", "c:th"), writes=("c:th",), tag="rradd")
    act_op(v_sc, v_th, AF.Sin, reads=("c:th",), writes=("c:sc",), tag="sinth")
    act_op(v_cth, v_th, AF.Sin, scale=0.5, reads=("c:th",), writes=("c:cth",),
           tag="sinth2")
    tt_op(v_cth, v_cth, v_cth, ALU.mult, reads=("c:cth",), writes=("c:cth",), tag="csq")
    ts_op(v_cth, v_cth, -2.0, 1.0, ALU.mult, ALU.add, reads=("c:cth",),
          writes=("c:cth",), tag="costh")
    P1H = float(np.float32(0.1))
    P1L = float(0.1 - P1H)
    tt_op(v_aqh, v_cth, v_qr, ALU.mult, reads=("c:cth", "c:qr"), writes=("c:aqh",),
          tag="qp1")
    tt_op(v_aql, v_sc, v_qi, ALU.mult, reads=("c:sc", "c:qi"), writes=("c:aql",),
          tag="qp2")
    tt_op(v_aqh, v_aqh, v_aql, ALU.add, reads=("c:aqh", "c:aql"), writes=("c:aqh",),
          tag="qp3")
    tt_op(v_tmp32, v_aqh, v_qi, ALU.mult, reads=("c:aqh", "c:qi"), writes=("c:t3",),
          tag="qp4")
    tt_op(v_aqh, v_tmp32, v_sc, ALU.mult, reads=("c:t3", "c:sc"), writes=("c:e1",),
          tag="e1")
    ts_op(v_E, v_aqh, P1H, None, ALU.mult, reads=("c:e1",), writes=("c:E",),
          tag="Eh")
    stt_op(v_E, v_aqh, P1L, v_E, ALU.mult, ALU.add, reads=("c:e1", "c:E"),
           writes=("c:E",), tag="El")
    tt_op(v_aqh, v_tmp32, v_cth, ALU.mult, reads=("c:t3", "c:cth"), writes=("c:e1",),
          tag="f1")
    ts_op(v_F, v_aqh, P1H, None, ALU.mult, reads=("c:e1",), writes=("c:F",),
          tag="Fh")
    stt_op(v_F, v_aqh, P1L, v_F, ALU.mult, ALU.add, reads=("c:e1", "c:F"),
           writes=("c:F",), tag="Fl")

    def ln_stats(xv, width, sqv, sq_key, x_key, tagp, packed=False, packed4=False,
                 sq_vec=False):
        if sq_vec:
            tt_op(sqv[:, 0:OG * width], xv[:, 0:OG * width], xv[:, 0:OG * width],
                  ALU.mult, reads=(x_key,), writes=(sq_key,), tag=tagp + "sq")
        else:
            act_op(sqv[:, 0:OG * width], xv[:, 0:OG * width], AF.Square,
                   reads=(x_key,), writes=(sq_key,), tag=tagp + "sq")
        if packed4:
            assert width <= 256
            rsum = pb[4][0:1, 0:width]
            rsq = pb[5][0:1, 0:width]
            ksum, ksq = "pb4", "pb5"
            mrep = pb[4][:, 256:256 + width]
            rrep = pb[5][:, 256:256 + width]
        elif packed:
            assert width <= 256
            rsum = pb[6][0:1, 0:width]
            rsq = pb[7][0:1, 0:width]
            ksum, ksq = "pb6", "pb7"
            mrep = pb[6][:, 256:256 + width]
            rrep = pb[7][:, 256:256 + width]
        else:
            rsum = pb[4][0:1, 0:width]
            rsq = pb[5][0:1, 0:width]
            ksum, ksq = "pb4", "pb5"
            mrep = pb[6][:, 0:width]
            rrep = pb[7][:, 0:width]
        for cg in range(CG):
            mm(rsum, v_onesP, xv[:, cg * width:(cg + 1) * width],
               cg == 0, cg == CG - 1, reads=("c:onesP", x_key), writes=(ksum,),
               tag=tagp + "summ")
        for cg in range(CG):
            mm(rsq, v_onesP, sqv[:, cg * width:(cg + 1) * width],
               cg == 0, cg == CG - 1, reads=("c:onesP", sq_key), writes=(ksq,),
               tag=tagp + "sqmm")
        m_row = r_rows[0:1, 0:width]
        v_row = r_rows[0:1, TSH:TSH + width]
        s_row = r_rows[0:1, 2 * TSH:2 * TSH + width]
        ts_op(m_row, rsum, invC, None, ALU.mult, reads=(ksum,), writes=("rows_m",),
              tag=tagp + "m")
        tt_op(v_row, m_row, m_row, ALU.mult, reads=("rows_m",), writes=("rows_v",),
              tag=tagp + "msq")
        stt_op(v_row, rsq, invC, v_row, ALU.mult, ALU.subtract,
               reads=(ksq, "rows_v"), writes=("rows_v",), tag=tagp + "var")
        act_op(v_row, v_row, AF.Ln, bias=v_eps[0:1, 0:1], reads=("rows_v", "c:eps"),
               writes=("rows_v",), tag=tagp + "lnv")
        act_op(s_row, v_row, AF.Exp, scale=-0.5, reads=("rows_v",), writes=("rows_s",),
               tag=tagp + "rstd")
        mm(mrep, r_ones1, m_row, True, True,
           reads=("c:ones1", "rows_m"), writes=(ksum if (packed or packed4) else "pb6",),
           tag=tagp + "mrep")
        mm(rrep, r_ones1, s_row, True, True,
           reads=("c:ones1", "rows_s"), writes=(ksq if (packed or packed4) else "pb7",),
           tag=tagp + "rrep")
        return mrep, rrep

    def weight_split(d_src, hi, lo, hkey, lkey, scr_list, scr_keys, slot, tag):
        nch = CG // 2
        for i in range(nch):
            scr = scr_list[i % len(scr_list)][:, 0:2 * C]
            skey = scr_keys[i % len(scr_keys)]
            dma("sync", slot, scr.rearrange("p (cg o) -> p cg o", cg=2),
                d_src[i * 256:(i + 1) * 256, :].rearrange("(cg p) o -> p cg o", p=128),
                reads=(), writes=(skey,), new_group=True, tag=tag + "ld")
            split(hi[:, i * 2 * C:(i + 1) * 2 * C], lo[:, i * 2 * C:(i + 1) * 2 * C],
                  scr, skey, hkey, lkey, tag=tag + "sp")

    phase_reset()
    a_x = [alloc(OG * TSH) for _ in range(2)]
    a_sq = alloc(OG * TSH)
    a_n = [alloc(OG * TSH) for _ in range(2)]
    a_d = alloc(4 * TS)
    a_mscr = alloc(4 * TS)
    a_ek = alloc(OG * TS)
    a_ekv = alloc(OG * TS)
    a_asc = alloc(OG * (TS + 1))
    a_bsc = alloc(OG * (TS + 1))
    a_ewrep = alloc(OG * TS)
    a_tots = alloc(4 * OG)
    a_gath = alloc(2 * B * NC * OG)
    a_Atmp = alloc(4 * OG)
    a_wkh = allocb(OG * C); a_wkl = allocb(OG * C)
    a_wvh = allocb(OG * C); a_wvl = allocb(OG * C)
    a_xkp = [[allocb(4 * TS), allocb(4 * TS)] for _ in range(2)]
    a_xvp = [[allocb(4 * TS), allocb(4 * TS)] for _ in range(2)]
    a_xrp = [[allocb(4 * TS), allocb(4 * TS)] for _ in range(2)]
    a_np = [[allocb(OG * TSH), allocb(OG * TSH)] for _ in range(2)]

    weight_split(d_wk, a_wkh, a_wkl, "wkh", "wkl", [a_x[0], a_x[1]], ["x0", "x1"],
                 "wsp", "wk")
    weight_split(d_wv, a_wvh, a_wvl, "wvh", "wvl", [a_x[0], a_x[1]], ["x0", "x1"],
                 "wsp", "wv")

    memset_op(a_ewrep, 1.0, writes=("ewrep",), tag="ewrep0")
    for og in range(OG):
        ts_op(a_ewrep[:, og * TS:(og + 1) * TS], a_ewrep[:, og * TS:(og + 1) * TS],
              v_ewcol[:, og:og + 1], None, ALU.mult,
              reads=("ewrep", "c:ew"), writes=("ewrep",), tag="ewrep")

    def p1a_A(u):
        b, s = divmod(u, NSUP)
        par = u % 2
        xv, nv = a_x[par], a_n[par]
        xkey, nkey = f"x{par}", f"n{par}"
        dma("sync", f"xin{par}", xv.rearrange("p (g w) -> p g w", g=OG),
            d_x[b, :, s * TS:s * TS + TSH].rearrange("(g p) w -> p g w", p=128),
            writes=(xkey,), new_group=True, tag="xload")
        mrep, rrep = ln_stats(xv, TSH, a_sq, "sq", xkey, "l1")
        nv3 = nv.rearrange("p (g w) -> p g w", g=OG)
        tt_op(nv3, xv.rearrange("p (g w) -> p g w", g=OG),
              mrep.unsqueeze(1).broadcast_to([128, OG, TSH]), ALU.subtract,
              reads=(xkey, "pb6"), writes=(nkey,), tag="ncent")
        tt_op(nv3, nv3, rrep.unsqueeze(1).broadcast_to([128, OG, TSH]), ALU.mult,
              reads=(nkey, "pb7"), writes=(nkey,), tag="nmul")
        split(a_np[par][0], a_np[par][1], nv, nkey, f"nh{par}", f"nl{par}", tag="nsp")
        for cg in range(4):
            off = cg * TSH
            sh = nv[:, off:off + TS] if cg < 2 else nv[:, off + 2:off + TS + 2]
            mid = nv[:, off + 1:off + TS + 1]
            dsl = a_d[:, cg * TS:(cg + 1) * TS]
            tt_op(dsl, sh, mid, ALU.subtract, reads=(nkey,), writes=("d",),
                  tag="dshift")
        for om_, pr, kk in ((v_omk, a_xkp[par], f"xk{par}"),
                            (v_omv, a_xvp[par], f"xv{par}"),
                            (v_omr, a_xrp[par], f"xr{par}")):
            ko = {id(v_omk): "c:omk", id(v_omv): "c:omv", id(v_omr): "c:omr"}[id(om_)]
            for cg in range(4):
                off = cg * TSH
                mid = nv[:, off + 1:off + TS + 1]
                stt_op(a_mscr[:, cg * TS:(cg + 1) * TS],
                       a_d[:, cg * TS:(cg + 1) * TS], om_[:, cg:cg + 1], mid,
                       ALU.mult, ALU.add, reads=("d", nkey, ko), writes=("mscr",),
                       tag="mix")
            split(pr[0], pr[1], a_mscr, "mscr", kk + "h", kk + "l", tag="xsp")

    def p1a_B(u):
        b, s = divmod(u, NSUP)
        par = u % 2
        nkey = f"n{par}"

        def rhs_pair(pref, half):
            def f(cg, pref=pref, half=half):
                if cg < 4:
                    src = (a_xkp if pref == "k" else a_xvp)[par][half]
                    return src[:, cg * TS:(cg + 1) * TS]
                nph = a_np[par][half]
                return nph[:, cg * TSH + 1:cg * TSH + 1 + TS]
            return f

        rk_h = (f"xk{par}h", f"nh{par}")
        rk_l = (f"xk{par}l", f"nl{par}")
        rv_h = (f"xv{par}h", f"nh{par}")
        rv_l = (f"xv{par}l", f"nl{par}")
        for og in range(OG):
            bk = (2 * og) % 6
            bkv = (2 * og + 1) % 6
            psk = pb[bk][:, 0:TS]
            psv = pb[bkv][:, 0:TS]
            mm3(psk,
                lambda cg, og=og: a_wkh[:, cg * C + og * 128:cg * C + (og + 1) * 128],
                lambda cg, og=og: a_wkl[:, cg * C + og * 128:cg * C + (og + 1) * 128],
                rhs_pair("k", 0), rhs_pair("k", 1), CG,
                reads=("wkh", "wkl") + rk_h + rk_l, writes=(f"pb{bk}",), tag="kmm")
            mm3(psv,
                lambda cg, og=og: a_wvh[:, cg * C + og * 128:cg * C + (og + 1) * 128],
                lambda cg, og=og: a_wvl[:, cg * C + og * 128:cg * C + (og + 1) * 128],
                rhs_pair("v", 0), rhs_pair("v", 1), CG,
                reads=("wvh", "wvl") + rv_h + rv_l, writes=(f"pb{bkv}",), tag="vmm")
            eksl = a_ek[:, og * TS:(og + 1) * TS]
            ekvsl = a_ekv[:, og * TS:(og + 1) * TS]
            kek, kekv = f"ek{og}", f"ekv{og}"
            act_op(eksl, psk, AF.Exp, reads=(f"pb{bk}",), writes=(kek,), tag="ek")
            tt_op(ekvsl, eksl, psv, ALU.mult, reads=(kek, f"pb{bkv}"),
                  writes=(kekv,), tag="ekv")
            W1p = TS + 1
            asl = a_asc[:, og * W1p:(og + 1) * W1p]
            bsl = a_bsc[:, og * W1p:(og + 1) * W1p]
            ka, kbk = f"asc{og}", f"bsc{og}"
            if s == 0:
                memset_op(asl[:, 0:1], 0.0, writes=(ka,), tag="acar0")
                memset_op(bsl[:, 0:1], 0.0, writes=(kbk,), tag="bcar0")
            else:
                copy_op(asl[:, 0:1], asl[:, TS:TS + 1], reads=(ka,), writes=(ka,),
                        tag="acar")
                copy_op(bsl[:, 0:1], bsl[:, TS:TS + 1], reads=(kbk,), writes=(kbk,),
                        tag="bcar")
            scan_op(asl[:, 1:TS + 1], a_ewrep[:, og * TS:(og + 1) * TS], ekvsl,
                    asl[:, 0:1], reads=("ewrep", kekv, ka), writes=(ka,), tag="ascan")
            scan_op(bsl[:, 1:TS + 1], a_ewrep[:, og * TS:(og + 1) * TS], eksl,
                    bsl[:, 0:1], reads=("ewrep", kek, kbk), writes=(kbk,), tag="bscan")
            stt_op(asl[:, 0:TS], ekvsl, v_eu[:, og:og + 1],
                   asl[:, 0:TS], ALU.mult, ALU.add,
                   reads=(kekv, "c:eu", ka), writes=(ka,), tag="numloc")
            stt_op(bsl[:, 0:TS], eksl, v_eu[:, og:og + 1],
                   bsl[:, 0:TS], ALU.mult, ALU.add,
                   reads=(kek, "c:eu", kbk), writes=(kbk,), tag="denloc")
            if s == NSUP - 1:
                copy_op(a_tots[:, (2 * b) * OG + og:(2 * b) * OG + og + 1],
                        asl[:, TS:TS + 1], reads=(ka,), writes=("tots",), tag="tota")
                copy_op(a_tots[:, (2 * b + 1) * OG + og:(2 * b + 1) * OG + og + 1],
                        bsl[:, TS:TS + 1], reads=(kbk,), writes=("tots",), tag="totb")
        allk = tuple(f"asc{og}" for og in range(OG))
        allkb = tuple(f"bsc{og}" for og in range(OG))
        dma("scalar", f"spnd{par}", d_num.ap()[u].rearrange("p (og w) -> p og w", og=OG),
            a_asc.rearrange("p (og w) -> p og w", og=OG)[:, :, 0:TS], reads=allk,
            writes=("D:num",), new_group=True, tag="spnum")
        dma("scalar", f"spnd{par}", d_den.ap()[u].rearrange("p (og w) -> p og w", og=OG),
            a_bsc.rearrange("p (og w) -> p og w", og=OG)[:, :, 0:TS], reads=allkb,
            writes=("D:den",), tag="spden")
        dma("sync", f"spb{par}", d_nh.ap()[u], a_np[par][0], reads=(f"nh{par}",),
            new_group=True,
            writes=("D:nh",), tag="spnh")
        dma("sync", f"spb{par}", d_nl.ap()[u], a_np[par][1], reads=(f"nl{par}",),
            writes=("D:nl",), tag="spnl")
        dma("sync", f"spb{par}", d_xrh.ap()[u], a_xrp[par][0], reads=(f"xr{par}h",),
            writes=("D:xrh",), tag="spxrh")
        dma("sync", f"spb{par}", d_xrl.ap()[u], a_xrp[par][1], reads=(f"xr{par}l",),
            writes=("D:xrl",), tag="spxrl")

    p1a_A(0)
    p1a_A(1)
    for u in range(NSUP2):
        p1a_B(u)
        if u + 2 < NSUP2:
            p1a_A(u + 2)

    dma("gpsimd", "tot", d_tot.ap(), a_tots, reads=("tots",),
        writes=("D:tot",), new_group=True, tag="totst")

    def f_cc(e):
        return e.collective_compute(
            "AllGather", ALU.bypass, replica_groups=[list(range(NC))],
            ins=[d_tot.ap().opt()], outs=[d_gat.ap().opt()])
    kb.manual("gpsimd", f_cc, reads=("D:tot",), writes=("D:gat",), semname="cc",
              inc=1, tag="allgather")

    dma("gpsimd", "gat",
        a_gath.rearrange("p (j w) -> p j w", j=NC),
        d_gat.ap().rearrange("(j p) w -> p j w", p=128),
        reads=("D:gat",), writes=("gath",), new_group=True, slow=True, tag="gatld")

    memset_op(v_A, 0.0, writes=("A",), tag="A0")
    for j in range(NC - 1):
        ts_op(a_Atmp, v_ew2r, v_mask[:, j:j + 1], 1.0, ALU.mult, ALU.add,
              reads=("c:ew2r", "c:mask"), writes=("Atmp",), tag="dsel")
        tt_op(v_A, v_A, a_Atmp, ALU.mult, reads=("A", "Atmp"), writes=("A",),
              tag="Amul")
        ts_op(a_Atmp, a_gath[:, j * (2 * B) * OG:(j + 1) * (2 * B) * OG],
              v_mask[:, j:j + 1], None, ALU.mult, reads=("gath", "c:mask"),
              writes=("Atmp",), tag="Tmask")
        tt_op(v_A, v_A, a_Atmp, ALU.add, reads=("A", "Atmp"), writes=("A",),
              tag="Aadd")

    kb.barrier()
    phase_reset()
    b_num = [alloc(OG * TS) for _ in range(2)]
    b_den = [alloc(OG * TS) for _ in range(2)]
    b_x = [alloc(OG * TS) for _ in range(2)]
    b_sr = alloc(OG * TS)
    b_attn = alloc(OG * TS)
    b_xp = [alloc(OG * TS) for _ in range(2)]
    b_ewpb = alloc(OG * TS)
    b_ews = alloc(NSUP * OG)
    b_As = [[alloc(OG), alloc(OG)] for _ in range(2)]
    b_wrh = allocb(OG * C); b_wrl = allocb(OG * C)
    b_woh = allocb(OG * C); b_wol = allocb(OG * C)
    b_np = [[allocb(OG * TSH), allocb(OG * TSH)] for _ in range(2)]
    b_xrp = [[allocb(4 * TS), allocb(4 * TS)] for _ in range(2)]
    b_gp = [allocb(OG * TS), allocb(OG * TS)]

    weight_split(d_wr, b_wrh, b_wrl, "wrh", "wrl", [b_num[0], b_num[1]],
                 ["bnum0", "bnum1"], "wsp2", "wr")
    weight_split(d_wo, b_woh, b_wol, "woh", "wol", [b_num[0], b_num[1]],
                 ["bnum0", "bnum1"], "wsp2", "wo")

    for og in range(OG):
        ps = pb[og % 2][:, 0:TS]
        mm(ps, r_sd[0:1, og * 128:(og + 1) * 128], r_i256, True, True,
           reads=("c:sdrow", "c:i256"), writes=(f"pb{og % 2}",), tag="ewpmm")
        act_op(b_ewpb[:, og * TS:(og + 1) * TS], ps, AF.Exp, scale=cfg["WSC"],
               reads=(f"pb{og % 2}",), writes=("ewpb",), tag="ewpexp")
    for og in range(OG):
        ps = pb[og % 2][:, 0:NSUP]
        mm(ps, r_sd[0:1, og * 128:(og + 1) * 128], r_i8, True, True,
           reads=("c:sdrow", "c:i8"), writes=(f"pb{og % 2}",), tag="ewsmm")
        act_op(b_ews[:, og * NSUP:(og + 1) * NSUP], ps, AF.Exp,
               scale=float(TS) * cfg["WSC"], reads=(f"pb{og % 2}",), writes=("ews",),
               tag="ewsexp")
    ews_v = b_ews.rearrange("p (og s) -> p og s", og=OG)

    def p1b_Aload(u):
        b, s = divmod(u, NSUP)
        par = u % 2
        numv, denv = b_num[par], b_den[par]
        dma("sync", f"ld1b{par}", numv, d_num.ap()[u], reads=("D:num",),
            writes=(f"bnum{par}",), new_group=True, tag="ldnum")
        dma("sync", f"ld1b{par}", denv, d_den.ap()[u], reads=("D:den",),
            writes=(f"bden{par}",), tag="ldden")
        dma("sync", f"ld1b{par}", b_np[par][0], d_nh.ap()[u], reads=("D:nh",),
            writes=(f"bnh{par}",), tag="ldnh")
        dma("sync", f"ld1b{par}", b_np[par][1], d_nl.ap()[u], reads=("D:nl",),
            writes=(f"bnl{par}",), tag="ldnl")
        dma("sync", f"ld1b{par}", b_xrp[par][0], d_xrh.ap()[u], reads=("D:xrh",),
            writes=(f"bxrh{par}",), tag="ldxrh")
        dma("sync", f"ld1b{par}", b_xrp[par][1], d_xrl.ap()[u], reads=("D:xrl",),
            writes=(f"bxrl{par}",), tag="ldxrl")
        dma("sync", f"ld1b{par}", b_x[par].rearrange("p (g w) -> p g w", g=OG),
            d_x[b, :, s * TS + 1:s * TS + 1 + TS].rearrange("(g p) w -> p g w", p=128),
            reads=(), writes=(f"bx{par}",), tag="ldx")

    def p1b_Afix(u):
        b, s = divmod(u, NSUP)
        par = u % 2
        numv, denv = b_num[par], b_den[par]
        for kind in range(2):
            r = 2 * b + kind
            tt_op(b_As[par][kind], v_A[:, r * OG:(r + 1) * OG], ews_v[:, :, s],
                  ALU.mult, reads=("A", "ews"), writes=(f"As{par}{kind}",),
                  tag="Ascale")
        for og in range(OG):
            sl = slice(og * TS, (og + 1) * TS)
            stt_op(numv[:, sl], b_ewpb[:, sl], b_As[par][0][:, og:og + 1],
                   numv[:, sl], ALU.mult, ALU.add,
                   reads=("ewpb", f"As{par}0", f"bnum{par}"),
                   writes=(f"bnum{par}",), tag="fixn")
            stt_op(denv[:, sl], b_ewpb[:, sl], b_As[par][1][:, og:og + 1],
                   denv[:, sl], ALU.mult, ALU.add,
                   reads=("ewpb", f"As{par}1", f"bden{par}"),
                   writes=(f"bden{par}",), tag="fixd")
        act_op(denv, denv, AF.Ln, reads=(f"bden{par}",), writes=(f"bden{par}",),
               tag="lnden")
        act_op(denv, denv, AF.Exp, scale=-1.0, reads=(f"bden{par}",),
               writes=(f"bden{par}",), tag="rden")
        tt_op(numv, numv, denv, ALU.mult, reads=(f"bnum{par}", f"bden{par}"),
              writes=(f"bnum{par}",), tag="wkv")

    def p1b_B(u):
        b, s = divmod(u, NSUP)
        par = u % 2
        numv, xpv = b_num[par], b_xp[par]

        def rrhs(half):
            def f(cg, half=half):
                if cg < 4:
                    return b_xrp[par][half][:, cg * TS:(cg + 1) * TS]
                return b_np[par][half][:, cg * TSH + 1:cg * TSH + 1 + TS]
            return f

        rkeys = (f"bxrh{par}", f"bxrl{par}", f"bnh{par}", f"bnl{par}")
        for og in range(OG):
            bk = og % 4
            ps = pb[bk][:, 0:TS]
            mm3(ps,
                lambda cg, og=og: b_wrh[:, cg * C + og * 128:cg * C + (og + 1) * 128],
                lambda cg, og=og: b_wrl[:, cg * C + og * 128:cg * C + (og + 1) * 128],
                rrhs(0), rrhs(1), CG,
                reads=("wrh", "wrl") + rkeys, writes=(f"pb{bk}",), tag="rmm")
            act_op(b_sr[:, og * TS:(og + 1) * TS], ps, AF.Sigmoid,
                   reads=(f"pb{bk}",), writes=("sr",), tag="sr")
            sl = slice(og * TS, (og + 1) * TS)
            tt_op(numv[:, sl], numv[:, sl], b_sr[:, sl], ALU.mult,
                  reads=(f"bnum{par}", "sr"), writes=(f"bnum{par}",), tag="gated")
            split(b_gp[0][:, sl], b_gp[1][:, sl], numv[:, sl], f"bnum{par}",
                  "gph", "gpl", tag="gsp")
        for dg in range(DG):
            bk = 4 + dg % 2
            ps = pb[bk][:, 0:TS]
            mm3(ps,
                lambda og, dg=dg: b_woh[:, og * C + dg * 128:og * C + (dg + 1) * 128],
                lambda og, dg=dg: b_wol[:, og * C + dg * 128:og * C + (dg + 1) * 128],
                lambda og: b_gp[0][:, og * TS:(og + 1) * TS],
                lambda og: b_gp[1][:, og * TS:(og + 1) * TS], OG,
                reads=("woh", "wol", "gph", "gpl"), writes=(f"pb{bk}",), tag="womm")
            act_op(b_attn[:, dg * TS:(dg + 1) * TS], ps, AF.Copy,
                   reads=(f"pb{bk}",), writes=("attn",), tag="attnev")
        mrep, rrep = ln_stats(b_attn, TS, xpv, f"bxp{par}", "attn", "l2",
                              packed=True)
        at3 = b_attn.rearrange("p (g w) -> p g w", g=OG)
        tt_op(at3, at3, mrep.unsqueeze(1).broadcast_to([128, OG, TS]), ALU.subtract,
              reads=("attn", "pb6"), writes=("attn",), tag="pcent")
        tt_op(at3, at3, rrep.unsqueeze(1).broadcast_to([128, OG, TS]), ALU.mult,
              reads=("attn", "pb7"), writes=("attn",), tag="pmul")
        tt_op(xpv, b_attn, b_x[par], ALU.add,
              reads=("attn", f"bx{par}"), writes=(f"bxp{par}",), tag="resid")
        dma("gpsimd", f"spx{par}", d_xp.ap()[u], xpv, reads=(f"bxp{par}",),
            writes=("D:xp",), new_group=True, tag="spxp")

    p1b_Aload(0)
    p1b_Aload(1)
    p1b_Afix(0)
    for u in range(NSUP2):
        p1b_B(u)
        if u + 1 < NSUP2:
            p1b_Afix(u + 1)
        if u + 2 < NSUP2:
            p1b_Aload(u + 2)

    kb.barrier()
    phase_reset()
    c_q = alloc(GRP2 * TS)
    c_ws = alloc(2 * C)
    c_xp = [alloc(OG * TS) for _ in range(4)]
    c_n2 = alloc(OG * TS)
    c_out = alloc(OG * TS)
    c_hb = [alloc(GRP2 * TS) for _ in range(2)]
    c_sh = alloc(GRP2 * TS)
    c_rows2 = alloc(2 * TS)
    c_w1 = [allocb(2 * C) for _ in range(2)]
    c_w2s = [allocb(2 * C) for _ in range(2)]
    c_actp = [[allocb(FG * TS), allocb(FG * TS)] for _ in range(2)]
    c_n2p = [[allocb(OG * TS), allocb(OG * TS)] for _ in range(2)]

    def p2_Pload(u):
        par = u % 4
        dma("sync", f"xpin{par}", c_xp[par], d_xp.ap()[u], reads=("D:xp",),
            writes=(f"cxp{par}",), new_group=True, tag="ldxp")

    def p2_Pstats(u):
        par = u % 4
        mem = u % 2
        xpv = c_xp[par]
        xkey = f"cxp{par}"
        tt_op(c_n2, xpv, xpv, ALU.mult, reads=(xkey,), writes=("n2f",),
              tag="l3sq")
        rsum = pb[4][0:1, mem * TS:(mem + 1) * TS]
        rsq = pb[5][0:1, mem * TS:(mem + 1) * TS]
        for cg in range(CG):
            mm(rsum, v_onesP, xpv[:, cg * TS:(cg + 1) * TS], cg == 0,
               cg == CG - 1, reads=("c:onesP", xkey), writes=("pb4",),
               tag="l3summ")
        for cg in range(CG):
            mm(rsq, v_onesP, c_n2[:, cg * TS:(cg + 1) * TS], cg == 0,
               cg == CG - 1, reads=("c:onesP", "n2f"), writes=("pb5",),
               tag="l3sqmm")
        if mem == 0:
            m_row = r_rows[0:1, 0:TS]
            s_row = r_rows[0:1, 2 * TSH:2 * TSH + TS]
            mkey, skey = "rows_m", "rows_s"
        else:
            m_row = c_rows2[0:1, 0:TS]
            s_row = c_rows2[0:1, TS:2 * TS]
            mkey, skey = "r2m", "r2s"
        v_row = r_rows[0:1, TSH:TSH + TS]
        ts_op(m_row, rsum, invC, None, ALU.mult, reads=("pb4",), writes=(mkey,),
              tag="l3m")
        tt_op(v_row, m_row, m_row, ALU.mult, reads=(mkey,), writes=("rows_v",),
              tag="l3msq")
        stt_op(v_row, rsq, invC, v_row, ALU.mult, ALU.subtract,
               reads=("pb5", "rows_v"), writes=("rows_v",), tag="l3var")
        act_op(v_row, v_row, AF.Ln, bias=v_eps[0:1, 0:1],
               reads=("rows_v", "c:eps"), writes=("rows_v",), tag="l3lnv")
        act_op(s_row, v_row, AF.Exp, scale=-0.5, reads=("rows_v",),
               writes=(skey,), tag="l3rstd")
        mm(pb[6][:, mem * TS:(mem + 1) * TS], r_ones1, m_row, mem == 0,
           mem == 1, reads=("c:ones1", mkey), writes=("pb6",), tag="l3mrep")
        mm(pb[7][:, mem * TS:(mem + 1) * TS], r_ones1, s_row, mem == 0,
           mem == 1, reads=("c:ones1", skey), writes=("pb7",), tag="l3rrep")

    def p2_Papply(u):
        par = u % 4
        mem = u % 2
        xpv = c_xp[par]
        xkey = f"cxp{par}"
        mrep = pb[6][:, mem * TS:(mem + 1) * TS]
        rrep = pb[7][:, mem * TS:(mem + 1) * TS]
        n23 = c_n2.rearrange("p (g w) -> p g w", g=OG)
        tt_op(n23, xpv.rearrange("p (g w) -> p g w", g=OG),
              mrep.unsqueeze(1).broadcast_to([128, OG, TS]), ALU.subtract,
              reads=(xkey, "pb6"), writes=("n2f",), tag="n2c")
        tt_op(n23, n23, rrep.unsqueeze(1).broadcast_to([128, OG, TS]), ALU.mult,
              reads=("n2f", "pb7"), writes=("n2f",), tag="n2m")
        split(c_n2p[mem][0], c_n2p[mem][1], c_n2, "n2f", f"n2h{mem}",
              f"n2l{mem}", tag="n2sp")

    def p2_FG(u):
        for grp in range(NGRP2):
            for fgl in range(GRP2):
                fg = grp * GRP2 + fgl
                wpar = fg % 2
                wpb = c_w1[wpar]
                if u == 0:
                    dma("sync", f"w1in{wpar}",
                        c_ws[:, 0:C].rearrange("p (cg f) -> p cg f", cg=CG),
                        d_w1[:, fg * 128:(fg + 1) * 128].rearrange(
                            "(cg p) f -> p cg f", p=128),
                        reads=("ws",), writes=("ws",), new_group=True, tag="w1ldf")
                    split(wpb[:, 0:C], wpb[:, C:2 * C], c_ws[:, 0:C], "ws",
                          f"w1p{wpar}", f"w1p{wpar}", tag="w1sp")
                    dma("gpsimd", "wst", d_w1s.ap()[fg], wpb,
                        reads=(f"w1p{wpar}",), writes=("D:w1s",),
                        new_group=True, tag="w1st")
                else:
                    dma("sync", f"w1in{wpar}", wpb, d_w1s.ap()[fg],
                        reads=("D:w1s",), writes=(f"w1p{wpar}",),
                        new_group=True, tag="w1ldp")
                hbk = fg % 6
                first = True
                for mem in range(2):
                    ps = pb[hbk][:, mem * TS:(mem + 1) * TS]
                    for pi, (lo_w, lo_r) in enumerate(((0, 0), (C, 0), (0, 1))):
                        for cg in range(CG):
                            mm(ps, wpb[:, lo_w + cg * 128:lo_w + (cg + 1) * 128],
                               c_n2p[mem][lo_r][:, cg * TS:(cg + 1) * TS],
                               first, mem == 1 and pi == 2 and cg == CG - 1,
                               reads=(f"w1p{wpar}", f"n2h{mem}", f"n2l{mem}"),
                               writes=(f"pb{hbk}",), tag="w1mm")
                            first = False
                for mem in range(2):
                    act_op(c_hb[mem][:, fgl * TS:(fgl + 1) * TS],
                           pb[hbk][:, mem * TS:(mem + 1) * TS], AF.Identity,
                           bias=v_b1[:, fg:fg + 1], reads=(f"pb{hbk}", "c:b1"),
                           writes=(f"hb{mem}f{fgl}",), tag="hbev")
            Ec = v_E[:, grp * GRP2:(grp + 1) * GRP2]
            Fc = v_F[:, grp * GRP2:(grp + 1) * GRP2]
            qb = (c_q, c_out)
            qk = ("q", "cout")
            shb = (c_sh, c_ws)
            shk = ("sh", "ws")
            for mem in range(2):
                hball = tuple(f"hb{mem}f{i}" for i in range(GRP2))
                ts_op(shb[mem], c_hb[mem], P1H, None, ALU.mult,
                      reads=hball, writes=(shk[mem],), tag="hh1")
                stt_op(shb[mem], c_hb[mem], P1L, shb[mem], ALU.mult, ALU.add,
                       reads=hball + (shk[mem],), writes=(shk[mem],), tag="hh2")
            for mem in range(2):
                act_op(qb[mem], shb[mem], AF.Sin, reads=(shk[mem],),
                       writes=(qk[mem],), tag="sinb")
            for mem in range(2):
                act_op(shb[mem], shb[mem], AF.Sin, scale=0.5, reads=(shk[mem],),
                       writes=(shk[mem],), tag="sinh2")
            for mem in range(2):
                act_op(shb[mem], shb[mem], AF.Square, reads=(shk[mem],),
                       writes=(shk[mem],), tag="shsq")
            for mem in range(2):
                act_op(shb[mem], shb[mem], AF.Copy, scale=-2.0, bias=1.0,
                       reads=(shk[mem],), writes=(shk[mem],), tag="cosb")
            for mem in range(2):
                for fgl in range(GRP2):
                    sl = slice(fgl * TS, (fgl + 1) * TS)
                    stt_op(qb[mem][:, sl], qb[mem][:, sl], Fc[:, fgl:fgl + 1],
                           c_hb[mem][:, sl], ALU.mult, ALU.add,
                           reads=(qk[mem], "c:F", f"hb{mem}f{fgl}"),
                           writes=(qk[mem],), tag="fsb")
            for mem in range(2):
                for fgl in range(GRP2):
                    sl = slice(fgl * TS, (fgl + 1) * TS)
                    stt_op(qb[mem][:, sl], shb[mem][:, sl], Ec[:, fgl:fgl + 1],
                           qb[mem][:, sl], ALU.mult, ALU.add,
                           reads=(qk[mem], "c:E", shk[mem]),
                           writes=(qk[mem],), tag="eq")
            for mem in range(2):
                akey = f"act{mem}g{grp}"
                a_h = c_actp[mem][0][:, grp * GRP2 * TS:(grp + 1) * GRP2 * TS]
                a_l = c_actp[mem][1][:, grp * GRP2 * TS:(grp + 1) * GRP2 * TS]
                act_op(a_h, qb[mem], AF.Relu, reads=(qk[mem],),
                       writes=(akey + "h",), tag="asphi")
                stt_op(a_l, qb[mem], 0.0, a_h, ALU.max, ALU.subtract,
                       reads=(qk[mem], akey + "h"), writes=(akey + "l",),
                       tag="asplo")

    def p2_DG(u):
        for fgl in range(FG):
            wpar = fgl % 2
            wpb = c_w2s[wpar]
            if u == 0:
                dma("sync", f"w2in{wpar}",
                    c_ws[:, 0:C], d_w2[fgl * 128:(fgl + 1) * 128, :].rearrange(
                        "(o p) d -> p (o d)", p=128),
                    reads=("ws",), writes=("ws",), new_group=True, tag="w2ldf")
                split(wpb[:, 0:C], wpb[:, C:2 * C], c_ws[:, 0:C], "ws",
                      f"w2p{wpar}", f"w2p{wpar}", tag="w2sp")
                dma("gpsimd", "wst", d_w2s.ap()[fgl], wpb,
                    reads=(f"w2p{wpar}",), writes=("D:w2s",),
                    new_group=True, tag="w2st")
            else:
                dma("sync", f"w2in{wpar}", wpb, d_w2s.ap()[fgl],
                    reads=("D:w2s",), writes=(f"w2p{wpar}",),
                    new_group=True, tag="w2ldp")
            for mem in range(2):
                akey = f"act{mem}g{fgl // GRP2}"
                ah = c_actp[mem][0][:, fgl * TS:(fgl + 1) * TS]
                al = c_actp[mem][1][:, fgl * TS:(fgl + 1) * TS]
                for dg in range(DG):
                    base = dg * 128
                    for pi, (off, ra) in enumerate(((0, ah), (C, ah), (0, al))):
                        mm(pb[dg][:, mem * TS:(mem + 1) * TS],
                           wpb[:, off + base:off + base + 128], ra,
                           fgl == 0 and mem == 0 and pi == 0,
                           fgl == FG - 1 and mem == 1 and pi == 2,
                           reads=(f"w2p{wpar}", akey + "h", akey + "l"),
                           writes=(f"pb{dg}",), tag="w2mm")

    def p2_Lpair(u):
        obuf = (c_out, c_q)
        okey = ("cout", "q")
        for mem in range(2):
            for dg in range(DG):
                ts_op(obuf[mem][:, dg * TS:(dg + 1) * TS],
                      pb[dg][:, mem * TS:(mem + 1) * TS],
                      v_b2[:, dg:dg + 1], None, ALU.add, reads=(f"pb{dg}", "c:b2"),
                      writes=(okey[mem],), tag="outev")
        for mem in range(2):
            um = u + mem
            par = um % 4
            b, s = divmod(um, NSUP)
            xkey = f"cxp{par}"
            ov = obuf[mem]
            ok = okey[mem]
            mrep, rrep = ln_stats(ov, TS, c_ws, "ws", ok, "l4", packed4=True)
            o3 = ov.rearrange("p (g w) -> p g w", g=OG)
            tt_op(o3, o3, mrep.unsqueeze(1).broadcast_to([128, OG, TS]),
                  ALU.subtract, reads=(ok, "pb4"), writes=(ok,), tag="fcent")
            tt_op(o3, o3, rrep.unsqueeze(1).broadcast_to([128, OG, TS]), ALU.mult,
                  reads=(ok, "pb5"), writes=(ok,), tag="fmul")
            tt_op(ov, ov, c_xp[par], ALU.add,
                  reads=(ok, xkey), writes=(ok,), tag="fresid")
            dma("gpsimd", "outst",
                d_out[b, :, s * TS:(s + 1) * TS].rearrange("(g p) w -> p g w", p=128),
                ov.rearrange("p (g w) -> p g w", g=OG), reads=(ok,),
                writes=("D:out",), new_group=True, tag="outst")

    NP2 = NSUP2 // 2
    p2_Pload(0)
    p2_Pload(1)
    p2_Pstats(0)
    p2_Pstats(1)
    p2_Papply(0)
    p2_Papply(1)
    for p in range(NP2):
        u = 2 * p
        if p > 0:
            p2_DG(u - 2)
            p2_Lpair(u - 2)
        if u + 2 < NSUP2:
            p2_Pload(u + 2)
            p2_Pload(u + 3)
            p2_Pstats(u + 2)
            p2_Pstats(u + 3)
        p2_FG(u)
        if u + 2 < NSUP2:
            p2_Papply(u + 2)
            p2_Papply(u + 3)
    p2_DG(NSUP2 - 2)
    p2_Lpair(NSUP2 - 2)

    kb.barrier()

    kb.resolve()
    blk = es.enter_context(nc.Block())
    with es:
        for name in ENGS:
            getattr(blk, name)(lambda eng, name=name: kb.replay(name, eng))
    return nc


def _fm_vec(v, groups):
    return np.ascontiguousarray(np.asarray(v, np.float32).reshape(groups, 128).T)


def stage_inputs(cfg, inputs):
    B, T, C, DFF, NC, TLOC = (cfg["B"], cfg["T"], cfg["C"], cfg["DFF"],
                              cfg["NC"], cfg["TLOC"])
    x = np.asarray(inputs["L0_attn_x"], np.float32)
    xt = np.transpose(x, (0, 2, 1))
    xpad = np.zeros((B, C, T + 2), np.float32)
    xpad[:, :, 1:T + 1] = xt
    shared = {
        "wkT": np.ascontiguousarray(np.asarray(inputs["L0_attn_Wk"], np.float32).T),
        "wvT": np.ascontiguousarray(np.asarray(inputs["L0_attn_Wv"], np.float32).T),
        "wrT": np.ascontiguousarray(np.asarray(inputs["L0_attn_Wr"], np.float32).T),
        "woT": np.ascontiguousarray(np.asarray(inputs["L0_attn_Wo"], np.float32).T),
        "w1": np.ascontiguousarray(np.asarray(inputs["L0_ffn_W1"], np.float32)),
        "w2": np.ascontiguousarray(np.asarray(inputs["L0_ffn_W2"], np.float32)),
        "sd_fm": _fm_vec(inputs["L0_attn_spatial_decay"], C // 128),
        "sf_fm": _fm_vec(inputs["L0_attn_spatial_first"], C // 128),
        "mixk_fm": _fm_vec(inputs["L0_attn_mix_k"], C // 128),
        "mixv_fm": _fm_vec(inputs["L0_attn_mix_v"], C // 128),
        "mixr_fm": _fm_vec(inputs["L0_attn_mix_r"], C // 128),
        "b1_fm": _fm_vec(inputs["L0_ffn_b1"], DFF // 128),
        "theta_fm": _fm_vec(inputs["L0_ffn_quantum_phases"], DFF // 128),
        "qwr_fm": _fm_vec(inputs["L0_ffn_quantum_weights_real"], DFF // 128),
        "qwi_fm": _fm_vec(inputs["L0_ffn_quantum_weights_imag"], DFF // 128),
        "b2_fm": _fm_vec(inputs["L0_ffn_b2"], C // 128),
        "sd_row": np.ascontiguousarray(
            np.asarray(inputs["L0_attn_spatial_decay"], np.float32).reshape(1, C)),
        "iota256": np.arange(cfg["TS"], dtype=np.float32).reshape(1, -1),
        "iota8": np.arange(cfg["NSUP"], dtype=np.float32).reshape(1, -1),
        "ones1": np.ones((1, 128), np.float32),
        "invC_row": np.full((1, 128), 1.0 / C, np.float32),
        "onesP": np.ones((128, 1), np.float32),
    }
    in_maps = []
    for i in range(NC):
        m = dict(shared)
        m["x_fm"] = np.ascontiguousarray(xpad[:, :, i * TLOC:(i + 1) * TLOC + 2])
        mask = np.zeros((128, 8), np.float32)
        mask[:, :i] = 1.0
        m["mask"] = mask
        in_maps.append(m)
    return in_maps


def assemble_output(cfg, results):
    outs = [np.transpose(np.asarray(r["out_fm"]), (0, 2, 1)) for r in results]
    return np.ascontiguousarray(np.concatenate(outs, axis=1))


def kernel(**inputs):
    x = np.asarray(inputs["L0_attn_x"])
    B, T, C = x.shape
    DFF = np.asarray(inputs["L0_ffn_W1"]).shape[1]
    cfg = make_cfg(B, T, C, DFF)
    nc_prog = build_program(cfg)
    in_maps = stage_inputs(cfg, inputs)
    res = run_bass_kernel_spmd(nc_prog, in_maps, core_ids=list(range(cfg["NC"])))
    return assemble_output(cfg, res.results)
